# version 07
# baseline (speedup 1.0000x reference)
import numpy as np
import concourse.bass as bass
import concourse.mybir as mybir
from concourse.bass_utils import run_bass_kernel_spmd
from contextlib import ExitStack

F32 = mybir.dt.float32
I32 = mybir.dt.int32
AF = mybir.ActivationFunctionType
ALU = mybir.AluOpType
AX = mybir.AxisListType

D = 1024
KT = 8
DI = 2048
DIT = 16
NS = 16
DTR = 64
TOK = 2048
GRID = 2051
XR = 2054
SEQ = 4096
EPS = 1e-5
WIN_OWN = [(a, a + 512) for a in range(0, 2048, 512)]
WIN_GRID = WIN_OWN + [(2048, 2051)]
ENGINES = ("tensor", "vector", "scalar", "gpsimd", "sync")
RG = [[0, 1], [2, 3], [4, 5], [6, 7]]

TAPS = []
LAST_TAPS = {}
SERIAL = False
SERIAL_SITES = set()


NQS = 6


class Em:
    def __init__(self):
        self.q = {e: [] for e in ENGINES}
        self.cnt = {e: 0 for e in ENGINES}
        self.cnt["cc"] = 0
        self.qs = {q: {"idx": 0, "c": [0] * NQS, "map": {}} for q in
                   ("sync", "gpsimd")}

    def op(self, eng, fn, waits=(), site=None):
        if (SERIAL is True or (SERIAL and SERIAL is not False and eng in SERIAL)
                or (site is not None and site in SERIAL_SITES)):
            waits = list(waits) + [(e + "*", self.cnt[e]) if e in ("sync", "gpsimd")
                                   else (e, self.cnt[e])
                                   for e in ENGINES if e != eng] + \
                    ([("cc", self.cnt["cc"])] if self.cnt["cc"] else [])
        step = 16 if eng in ("sync", "gpsimd") else 1
        self.cnt[eng] += step
        if eng in ("sync", "gpsimd"):
            qs = self.qs[eng]
            j = qs["idx"]; qs["idx"] = (j + 1) % NQS
            qs["c"][j] += 1
            qs["map"][self.cnt[eng]] = (j, qs["c"][j], tuple(qs["c"]))
        self.q[eng].append((fn, [(s, v) for (s, v) in waits if v > 0],
                            eng, self.cnt[eng]))
        return self.cnt[eng]

    def cc(self, fn, waits=()):
        self.cnt["cc"] += 1
        self.q["gpsimd"].append((fn, [(s, v) for (s, v) in waits if v > 0],
                                 "cc", self.cnt["cc"]))
        return self.cnt["cc"]

    def snap(self):
        return dict(self.cnt)

    def bw(self, snap_):
        w = []
        for e in ENGINES:
            w.append((e + "*", snap_[e]) if e in ("sync", "gpsimd")
                     else (e, snap_[e]))
        if snap_.get("cc"):
            w.append(("cc", snap_["cc"]))
        return w


def build(dbg_taps=()):
    nc = bass.Bass("TRN2", target_bir_lowering=False)
    E = Em()

    def din(name, shape, dt=F32):
        return nc.dram_tensor(name, shape, dt, kind="ExternalInput").ap()

    xf_in = din("xf", [D, XR])
    pw = {}
    for p in ("a", "b"):
        pw[p] = dict(
            winT=din(p + "_winT", [D, 2 * DI]),
            wconv=din(p + "_wconv", [128, DIT * 4]),
            bconv=din(p + "_bconv", [128, DIT]),
            wxT=din(p + "_wxT", [DI, 96]),
            wdtT=din(p + "_wdtT", [DTR, DI]),
            bdt=din(p + "_bdt", [128, DIT]),
            alog=din(p + "_alog", [128, DIT * NS]),
            dp=din(p + "_dp", [128, DIT]),
            woutT=din(p + "_woutT", [DI, D]),
            wprojT=din(p + "_wprojT", [D, D]),
        )
    bproj = din("bproj", [128, KT])
    w1 = din("w1", [D, 4 * D]); b1 = din("b1", [128, 32])
    w2 = din("w2", [4 * D, D]); b2 = din("b2", [128, KT])
    wqT = din("wqT", [D, D]); bq = din("bq", [128, KT])
    wkT = din("wkT", [D, D]); bk = din("bk", [128, KT])
    wvT = din("wvT", [D, D]); bv_row = din("bv_row", [1, D])
    woT = din("woT", [D, D]); bo = din("bo", [128, KT])
    qTw = din("qTw", [6, D, D], I32)
    scrT = din("scrT", [6, D, D])
    qb = din("qb", [6, 128, KT])
    laT = din("laT", [6, D, 32])
    lbT = din("lbT", [6, 32, D])
    lng = din("lng", [3, 128, KT])
    lnb = din("lnb", [3, 128, KT])
    ident_in = din("ident_in", [128, 128])

    out_fm = nc.dram_tensor("out_fm", [D, TOK], F32, kind="ExternalOutput").ap()

    d_u = [nc.dram_tensor(f"d_u{p}", [DIT, 128, TOK], F32).ap() for p in range(2)]
    d_zs = [nc.dram_tensor(f"d_zs{p}", [DIT, 128, TOK], F32).ap() for p in range(2)]
    d_dl = [nc.dram_tensor(f"d_dl{p}", [DIT, 128, TOK], F32).ap() for p in range(2)]
    d_y = [nc.dram_tensor(f"d_y{p}", [DIT, 128, TOK], F32).ap() for p in range(2)]
    d_bc = [nc.dram_tensor(f"d_bc{p}", [2, NS, TOK], F32).ap() for p in range(2)]
    cc_st_in = nc.dram_tensor("cc_st_in", [DI, NS], F32)
    cc_st_out = nc.dram_tensor("cc_st_out", [DI, NS], F32)
    cc_k_in = nc.dram_tensor("cc_k_in", [D, TOK], F32)
    cc_k_out = nc.dram_tensor("cc_k_out", [2, D, TOK], F32)
    cc_vin = nc.dram_tensor("cc_vin", [TOK, D], F32)
    cc_vout = nc.dram_tensor("cc_vout", [2, TOK, D], F32)
    d_q = nc.dram_tensor("d_q", [D, TOK], F32).ap()
    d_x1 = nc.dram_tensor("d_x1", [D, TOK], F32).ap()
    d_xn1 = nc.dram_tensor("d_xn1", [D, TOK], F32).ap()
    d_g = nc.dram_tensor("d_g", [D, TOK], F32).ap()
    d_nrm = nc.dram_tensor("d_nrm", [1, TOK], F32).ap()
    d_rows = nc.dram_tensor("d_rows", [2, XR], F32).ap()
    d_s = nc.dram_tensor("d_s", [D, TOK], F32).ap()
    d_g0 = nc.dram_tensor("d_g0", [D, TOK], F32).ap()

    taps = {}
    tap_shapes = {"s_acc": [D, TOK], "x1": [D, TOK], "xn": [D, XR], "g1": [D, TOK],
                  "xn1": [D, TOK], "y0": [DIT, 128, TOK], "u0": [DIT, 128, TOK],
                  "dl0": [DIT, 128, TOK], "zs0": [DIT, 128, TOK], "bc0": [2, NS, TOK],
                  "y1": [DIT, 128, TOK], "q": [D, TOK], "kf": [2, D, TOK],
                  "vf": [2, TOK, D], "attn": [D, TOK], "sattn": [D, TOK],
                  "st": [DI, NS], "bi": [DI, NS],
                  "ql0": [D, TOK], "qln0": [D, TOK], "qh0": [D, TOK], "sA": [D, TOK],
                  "ygA": [DIT, 128, 512], "gfA": [2, 128, 512], "yg2": [2, 128, 512], "g0": [D, TOK]}
    for nm in dbg_taps:
        taps[nm] = nc.dram_tensor("tap_" + nm, tap_shapes[nm], F32,
                                  kind="ExternalOutput").ap()

    es = ExitStack()
    sem = {}
    for name in ("tensor", "vector", "scalar", "cc"):
        sem[name] = es.enter_context(nc.semaphore("sem_" + name))
    qsems = {}
    for qn in ("sync", "gpsimd"):
        qsems[qn] = [es.enter_context(nc.semaphore(f"sem_{qn}{j}"))
                     for j in range(NQS)]

    def mk_sb(stack):
        return lambda n, shp, dt=F32: stack.enter_context(nc.sbuf_tensor(n, shp, dt))

    sb = mk_sb(es)
    ones_col = sb("ones_col", [128, 1])
    ones_row = sb("ones_row", [1, 128])
    sqt = sb("sqt", [128, 512])
    small = {}
    for p in range(2):
        small[p] = dict(
            wconv=sb(f"wconv{p}", [128, DIT, 4]),
            bconv=sb(f"bconv{p}", [128, DIT, 1]),
            bdt=sb(f"bdt{p}", [128, DIT, 1]),
            A=sb(f"A{p}", [128, DIT, NS]),
            dp=sb(f"dp{p}", [128, DIT, 1]),
        )
    bproj_sb = sb("bproj_sb", [128, KT, 1])
    states = sb("states", [128, DIT, NS])
    binit = sb("binit", [128, DIT, NS])

    ps = {}
    for nm in ("A", "B", "C", "D"):
        ps[nm] = es.enter_context(nc.psum_tensor("ps" + nm, [128, 512], F32))
    ps_big = es.enter_context(nc.psum_tensor("ps_big", [128, 2048], F32))
    psf = {k: ("vector", 0) for k in ("A", "B", "C", "D", "big")}

    def dma(qeng, out, in_, waits=(), site=None):
        return E.op(qeng, lambda e, o=out, i=in_: e.dma_start(out=o, in_=i), waits,
                    site=site)

    def rw(dram_ap, cols):
        return dram_ap.rearrange("(k p) n -> p k n", p=128)

    def ln_stats(xt, ncols, rows, waits_in):
        rm, rs, rt = rows
        wins = [(a, min(a + 512, ncols)) for a in range(0, ncols, 512)]
        vm = 0
        for (a, b) in wins:
            w = b - a
            last = 0
            for k in range(KT):
                wl = list(waits_in) + [psf["A"], ("vector", vm)] if k == 0 else []
                last = E.op("tensor", lambda e, k=k, a=a, b=b, w=w: e.matmul(
                    out=ps["A"][0:1, 0:w], lhsT=ones_col[:], rhs=xt[:, k, a:b],
                    start=(k == 0), stop=(k == KT - 1), skip_group_check=True), wl)
            m1 = E.op("vector", lambda e, a=a, w=w: e.tensor_scalar(
                out=rm[0:1, a:a + w], in0=ps["A"][0:1, 0:w], scalar1=1.0 / D,
                scalar2=None, op0=ALU.mult), [("tensor", last)])
            psf["A"] = ("vector", m1)
            last = 0
            for k in range(KT):
                wl = list(waits_in) + [psf["B"]] if k == 0 else [("tensor", last)]
                sqm = E.op("scalar", lambda e, k=k, a=a, b=b: e.activation(
                    out=sqt[:, 0:b - a], in_=xt[:, k, a:b], func=AF.Square), wl)
                last = E.op("tensor", lambda e, k=k, w=w: e.matmul(
                    out=ps["B"][0:1, 0:w], lhsT=ones_col[:], rhs=sqt[:, 0:w],
                    start=(k == 0), stop=(k == KT - 1), skip_group_check=True),
                    [("scalar", sqm)])
            vm = E.op("vector", lambda e, a=a, w=w: e.tensor_scalar(
                out=rs[0:1, a:a + w], in0=ps["B"][0:1, 0:w], scalar1=1.0 / D,
                scalar2=None, op0=ALU.mult), [("tensor", last)])
            psf["B"] = ("vector", vm)
        v1 = E.op("vector", lambda e, n=ncols: e.tensor_tensor(
            out=rt[0:1, 0:n], in0=rm[0:1, 0:n], in1=rm[0:1, 0:n], op=ALU.mult))
        v2 = E.op("vector", lambda e, n=ncols: e.tensor_tensor(
            out=rs[0:1, 0:n], in0=rs[0:1, 0:n], in1=rt[0:1, 0:n],
            op=ALU.subtract), [("vector", v1)])
        v3 = E.op("vector", lambda e, n=ncols: e.tensor_scalar(
            out=rs[0:1, 0:n], in0=rs[0:1, 0:n], scalar1=EPS, scalar2=None,
            op0=ALU.add), [("vector", v2)])
        s1 = E.op("scalar", lambda e, n=ncols: e.activation(
            out=rt[0:1, 0:n], in_=rs[0:1, 0:n], func=AF.Ln), [("vector", v3)])
        return E.op("scalar", lambda e, n=ncols: e.activation(
            out=rs[0:1, 0:n], in_=rt[0:1, 0:n], func=AF.Exp, scale=-0.5),
            [("scalar", s1)])

    def bcast2(rows, a, w, waits):
        rm, rs = rows[0], rows[1]
        t1 = E.op("tensor", lambda e, a=a, w=w: e.matmul(
            out=ps_big[:, 0:w], lhsT=ones_row[0:1, :], rhs=rm[0:1, a:a + w],
            start=True, stop=True, skip_group_check=True), list(waits) + [psf["big"]])
        return E.op("tensor", lambda e, a=a, w=w: e.matmul(
            out=ps_big[:, 512:512 + w], lhsT=ones_row[0:1, :],
            rhs=rs[0:1, a:a + w],
            start=True, stop=True, skip_group_check=True), [("tensor", t1)])

    r1 = ExitStack(); sb1 = mk_sb(r1)
    xn = sb1("xn", [128, KT, XR])
    r1a = ExitStack(); sb1a = mk_sb(r1a)
    rows0 = tuple(sb1a(f"rows0_{i}", [1, XR]) for i in range(3))
    x_t = sb1a("x_t", [128, KT, XR])

    ld_x = dma("sync", x_t[:], rw(xf_in, XR))
    for p, pn in ((0, "a"), (1, "b")):
        for nm, key, cols in (("wconv", "wconv", 4), ("bconv", "bconv", 1),
                              ("bdt", "bdt", 1), ("dp", "dp", 1), ("alog", "A", NS)):
            dma("sync", small[p][key][:],
                pw[pn][nm].rearrange("p (k c) -> p k c", c=cols))
    dma("sync", bproj_sb[:], bproj.unsqueeze(2))
    ld_all = E.cnt["sync"]

    mz = E.op("vector", lambda e: e.memset(ones_col[:], 1.0))
    mz2 = E.op("vector", lambda e: e.memset(ones_row[:], 1.0), [("vector", mz)])
    aexp_last = 0
    for p in range(2):
        s1 = E.op("scalar", lambda e, p=p: e.activation(
            out=small[p]["A"][:], in_=small[p]["A"][:], func=AF.Exp),
            [("sync*", ld_all)])
        aexp_last = E.op("vector", lambda e, p=p: e.tensor_scalar(
            out=small[p]["A"][:], in0=small[p]["A"][:], scalar1=-1.0, scalar2=None,
            op0=ALU.mult), [("scalar", s1)])

    ln0_done = ln_stats(x_t, XR, rows0, [("sync", ld_x), ("vector", mz2)])
    xn_done = 0
    for (a, b) in [(i, min(i + 512, XR)) for i in range(0, XR, 512)]:
        w = b - a
        br = bcast2(rows0, a, w, [("scalar", ln0_done)])
        for k in range(KT):
            v1 = E.op("vector", lambda e, k=k, a=a, b=b, w=w: e.tensor_tensor(
                out=xn[:, k, a:b], in0=x_t[:, k, a:b], in1=ps_big[:, 0:w],
                op=ALU.subtract), [("tensor", br)])
            xn_done = E.op("vector", lambda e, k=k, a=a, b=b, w=w: e.tensor_tensor(
                out=xn[:, k, a:b], in0=xn[:, k, a:b], in1=ps_big[:, 512:512 + w],
                op=ALU.mult), [("vector", v1)])
        psf["big"] = ("vector", xn_done)
    dma("sync", d_rows[0:1, :], rows0[0][0:1, :], [("scalar", ln0_done)])
    dma("sync", d_rows[1:2, :], rows0[1][0:1, :], [("scalar", ln0_done)])
    if "xn" in taps:
        for k in range(KT):
            dma("sync", taps["xn"][k * 128:(k + 1) * 128, :], xn[:, k, :],
                [("vector", xn_done)])
    r1a.close()

    def moving_pieces(pipe, a, b, grid):
        out = []
        if pipe == 0:
            if grid:
                out.append((0, b - a, lambda k, a=a, b=b: xn[:, k, a:b]))
            else:
                out.append((0, b - a, lambda k, a=a, b=b: xn[:, k, 3 + a:3 + b]))
        else:
            if grid:
                if a < 3:
                    h = min(b, 3)
                    out.append((0, h - a, lambda k, a=a, h=h: xn[:, k, 2051 + a:2051 + h]))
                if b > 3:
                    p0 = max(a, 3)
                    out.append((p0 - a, b - p0,
                                lambda k, p0=p0, b=b: xn[:, k, 2053 - p0:2053 - b:-1]))
            else:
                out.append((0, b - a, lambda k, a=a, b=b: xn[:, k, 2050 - a:2050 - b:-1]))
        return out

    rS2 = ExitStack(); sbS2 = mk_sb(rS2)
    bcbc = sbS2("bcbc", [128, 8, 512])
    ccbc = sbS2("ccbc", [128, 8, 512])
    dl_c = sbS2("dl_c", [128, 512])
    u_c = sbS2("u_c", [128, 512])
    du_c = sbS2("du_c", [128, 512])
    y_c = sbS2("y_c", [128, 512])
    yo_c = sbS2("yo_c", [128, 512])
    dA_c = sbS2("dA_c", [128, 512])
    dBu_c = sbS2("dBu_c", [128, 512])
    h_c = sbS2("h_c", [128, 512])
    r1L = ExitStack(); sb1L = mk_sb(r1L)
    wslab = sb1L("wslab", [128, KT, 128])
    u_raw = sb1L("u_raw", [128, GRID])
    u_s = sb1L("u_s", [128, TOK])
    zbuf = sb1L("zbuf", [128, 512])
    wx_sb = sb1L("wx_sb", [128, DIT, 96])
    wdt_sb = sb1L("wdt_sb", [DTR, DI])
    dt_sb = sb1L("dt_sb", [DTR, TOK])
    bc_sb = sb1L("bc_sb", [32, TOK])
    dlw = sb1L("dlw", [128, 512])
    dl_w2 = sb1L("dl_w2", [128, 2, 512])
    slab_used = [0, 0]
    zb_used = [0, 0]

    def stage1(pipe, pname, dbc_prev_evac, bw0):
        wname = pw[pname]
        ld_wx = dma("sync", wx_sb[:], rw(wname["wxT"], 96),
                    [("vector", xn_done)] + bw0)
        ld_wdt = dma("sync", wdt_sb[:], wname["wdtT"],
                     [("vector", xn_done)] + bw0)
        u_s_free = ("gpsimd", 0)
        u_raw_free = ("vector", 0)
        for m in range(32):
            ld = dma("gpsimd", wslab[:],
                     rw(wname["winT"], 2 * DI)[:, :, m * 128:(m + 1) * 128],
                     [("tensor", slab_used[0]), ("vector", xn_done)])
            is_u = m < 16
            dt_i = m if is_u else m - 16
            wins = WIN_GRID if is_u else WIN_OWN
            for wi, (a, b) in enumerate(wins):
                pslot = "A" if wi % 2 == 0 else "B"
                last_mm = 0
                first = True
                for (coff, width, mov) in moving_pieces(pipe, a, b, is_u):
                    for k in range(KT):
                        wl = []
                        if first:
                            wl = [("gpsimd", ld), psf[pslot], ("vector", xn_done)]
                            first = False
                        last_mm = E.op("tensor", lambda e, k=k, coff=coff,
                                       width=width, mov=mov, pslot=pslot,
                                       st=(k == 0), sp=(k == KT - 1): e.matmul(
                            out=ps[pslot][:, coff:coff + width],
                            lhsT=wslab[:, k, :], rhs=mov(k),
                            start=st, stop=sp, skip_group_check=True), wl)
                slab_used[0] = last_mm
                if is_u:
                    ev = E.op("scalar", lambda e, a=a, b=b, pslot=pslot: e.activation(
                        out=u_raw[:, a:b], in_=ps[pslot][:, 0:b - a], func=AF.Copy),
                        [("tensor", last_mm), u_raw_free])
                    psf[pslot] = ("scalar", ev)
                else:
                    ev = E.op("scalar", lambda e, b=b, a=a, pslot=pslot: e.activation(
                        out=zbuf[:, 0:b - a], in_=ps[pslot][:, 0:b - a],
                        func=AF.Silu),
                        [("tensor", last_mm), ("gpsimd", zb_used[0])])
                    psf[pslot] = ("scalar", ev)
                    zb_used[0] = dma("gpsimd", d_zs[pipe][dt_i, :, a:b],
                                     zbuf[:, 0:b - a], [("scalar", ev)])
            if is_u:
                sA = psf["A"][1] if psf["A"][0] == "scalar" else 0
                sB = psf["B"][1] if psf["B"][0] == "scalar" else 0
                c0 = E.op("vector", lambda e, dt_i=dt_i, p=pipe: e.tensor_scalar(
                    out=u_s[:], in0=u_raw[:, 0:TOK],
                    scalar1=small[p]["wconv"][:, dt_i, 0:1], scalar2=None,
                    op0=ALU.mult),
                    [("scalar", sA), ("scalar", sB), ("vector", aexp_last), u_s_free])
                for j in (1, 2, 3):
                    c0 = E.op("vector", lambda e, j=j, dt_i=dt_i, p=pipe: e.scalar_tensor_tensor(
                        out=u_s[:], in0=u_raw[:, j:j + TOK],
                        scalar=small[p]["wconv"][:, dt_i, j:j + 1], in1=u_s[:],
                        op0=ALU.mult, op1=ALU.add), [("vector", c0)])
                u_raw_free = ("vector", c0)
                cs = E.op("scalar", lambda e, dt_i=dt_i, p=pipe: e.activation(
                    out=u_s[:], in_=u_s[:], func=AF.Silu,
                    bias=small[p]["bconv"][:, dt_i, 0:1]), [("vector", c0)])
                for w2 in range(4):
                    wl = [("scalar", cs), ("sync", ld_wx)]
                    if dt_i == 0 and w2 == 0:
                        wl.append(dbc_prev_evac)
                    E.op("tensor", lambda e, w2=w2, dt_i=dt_i,
                         st=(dt_i == 0), sp=(dt_i == DIT - 1): e.matmul(
                        out=ps_big[0:96, w2 * 512:(w2 + 1) * 512],
                        lhsT=wx_sb[:, dt_i, :], rhs=u_s[:, w2 * 512:(w2 + 1) * 512],
                        start=st, stop=sp, skip_group_check=True),
                        wl if w2 == 0 else [])
                usp = dma("gpsimd", d_u[pipe][dt_i, :, :], u_s[:],
                          [("scalar", cs), ("tensor", E.cnt["tensor"])])
                u_s_free = ("gpsimd", usp)
            yield
        tm = E.cnt["tensor"]
        e1 = E.op("vector", lambda e: e.tensor_copy(out=dt_sb[:], in_=ps_big[0:DTR, :]),
                  [("tensor", tm)] + bw0)
        e2 = E.op("vector", lambda e: e.tensor_copy(out=bc_sb[:], in_=ps_big[DTR:96, :]),
                  [("vector", e1)] + bw0)
        dma("gpsimd", d_bc[pipe].rearrange("a n t -> (a n) t"), bc_sb[:],
            [("vector", e2)])
        psf["big"] = ("vector", e2)
        dl_free = [("gpsimd", 0), ("gpsimd", 0)]
        psc_free = psf["C"]
        for dt_i in range(DIT):
            for w2 in range(4):
                ws = w2 % 2
                mm = E.op("tensor", lambda e, dt_i=dt_i, w2=w2: e.matmul(
                    out=ps["C"][:, 0:512], lhsT=wdt_sb[:, dt_i * 128:(dt_i + 1) * 128],
                    rhs=dt_sb[:, w2 * 512:(w2 + 1) * 512], start=True, stop=True,
                    skip_group_check=True),
                    [("vector", e1), ("sync", ld_wdt), psc_free])
                s1 = E.op("scalar", lambda e, dt_i=dt_i, p=pipe: e.activation(
                    out=dlw[:], in_=ps["C"][:, 0:512], func=AF.Abs,
                    bias=small[p]["bdt"][:, dt_i, 0:1]), [("tensor", mm)])
                s2 = E.op("scalar", lambda e: e.activation(
                    out=dlw[:], in_=dlw[:], func=AF.Exp, scale=-1.0), [("scalar", s1)])
                s3 = E.op("scalar", lambda e: e.activation(
                    out=dlw[:], in_=dlw[:], func=AF.Ln, bias=1.0), [("scalar", s2)])
                s4 = E.op("scalar", lambda e, dt_i=dt_i, ws=ws, p=pipe: e.activation(
                    out=dl_w2[:, ws, :], in_=ps["C"][:, 0:512],
                    func=AF.Relu, bias=small[p]["bdt"][:, dt_i, 0:1]),
                    [("scalar", s3), dl_free[ws]])
                psc_free = ("scalar", s4)
                vlast = E.op("vector", lambda e, ws=ws: e.tensor_tensor(
                    out=dl_w2[:, ws, :], in0=dl_w2[:, ws, :], in1=dlw[:],
                    op=ALU.add), [("scalar", s4)])
                dl_free[ws] = ("gpsimd", dma(
                    "gpsimd", d_dl[pipe][dt_i, :, w2 * 512:(w2 + 1) * 512],
                    dl_w2[:, ws, :], [("vector", vlast)])[1] if False else dma(
                    "gpsimd", d_dl[pipe][dt_i, :, w2 * 512:(w2 + 1) * 512],
                    dl_w2[:, ws, :], [("vector", vlast)]))
            yield
        psf["C"] = psc_free
        st1_ret[pipe] = ("vector", e2)

    st1_ret = {}
    s2c_ret = {}

    def drive(gens):
        gens = list(gens)
        while gens:
            for g in list(gens):
                try:
                    next(g)
                except StopIteration:
                    gens.remove(g)

    def drive_until(primary, bg):
        while True:
            try:
                next(primary)
            except StopIteration:
                return
            try:
                next(bg)
            except StopIteration:
                pass

    for _ in stage1(0, "a", ("vector", 0), []):
        pass
    dbc_ev = st1_ret[0]
    snapA1 = E.snap()
    for nm, srcd in (("u0", d_u[0]), ("dl0", d_dl[0]), ("zs0", d_zs[0])):
        if nm in taps:
            for dt in range(DIT):
                dma("sync", taps[nm][dt], srcd[dt], E.bw(snapA1))
    if "bc0" in taps:
        dma("sync", taps["bc0"], d_bc[0], E.bw(snapA1))

    g_b1 = stage1(1, "b", dbc_ev, E.bw(snapA1))
    snapB1 = E.bw

    binit_done = 0

    def stage2c(pipe, bsnap):
        bw = E.bw(bsnap)
        last_y = 0
        last_sp = 0
        yo_cons = 0
        bc_cons = 0
        for half in range(2):
            for ch in range(4):
                c0, c1 = ch * 512, (ch + 1) * 512
                bb = dma("sync", bcbc[:],
                         d_bc[pipe][0, half * 8:(half + 1) * 8, c0:c1]
                         .unsqueeze(0).broadcast_to([128, 8, 512]),
                         bw + [("vector", bc_cons)])
                cb = dma("sync", ccbc[:],
                         d_bc[pipe][1, half * 8:(half + 1) * 8, c0:c1]
                         .unsqueeze(0).broadcast_to([128, 8, 512]),
                         bw + [("vector", bc_cons)])
                for dt_i in range(DIT):
                    fd = dma("sync", dl_c[:], d_dl[pipe][dt_i, :, c0:c1],
                             bw + [("vector", last_y)])
                    fu = dma("sync", u_c[:], d_u[pipe][dt_i, :, c0:c1],
                             bw + [("vector", last_y)])
                    vdu = E.op("vector", lambda e: e.tensor_tensor(
                        out=du_c[:], in0=dl_c[:], in1=u_c[:], op=ALU.mult),
                        [("sync", fd), ("sync", fu)])
                    if half == 0:
                        vy = E.op("vector", lambda e, dt_i=dt_i, p=pipe: e.tensor_scalar(
                            out=y_c[:], in0=u_c[:],
                            scalar1=small[p]["dp"][:, dt_i, 0:1],
                            scalar2=None, op0=ALU.mult),
                            [("vector", vdu), ("sync", last_sp)])
                        fy = 0
                    else:
                        fy = dma("sync", yo_c[:], d_y[pipe][dt_i, :, c0:c1],
                                 bw + [("vector", yo_cons)])
                        vy = E.op("vector", lambda e: e.memset(y_c[:], 0.0),
                                  [("vector", vdu), ("sync", last_sp)])
                    sprev = vy
                    for n in range(8):
                        ns = half * 8 + n
                        sA = E.op("scalar", lambda e, dt_i=dt_i, ns=ns, p=pipe: e.activation(
                            out=dA_c[:], in_=dl_c[:], func=AF.Exp,
                            scale=small[p]["A"][:, dt_i, ns:ns + 1]),
                            [("sync", fd), ("vector", sprev)])
                        vB = E.op("vector", lambda e, n=n: e.tensor_tensor(
                            out=dBu_c[:], in0=du_c[:], in1=bcbc[:, n, :],
                            op=ALU.mult), [("vector", vy), ("sync", bb), ("sync", cb)])
                        if ch == 0:
                            if pipe == 0:
                                vS = E.op("vector", lambda e: e.tensor_tensor_scan(
                                    out=h_c[:], data0=dA_c[:], data1=dBu_c[:],
                                    initial=0.0, op0=ALU.mult, op1=ALU.add),
                                    [("scalar", sA), ("vector", vB)])
                            else:
                                vS = E.op("vector", lambda e, dt_i=dt_i, ns=ns: e.tensor_tensor_scan(
                                    out=h_c[:], data0=dA_c[:], data1=dBu_c[:],
                                    initial=binit[:, dt_i, ns:ns + 1],
                                    op0=ALU.mult, op1=ALU.add),
                                    [("scalar", sA), ("vector", vB),
                                     ("vector", binit_done)])
                        else:
                            vS = E.op("vector", lambda e, dt_i=dt_i, ns=ns: e.tensor_tensor_scan(
                                out=h_c[:], data0=dA_c[:], data1=dBu_c[:],
                                initial=states[:, dt_i, ns:ns + 1],
                                op0=ALU.mult, op1=ALU.add),
                                [("scalar", sA), ("vector", vB)])
                        vT = E.op("vector", lambda e, n=n: e.tensor_tensor(
                            out=dBu_c[:], in0=h_c[:], in1=ccbc[:, n, :],
                            op=ALU.mult), [("vector", vS)])
                        vYa = E.op("vector", lambda e: e.tensor_tensor(
                            out=y_c[:], in0=y_c[:], in1=dBu_c[:], op=ALU.add),
                            [("vector", vT)])
                        if ch < 3 or pipe == 0:
                            vYa = E.op("vector", lambda e, dt_i=dt_i, ns=ns: e.tensor_copy(
                                out=states[:, dt_i, ns:ns + 1], in_=h_c[:, 511:512]),
                                [("vector", vYa)])
                        sprev = vYa
                    if half == 1:
                        sprev = E.op("vector", lambda e: e.tensor_tensor(
                            out=y_c[:], in0=y_c[:], in1=yo_c[:], op=ALU.add),
                            [("vector", sprev), ("sync", fy)])
                        yo_cons = sprev
                    last_y = sprev
                    last_sp = dma("sync", d_y[pipe][dt_i, :, c0:c1], y_c[:],
                                  [("vector", sprev)])
                    if dt_i % 4 == 3:
                        yield
                bc_cons = last_y
        s2c_ret[pipe] = last_sp

    g_a2 = stage2c(0, snapA1)

    rF = ExitStack(); sbF = mk_sb(rF)
    h1f = sbF("h1f", [128, 16, 256])
    w1f_sl = sbF("w1f_sl", [128, KT, 128])
    w2f_sl = sbF("w2f_sl", [128, 16, 128])
    gevf = sbF("gevf", [128, 256])
    b1_sb = sbF("b1_sb", [128, 32, 1])
    b2_sb = sbF("b2_sb", [128, KT, 1])
    bwF = E.bw(snapA1)
    dma("gpsimd", b1_sb[:], b1.unsqueeze(2), bwF)
    dma("gpsimd", b2_sb[:], b2.unsqueeze(2), bwF)
    bloadF = E.cnt["gpsimd"]

    def ffn_gen():
      gev_free = ("gpsimd", 0)
      h1_free = ("tensor", 0)
      w1u = [0, 0]
      w2u = 0
      sg_last = 0
      for wi8 in range(8):
        a = wi8 * 256
        mz0 = E.op("vector", lambda e: e.memset(ps_big[:], 0.0),
                   [psf["big"], ("scalar", sg_last)])
        psf["big"] = ("vector", mz0)
        for ih in range(2):
          for it2 in range(16):
              it = ih * 16 + it2
              ld1 = dma("gpsimd", w1f_sl[:],
                        rw(w1, 4 * D)[:, :, it * 128:(it + 1) * 128],
                        bwF + [("tensor", w1u[0])])
              mm = 0
              for k in range(KT):
                  wl = ([("gpsimd", ld1), ("vector", xn_done), psf["A"]]
                        if k == 0 else [])
                  mm = E.op("tensor", lambda e, k=k, a=a, st=(k == 0),
                            sp=(k == KT - 1): e.matmul(
                      out=ps["A"][:, 0:256], lhsT=w1f_sl[:, k, :],
                      rhs=xn[:, k, 3 + a:3 + a + 256], start=st, stop=sp,
                      skip_group_check=True), wl)
              w1u[0] = mm
              sev = E.op("scalar", lambda e, it2=it2, it=it: e.activation(
                  out=h1f[:, it2, :], in_=ps["A"][:, 0:256], func=AF.Relu,
                  bias=b1_sb[:, it, 0:1]),
                  [("tensor", mm), ("gpsimd", bloadF), h1_free])
              psf["A"] = ("scalar", sev)
              if it2 % 8 == 7:
                  yield
          sm_h1 = E.cnt["scalar"]
          for ot in range(KT):
              ld2 = dma("gpsimd", w2f_sl[:],
                        w2[ih * 2048:(ih + 1) * 2048, ot * 128:(ot + 1) * 128]
                        .rearrange("(k q) m -> q k m", q=128),
                        bwF + [("tensor", w2u)])
              mm2 = 0
              for it2 in range(16):
                  wl = ([("gpsimd", ld2), ("scalar", sm_h1),
                         ("vector", mz0)] if it2 == 0 else [])
                  mm2 = E.op("tensor", lambda e, it2=it2, ih=ih, ot=ot,
                             sp=(ih == 1 and it2 == 15): e.matmul(
                      out=ps_big[:, ot * 256:(ot + 1) * 256],
                      lhsT=w2f_sl[:, it2, :], rhs=h1f[:, it2, :],
                      start=False, stop=sp, skip_group_check=True), wl)
              w2u = mm2
              if ot % 4 == 3:
                  yield
          h1_free = ("tensor", w2u)
        for ot in range(KT):
            sg = E.op("scalar", lambda e, ot=ot: e.activation(
                out=gevf[:], in_=ps_big[:, ot * 256:(ot + 1) * 256], func=AF.Silu,
                bias=b2_sb[:, ot, 0:1]), [("tensor", w2u), gev_free])
            sg_last = sg
            gsp = dma("gpsimd", d_g0[ot * 128:(ot + 1) * 128, a:a + 256],
                      gevf[:], [("scalar", sg)])
            gev_free = ("gpsimd", gsp)
        yield

    drive_until(g_b1, g_a2)
    drive_until(ffn_gen(), g_a2)
    for _ in g_a2:
        pass
    yA_sp = s2c_ret[0]
    stA_v = E.cnt["vector"]
    if "y0" in taps:
        snap = E.snap()
        for dt in range(DIT):
            dma("sync", taps["y0"][dt], d_y[0][dt], E.bw(snap))
    if "g0" in taps:
        snap = E.snap()
        dma("sync", taps["g0"], d_g0, E.bw(snap))

    st_dma = dma("sync", cc_st_in.ap().rearrange("(d p) n -> p d n", p=128),
                 states[:], [("vector", stA_v), ("sync", yA_sp)])
    ccm = E.cc(lambda e: e.collective_compute(
        "AllReduce", ALU.add, replica_groups=RG,
        ins=[cc_st_in.ap().opt()], outs=[cc_st_out.ap().opt()]), [("sync", st_dma)])
    st_back = dma("sync", binit[:], cc_st_out.ap().rearrange("(d p) n -> p d n", p=128),
                  [("cc", ccm)])
    binit_done = E.op("vector", lambda e: e.tensor_tensor(
        out=binit[:], in0=binit[:], in1=states[:], op=ALU.subtract),
        [("sync", st_back)])
    if "st" in taps:
        dma("sync", taps["st"].rearrange("(d p) n -> p d n", p=128), states[:],
            [("vector", binit_done)])
    if "bi" in taps:
        dma("sync", taps["bi"].rearrange("(d p) n -> p d n", p=128), binit[:],
            [("vector", binit_done)])

    snapF = E.snap()
    rF.close()
    r1L.close()

    rA3 = ExitStack(); sbA3 = mk_sb(rA3)
    yg = sbA3("yg", [128, DIT, 512])
    g_f = sbA3("g_f", [128, 2, 2, 512])
    wout_sl = sbA3("wout_sl", [128, DIT, 128])
    wproj_sl = sbA3("wproj_sl", [128, 2, KT, 128])
    sdir = sbA3("sdir", [128, KT, 512])
    s_w = sbA3("s_w", [128, 2, 512])
    wsl_used = [0, 0]
    wpj_used = [0, 0]

    def stage3A(bsnap):
        bw = E.bw(bsnap)
        sdir_free = ("tensor", 0)
        yg_free = ("tensor", 0)
        sw_used = [0, 0]
        gm = []
        for wi, (a, b) in enumerate(WIN_OWN):
            for dt_i in range(DIT):
                slot = dt_i % 2
                anti = [("vector", gm[-2] if len(gm) >= 2 else 0)]
                f1 = dma("sync", g_f[:, slot, 0, :], d_y[0][dt_i, :, a:b],
                         bw + anti)
                f3 = dma("sync", g_f[:, slot, 1, :], d_zs[0][dt_i, :, a:b],
                         bw + anti)
                if wi == 0 and dt_i == 0 and "gfA" in taps:
                    dma("sync", taps["gfA"][0], g_f[:, 0, 0, :],
                        [("sync", f1), ("sync", f3)])
                    dma("sync", taps["gfA"][1], g_f[:, 0, 1, :],
                        [("sync", f1), ("sync", f3)])
                gm.append(E.op("vector", lambda e, slot=slot, dt_i=dt_i: e.tensor_tensor(
                    out=yg[:, dt_i, :], in0=g_f[:, slot, 0, :],
                    in1=g_f[:, slot, 1, :], op=ALU.mult),
                    [("sync", f1), ("sync", f3), yg_free]))
                if wi == 0 and dt_i == 0 and "yg2" in taps:
                    dma("sync", taps["yg2"][0], yg[:, 0, :],
                        [("vector", gm[-1])])
            if wi == 0 and "ygA" in taps:
                snapyg = E.snap()
                for dt in range(DIT):
                    dma("sync", taps["ygA"][dt], yg[:, dt, :], E.bw(snapyg))
            if wi == 0 and "yg2" in taps:
                dma("sync", taps["yg2"][1], yg[:, 0, :], E.bw(E.snap()))
            ev_last = 0
            for dm in range(KT):
                pslot = "A" if dm % 2 == 0 else "B"
                ldw = dma("gpsimd", wout_sl[:],
                          rw(pw["a"]["woutT"], D)[:, :, dm * 128:(dm + 1) * 128],
                          bw + [("tensor", wsl_used[0])])
                mm = 0
                for dt_i in range(DIT):
                    wl = ([("gpsimd", ldw), ("vector", gm[-1]), psf[pslot],
                           sdir_free] if dt_i == 0 else [])
                    mm = E.op("tensor", lambda e, dt_i=dt_i, pslot=pslot,
                              st=(dt_i == 0), sp=(dt_i == DIT - 1): e.matmul(
                        out=ps[pslot][:, 0:512], lhsT=wout_sl[:, dt_i, :],
                        rhs=yg[:, dt_i, :], start=st, stop=sp,
                        skip_group_check=True), wl)
                wsl_used[0] = mm
                ev_last = E.op("scalar", lambda e, dm=dm, pslot=pslot: e.activation(
                    out=sdir[:, dm, :], in_=ps[pslot][:, 0:512], func=AF.Copy),
                    [("tensor", mm), sdir_free])
                psf[pslot] = ("scalar", ev_last)
                if dm % 2 == 1:
                    yield
            yg_free = ("tensor", wsl_used[0])
            for dmp in range(KT):
                slot = dmp % 2
                ldp = dma("gpsimd", wproj_sl[:, slot, :, :],
                          rw(pw["a"]["wprojT"], D)[:, :, dmp * 128:(dmp + 1) * 128],
                          bw + [("tensor", wpj_used[slot])])
                mm = 0
                for dm in range(KT):
                    wl = ([("scalar", ev_last), ("gpsimd", ldp), psf["C"]]
                          if dm == 0 else [])
                    mm = E.op("tensor", lambda e, dmp=dmp, dm=dm, slot=slot,
                              st=(dm == 0), sp=(dm == KT - 1): e.matmul(
                        out=ps["C"][:, 0:512], lhsT=wproj_sl[:, slot, dm, :],
                        rhs=sdir[:, dm, :], start=st, stop=sp,
                        skip_group_check=True), wl)
                wpj_used[slot] = mm
                sdir_free = ("tensor", mm)
                sws = dmp % 2
                evc = E.op("scalar", lambda e, sws=sws, dmp=dmp: e.activation(
                    out=s_w[:, sws, :], in_=ps["C"][:, 0:512],
                    func=AF.Identity, bias=bproj_sb[:, dmp, 0:1]),
                    [("tensor", mm), ("gpsimd", sw_used[sws])])
                psf["C"] = ("scalar", evc)
                sw_used[sws] = dma("gpsimd",
                                   d_s[dmp * 128:(dmp + 1) * 128, a:b],
                                   s_w[:, sws, :], [("scalar", evc)])
                if dmp % 2 == 1:
                    yield

    stage2c_b_snap = E.snap()
    drive([stage3A(snapF), stage2c(1, stage2c_b_snap)])
    if "y1" in taps:
        snap = E.snap()
        for dt in range(DIT):
            dma("sync", taps["y1"][dt], d_y[1][dt], E.bw(snap))
    snapS3 = E.snap()
    if "sA" in taps:
        dma("sync", taps["sA"], d_s, E.bw(snapS3))
    rA3.close()
    rS2.close()
    r1.close()

    r3 = ExitStack(); sb3 = mk_sb(r3)
    s_acc = sb3("s_acc", [128, KT, TOK])
    r3a = ExitStack(); sb3a = mk_sb(r3a)
    ygB = sb3a("ygB", [128, DIT, 512])
    g_fB = sb3a("g_fB", [128, 2, 2, 512])
    wout_slB = sb3a("wout_slB", [128, 2, DIT, 128])
    wproj_sbB = sb3a("wproj_sbB", [128, KT, KT, 128])
    sdirB = sb3a("sdirB", [128, KT, 512])
    wslB_used = [0, 0]
    bwB3 = E.bw(snapS3)
    sacc_ld = 0
    for k in range(KT):
        sacc_ld = dma("sync", s_acc[:, k, :], d_s[k * 128:(k + 1) * 128, :], bwB3)

    def stage3B(bsnap):
        bw = E.bw(bsnap) + [("sync", sacc_ld)]
        ldp = dma("sync", wproj_sbB[:],
                  rw(pw["b"]["wprojT"], D).rearrange("p k (m j) -> p k m j", j=128),
                  bw)
        sdirB_free = ("tensor", 0)
        ygB_free = ("tensor", 0)
        gm = []
        for wi, (a, b) in enumerate(WIN_OWN):
            for dt_i in range(DIT):
                slot = dt_i % 2
                anti = [("vector", gm[-2] if len(gm) >= 2 else 0)]
                f1 = dma("sync", g_fB[:, slot, 0, :], d_y[1][dt_i, :, a:b], bw + anti)
                f3 = dma("gpsimd", g_fB[:, slot, 1, :], d_zs[1][dt_i, :, a:b],
                         bw + anti)
                gm.append(E.op("vector", lambda e, slot=slot, dt_i=dt_i: e.tensor_tensor(
                    out=ygB[:, dt_i, :], in0=g_fB[:, slot, 0, :],
                    in1=g_fB[:, slot, 1, :], op=ALU.mult),
                    [("sync", f1), ("gpsimd", f3), ygB_free]))
            ev_last = 0
            for dm in range(KT):
                slot = dm % 2
                pslot = "A" if dm % 2 == 0 else "B"
                ldw = dma("gpsimd", wout_slB[:, slot, :, :],
                          rw(pw["b"]["woutT"], D)[:, :, dm * 128:(dm + 1) * 128],
                          bw + [("tensor", wslB_used[slot])])
                mm = 0
                for dt_i in range(DIT):
                    wl = ([("gpsimd", ldw), ("vector", gm[-1]), psf[pslot],
                           sdirB_free] if dt_i == 0 else [])
                    mm = E.op("tensor", lambda e, slot=slot, dt_i=dt_i, pslot=pslot,
                              st=(dt_i == 0), sp=(dt_i == DIT - 1): e.matmul(
                        out=ps[pslot][:, 0:512], lhsT=wout_slB[:, slot, dt_i, :],
                        rhs=ygB[:, dt_i, :], start=st, stop=sp,
                        skip_group_check=True), wl)
                wslB_used[slot] = mm
                ev_last = E.op("scalar", lambda e, dm=dm, pslot=pslot: e.activation(
                    out=sdirB[:, dm, :], in_=ps[pslot][:, 0:512], func=AF.Copy),
                    [("tensor", mm), sdirB_free])
                psf[pslot] = ("scalar", ev_last)
            ygB_free = ("tensor", max(wsl_used))
            for dmp in range(KT):
                mm = 0
                for dm in range(KT):
                    wl = ([("scalar", ev_last), ("sync", ldp), psf["C"]]
                          if dm == 0 else [])
                    mm = E.op("tensor", lambda e, dmp=dmp, dm=dm, st=(dm == 0),
                              sp=(dm == KT - 1): e.matmul(
                        out=ps["C"][:, 0:512], lhsT=wproj_sbB[:, dm, dmp, :],
                        rhs=sdirB[:, dm, :], start=st, stop=sp,
                        skip_group_check=True), wl)
                sdirB_free = ("tensor", mm)
                evc = E.op("vector", lambda e, dmp=dmp, a=a, b=b: e.tensor_tensor(
                    out=s_acc[:, dmp, 2048 - b:2048 - a],
                    in0=ps["C"][:, 511::-1],
                    in1=s_acc[:, dmp, 2048 - b:2048 - a], op=ALU.add),
                    [("tensor", mm)])
                psf["C"] = ("vector", evc)

    stage3B(snapS3)
    if "s_acc" in taps:
        snap = E.snap()
        for k in range(KT):
            dma("sync", taps["s_acc"][k * 128:(k + 1) * 128, :], s_acc[:, k, :],
                E.bw(snap))
    r3a.close()

    s3_done = E.snap()
    bw4 = E.bw(s3_done)
    r3b = ExitStack(); sb3b = mk_sb(r3b)
    xnw = sb3b("xnw", [128, KT, 512])
    g0w = sb3b("g0w", [128, KT, 512])
    xres = sb3b("xres", [128, 512])
    rows4 = tuple(sb3b(f"rows4_{i}", [1, XR]) for i in range(2))
    rows1 = (rows4[0], rows4[1], sb3b("rows1_t", [1, TOK]))
    dma("sync", rows4[0][0:1, :], d_rows[0:1, :], bw4)
    dma("sync", rows4[1][0:1, :], d_rows[1:2, :], bw4)
    bload = E.cnt["sync"]
    x1_last = 0
    for wi, (a, b) in enumerate(WIN_OWN):
        fg = dma("sync", g0w[:], rw(d_g0, TOK)[:, :, a:b],
                 bw4 + [("sync*", bload), ("vector", x1_last)])
        for ot in range(KT):
            v1 = E.op("vector", lambda e, ot=ot, a=a: e.tensor_tensor(
                out=g0w[:, ot, :], in0=g0w[:, ot, :], in1=s_acc[:, ot, a:a + 512],
                op=ALU.mult), [("sync", fg), ("vector", x1_last)])
            fxr = dma("sync", xres[:],
                      rw(xf_in, XR)[:, ot, 3 + a:3 + b].unsqueeze(1),
                      [("vector", x1_last)])
            v2 = E.op("vector", lambda e, ot=ot, a=a: e.tensor_tensor(
                out=s_acc[:, ot, a:a + 512], in0=xres[:].unsqueeze(1),
                in1=g0w[:, ot, :].unsqueeze(1), op=ALU.add),
                [("vector", v1), ("sync", fxr)])
            x1_last = v2
            dma("gpsimd", d_x1[ot * 128:(ot + 1) * 128, a:a + 512],
                s_acc[:, ot, a:a + 512], [("vector", v2)])
    x1 = s_acc
    if "x1" in taps:
        snap = E.snap()
        for k in range(KT):
            dma("sync", taps["x1"][k * 128:(k + 1) * 128, :], x1[:, k, :], E.bw(snap))

    p4_done = E.snap()
    ln1_done = ln_stats(x1, TOK, rows1, E.bw(p4_done))
    xn1_done = 0
    for (a, b) in WIN_OWN:
        br = bcast2(rows1, a, 512, [("scalar", ln1_done)])
        for k in range(KT):
            v1 = E.op("vector", lambda e, k=k, a=a, b=b: e.tensor_tensor(
                out=x1[:, k, a:b], in0=x1[:, k, a:b], in1=ps_big[:, 0:512],
                op=ALU.subtract), [("tensor", br)])
            xn1_done = E.op("vector", lambda e, k=k, a=a, b=b: e.tensor_tensor(
                out=x1[:, k, a:b], in0=x1[:, k, a:b], in1=ps_big[:, 512:1024],
                op=ALU.mult), [("vector", v1)])
        psf["big"] = ("vector", xn1_done)
    for k in range(KT):
        dma("sync", d_xn1[k * 128:(k + 1) * 128, :], x1[:, k, :],
            [("vector", xn1_done)])
    if "xn1" in taps:
        snap = E.snap()
        dma("sync", taps["xn1"], d_xn1, E.bw(snap))
    r3b.close()
    r3.close()

    p4x_done = E.snap()
    bw5 = E.bw(p4x_done)
    r4 = ExitStack(); sb4 = mk_sb(r4)
    xn1_sb = sb4("xn1_sb", [128, KT, TOK])
    whold = sb4("whold", [128, KT, D])
    kv_t = sb4("kv_t", [128, 512])
    bq_sb = sb4("bq_sb", [128, KT, 1])
    bk_sb = sb4("bk_sb", [128, KT, 1])
    bv_b = sb4("bv_b", [128, D])
    xn1_ld = dma("sync", xn1_sb[:], rw(d_xn1, TOK), bw5)
    dma("sync", bq_sb[:], bq.unsqueeze(2), bw5)
    dma("sync", bk_sb[:], bk.unsqueeze(2), bw5)
    bvb = dma("sync", bv_b[:], bv_row.broadcast_to([128, D]), bw5)
    bld = E.cnt["sync"]
    vq = E.op("vector", lambda e: e.tensor_scalar(
        out=bq_sb[:], in0=bq_sb[:], scalar1=1.0 / 32.0, scalar2=None, op0=ALU.mult),
        [("sync*", bld)])
    whold_free = ("tensor", 0)
    kvt_free = ("gpsimd", 0)

    def proj_fm(wdram, dest, bias_sb, scale):
        nonlocal whold_free, kvt_free
        ld = dma("sync", whold[:], rw(wdram, D), bw5 + [whold_free])
        last_sp = 0
        for m in range(KT):
            for (a, b) in WIN_OWN:
                pslot = "A" if (m + a // 512) % 2 == 0 else "B"
                mm = 0
                for k in range(KT):
                    wl = ([("sync", ld), ("sync", xn1_ld), psf[pslot]]
                          if k == 0 else [])
                    mm = E.op("tensor", lambda e, m=m, k=k, a=a, b=b, pslot=pslot,
                              st=(k == 0), sp=(k == KT - 1): e.matmul(
                        out=ps[pslot][:, 0:512],
                        lhsT=whold[:, k, m * 128:(m + 1) * 128],
                        rhs=xn1_sb[:, k, a:b], start=st, stop=sp,
                        skip_group_check=True), wl)
                ev = E.op("scalar", lambda e, m=m, scale=scale, pslot=pslot: e.activation(
                    out=kv_t[:], in_=ps[pslot][:, 0:512], func=AF.Identity,
                    bias=bias_sb[:, m, 0:1], scale=scale),
                    [("tensor", mm), kvt_free, ("vector", vq)])
                psf[pslot] = ("scalar", ev)
                last_sp = dma("gpsimd", dest[m * 128:(m + 1) * 128, a:b], kv_t[:],
                              [("scalar", ev)])
                kvt_free = ("gpsimd", last_sp)
        whold_free = ("tensor", E.cnt["tensor"])
        return last_sp

    proj_fm(wqT, d_q, bq_sb, 1.0 / 32.0)
    kd = proj_fm(wkT, cc_k_in.ap(), bk_sb, 1.0)
    ldv = dma("sync", whold[:], rw(wvT, D), bw5 + [whold_free])
    vsp = 0
    for tt in range(16):
        for dvw in range(2):
            pslot = "C" if (tt + dvw) % 2 == 0 else "D"
            mm = 0
            for k in range(KT):
                wl = ([("sync", ldv), ("sync", xn1_ld), psf[pslot]]
                      if k == 0 else [])
                mm = E.op("tensor", lambda e, tt=tt, dvw=dvw, k=k, pslot=pslot,
                          st=(k == 0), sp=(k == KT - 1): e.matmul(
                    out=ps[pslot][:, 0:512],
                    lhsT=xn1_sb[:, k, tt * 128:(tt + 1) * 128],
                    rhs=whold[:, k, dvw * 512:(dvw + 1) * 512],
                    start=st, stop=sp, skip_group_check=True), wl)
            ev = E.op("vector", lambda e, dvw=dvw, pslot=pslot: e.tensor_tensor(
                out=kv_t[:], in0=ps[pslot][:, 0:512],
                in1=bv_b[:, dvw * 512:(dvw + 1) * 512], op=ALU.add),
                [("tensor", mm), ("sync", bvb), kvt_free])
            psf[pslot] = ("vector", ev)
            vsp = dma("gpsimd", cc_vin.ap()[tt * 128:(tt + 1) * 128,
                                             dvw * 512:(dvw + 1) * 512],
                      kv_t[:], [("vector", ev)])
            kvt_free = ("gpsimd", vsp)
    whold_free = ("tensor", E.cnt["tensor"])
    cc_k = E.cc(lambda e: e.collective_compute(
        "AllGather", ALU.bypass, replica_groups=RG,
        ins=[cc_k_in.ap().opt()], outs=[cc_k_out.ap().opt()]),
        [("gpsimd", vsp), ("gpsimd", kd)])
    cc_v = E.cc(lambda e: e.collective_compute(
        "AllGather", ALU.bypass, replica_groups=RG,
        ins=[cc_vin.ap().opt()], outs=[cc_vout.ap().opt()]), [("cc", cc_k)])
    if "q" in taps:
        snap = E.snap()
        dma("sync", taps["q"], d_q, E.bw(snap))
    r4.close()

    p5_done = E.snap()
    bw6 = [(s, v) for (s, v) in E.bw(p5_done) if s != "cc"]
    r5 = ExitStack(); sb5 = mk_sb(r5)
    hq = sb5("hq", [128, KT, TOK])
    tq = sb5("tq", [128, KT, TOK])
    rowsq_mr = tuple(sb5(f"rowsq_{i}", [1, TOK]) for i in range(2))
    wdeq = sb5("wdeq", [128, KT, D])
    qraw = sb5("qraw", [128, 512], I32)
    sraw = sb5("sraw", [128, 512])
    ar_sb = sb5("ar_sb", [32, TOK])
    la_sb = sb5("la_sb", [128, KT, 32])
    lb_sb = sb5("lb_sb", [32, D])
    qb_sb = sb5("qb_sb", [128, KT, 1])
    lgt_g = sb5("lgt_g", [128, KT, 1])
    lgt_b = sb5("lgt_b", [128, KT, 1])
    h_ready = dma("sync", hq[:], rw(d_xn1, TOK), bw6)
    wdeq_free = ("tensor", 0)

    def qlora_linear(idx, src, dst, add_residual, waits0):
        nonlocal wdeq_free
        dq_last = 0
        for dt_i in range(KT):
            for jh in range(2):
                anti = [("vector", dq_last), wdeq_free]
                l1 = dma("sync", qraw[:],
                         qTw[idx, dt_i * 128:(dt_i + 1) * 128,
                             jh * 512:(jh + 1) * 512], list(waits0) + anti)
                l2 = dma("sync", sraw[:],
                         scrT[idx, dt_i * 128:(dt_i + 1) * 128,
                              jh * 512:(jh + 1) * 512], list(waits0))
                s1 = E.op("scalar", lambda e, dt_i=dt_i, jh=jh: e.activation(
                    out=wdeq[:, dt_i, jh * 512:(jh + 1) * 512], in_=qraw[:],
                    func=AF.Copy, scale=1.0 / 7.5, bias=-1.0), [("sync", l2)])
                dq_last = E.op("vector", lambda e, dt_i=dt_i, jh=jh: e.tensor_tensor(
                    out=wdeq[:, dt_i, jh * 512:(jh + 1) * 512],
                    in0=wdeq[:, dt_i, jh * 512:(jh + 1) * 512], in1=sraw[:],
                    op=ALU.mult), [("scalar", s1)])
        lb_ld = dma("sync", lb_sb[:], lbT[idx], list(waits0))
        la_ld = dma("sync", la_sb[:], laT[idx].rearrange("(k p) r -> p k r", p=128),
                    list(waits0))
        qb_ld = dma("sync", qb_sb[:], qb[idx].unsqueeze(2), list(waits0))
        ar_done = 0
        for (a, b) in WIN_OWN:
            mm = 0
            for k in range(KT):
                wl = ([("sync", la_ld), psf["D"]] + list(waits0)) if k == 0 else []
                mm = E.op("tensor", lambda e, k=k, a=a, b=b, st=(k == 0),
                          sp=(k == KT - 1): e.matmul(
                    out=ps["D"][0:32, 0:512], lhsT=la_sb[:, k, :], rhs=src[:, k, a:b],
                    start=st, stop=sp, skip_group_check=True), wl)
            ar_done = E.op("vector", lambda e, a=a: e.tensor_copy(
                out=ar_sb[:, a:a + 512], in_=ps["D"][0:32, 0:512]), [("tensor", mm)])
            psf["D"] = ("vector", ar_done)
        out_last = 0
        for m in range(KT):
            for (a, b) in WIN_OWN:
                pslot = "A" if (m + a // 512) % 2 == 0 else "B"
                mm = 0
                for k in range(KT):
                    wl = ([("vector", dq_last), psf[pslot]] + list(waits0)
                          if k == 0 else [])
                    mm = E.op("tensor", lambda e, m=m, k=k, a=a, b=b, pslot=pslot,
                              st=(k == 0): e.matmul(
                        out=ps[pslot][:, 0:512],
                        lhsT=wdeq[:, k, m * 128:(m + 1) * 128], rhs=src[:, k, a:b],
                        start=st, stop=False, skip_group_check=True), wl)
                mm = E.op("tensor", lambda e, m=m, a=a, b=b, pslot=pslot: e.matmul(
                    out=ps[pslot][:, 0:512], lhsT=lb_sb[:, m * 128:(m + 1) * 128],
                    rhs=ar_sb[:, a:b], start=False, stop=True, skip_group_check=True),
                    [("sync", lb_ld), ("vector", ar_done)])
                if add_residual:
                    sev = E.op("scalar", lambda e, m=m, pslot=pslot: e.activation(
                        out=sraw[:], in_=ps[pslot][:, 0:512], func=AF.Identity,
                        bias=qb_sb[:, m, 0:1]),
                        [("tensor", mm), ("sync", qb_ld),
                         ("vector", out_last if isinstance(out_last, int) else 0)])
                    psf[pslot] = ("scalar", sev)
                    out_last = E.op("vector", lambda e, m=m, a=a: e.tensor_tensor(
                        out=dst[:, m, a:a + 512], in0=sraw[:],
                        in1=hq[:, m, a:a + 512], op=ALU.add), [("scalar", sev)])
                else:
                    sev = E.op("scalar", lambda e, m=m, a=a, pslot=pslot: e.activation(
                        out=tq[:, m, a:a + 512], in_=ps[pslot][:, 0:512],
                        func=AF.Identity,
                        bias=qb_sb[:, m, 0:1]), [("tensor", mm), ("sync", qb_ld)])
                    psf[pslot] = ("scalar", sev)
                    out_last = sev
        wdeq_free = ("tensor", E.cnt["tensor"])
        return out_last

    prevw = [("sync", h_ready)]
    for blk in range(3):
        m1 = qlora_linear(2 * blk, hq, tq, False,
                          prevw + (bw6 if blk == 0 else []))
        if blk == 0 and "ql0" in taps:
            snap = E.snap()
            for k in range(KT):
                dma("sync", taps["ql0"][k * 128:(k + 1) * 128, :], tq[:, k, :],
                    E.bw(snap))
        lb_ld2 = dma("sync", lgt_b[:], lnb[blk].unsqueeze(2), bw6)
        lg_ld = dma("sync", lgt_g[:], lng[blk].unsqueeze(2), bw6)
        lnm = ln_stats(tq, TOK, (rowsq_mr[0], rowsq_mr[1], ar_sb), [("scalar", m1)])
        gm_last = 0
        for (a, b) in WIN_OWN:
            br = bcast2(rowsq_mr, a, 512, [("scalar", lnm)])
            for k in range(KT):
                v1 = E.op("vector", lambda e, k=k, a=a, b=b: e.tensor_tensor(
                    out=tq[:, k, a:b], in0=tq[:, k, a:b], in1=ps_big[:, 0:512],
                    op=ALU.subtract), [("tensor", br)])
                v2 = E.op("vector", lambda e, k=k, a=a, b=b: e.tensor_tensor(
                    out=tq[:, k, a:b], in0=tq[:, k, a:b], in1=ps_big[:, 512:1024],
                    op=ALU.mult), [("vector", v1)])
                v3 = E.op("vector", lambda e, k=k, a=a, b=b: e.tensor_scalar(
                    out=tq[:, k, a:b], in0=tq[:, k, a:b], scalar1=lgt_g[:, k, 0:1],
                    scalar2=lgt_b[:, k, 0:1], op0=ALU.mult, op1=ALU.add),
                    [("vector", v2), ("sync", lg_ld), ("sync", lb_ld2)])
                gm_last = E.op("scalar", lambda e, k=k, a=a, b=b: e.activation(
                    out=tq[:, k, a:b], in_=tq[:, k, a:b], func=AF.Gelu),
                    [("vector", v3)])
            psf["big"] = ("scalar", gm_last)
        if blk == 0 and "qln0" in taps:
            snap = E.snap()
            for k in range(KT):
                dma("sync", taps["qln0"][k * 128:(k + 1) * 128, :], tq[:, k, :],
                    E.bw(snap))
        ql2 = qlora_linear(2 * blk + 1, tq, hq, True, [("scalar", gm_last)])
        prevw = [("vector", ql2)]
        if blk == 0 and "qh0" in taps:
            snap = E.snap()
            for k in range(KT):
                dma("sync", taps["qh0"][k * 128:(k + 1) * 128, :], hq[:, k, :],
                    E.bw(snap))
    g1_done = 0
    for k in range(KT):
        g1_done = E.op("scalar", lambda e, k=k: e.activation(
            out=hq[:, k, :], in_=hq[:, k, :], func=AF.Silu),
            [("vector", E.cnt["vector"])])
        dma("sync", d_g[k * 128:(k + 1) * 128, :], hq[:, k, :],
            [("scalar", g1_done)])
    if "g1" in taps:
        snap = E.snap()
        dma("sync", taps["g1"], d_g, E.bw(snap))
    if "kf" in taps:
        snap = E.snap()
        dma("sync", taps["kf"], cc_k_out.ap(), E.bw(snap) + [("cc", E.cnt["cc"])])
    if "vf" in taps:
        snap = E.snap()
        dma("sync", taps["vf"], cc_vout.ap(), E.bw(snap) + [("cc", E.cnt["cc"])])
    r5.close()

    p6_done = E.snap()
    bw7 = E.bw(p6_done) + [("cc", E.cnt["cc"])]
    r6 = ExitStack(); sb6 = mk_sb(r6)
    kslab = sb6("kslab", [128, KT, 512])
    vslab = sb6("vslab", [128, 4, D])
    qtile = sb6("qtile", [128, KT, 128])
    wsm = sb6("wsm", [128, 512])
    wTs = sb6("wTs", [128, 4, TOK])
    mx4 = sb6("mx4", [128, 4])
    den4 = sb6("den4", [128, 4])
    nrm = sb6("nrm", [128, 16, 32])
    nrt = sb6("nrt", [128, 16])
    nr_row = sb6("nr_row", [16, 128])
    at_acc = sb6("at_acc", [128, KT, TOK])
    sattn = sb6("sattn", [128, KT, 512])
    xf_t2 = sb6("xf_t2", [128, KT, 512])
    g1w = sb6("g1w", [128, KT, 512])
    rnb = sb6("rnb", [128, 512])
    wo_sl = sb6("wo_sl", [128, KT, 128])
    bo_sb = sb6("bo_sb", [128, KT, 1])
    ident = sb6("ident", [128, 128])
    bo_ld = dma("sync", bo_sb[:], bo.unsqueeze(2), bw7)
    id_ld = dma("sync", ident[:], ident_in, bw7)
    acc_zero = 0
    for k in range(KT):
        acc_zero = E.op("vector", lambda e, k=k: e.memset(at_acc[:, k, :], 0.0),
                        bw7 if k == 0 else [])
    acc_zero = E.op("vector", lambda e: e.memset(nrm[:], 0.0))
    wsm_free = ("tensor", 0)
    wts_free = ("tensor", 0)
    psbig_free = psf["big"]
    qt_free = ("tensor", 0)

    for sl in range(8):
        half, q4 = sl // 4, sl % 4
        lk = dma("sync", kslab[:],
                 cc_k_out.ap()[half].rearrange("(k p) t -> p k t", p=128)
                 [:, :, q4 * 512:(q4 + 1) * 512],
                 bw7 + [("tensor", E.cnt["tensor"])])
        lv = dma("sync", vslab[:],
                 cc_vout.ap()[half, q4 * 512:(q4 + 1) * 512, :]
                 .rearrange("(s p) d -> p s d", p=128),
                 bw7 + [("tensor", E.cnt["tensor"])])
        wt_last = 0
        for qt_i in range(16):
            lq = dma("gpsimd", qtile[:],
                     d_q.rearrange("(k p) t -> p k t", p=128)
                     [:, :, qt_i * 128:(qt_i + 1) * 128], bw7 + [qt_free])
            mm = 0
            for k in range(KT):
                wl = ([("sync", lk), ("gpsimd", lq), psf["A"],
                       ("vector", acc_zero)] if k == 0 else [])
                mm = E.op("tensor", lambda e, k=k, st=(k == 0),
                          sp=(k == KT - 1): e.matmul(
                    out=ps["A"][:, 0:512], lhsT=qtile[:, k, :], rhs=kslab[:, k, :],
                    start=st, stop=sp, skip_group_check=True), wl)
            qt_free = ("tensor", mm)
            r1m = E.op("vector", lambda e: e.tensor_reduce(
                out=mx4[:], in_=ps["A"][:, 0:512].rearrange("p (g j) -> p g j", j=128),
                axis=AX.X, op=ALU.max), [("tensor", mm)])
            r2m = E.op("vector", lambda e: e.tensor_tensor(
                out=wsm[:].rearrange("p (g j) -> p g j", j=128),
                in0=ps["A"][:, 0:512].rearrange("p (g j) -> p g j", j=128),
                in1=mx4[:].unsqueeze(2).broadcast_to([128, 4, 128]),
                op=ALU.subtract),
                [("vector", r1m), wsm_free])
            psf["A"] = ("vector", r2m)
            se = 0
            for g in range(4):
                se = E.op("scalar", lambda e, g=g: e.activation(
                    out=wsm[:, g * 128:(g + 1) * 128],
                    in_=wsm[:, g * 128:(g + 1) * 128], func=AF.Exp,
                    accum_out=den4[:, g:g + 1]), [("vector", r2m)])
            rc = E.op("vector", lambda e: e.reciprocal(out=mx4[:], in_=den4[:]),
                      [("scalar", se)])
            wn = E.op("vector", lambda e: e.tensor_tensor(
                out=wsm[:].rearrange("p (g j) -> p g j", j=128),
                in0=wsm[:].rearrange("p (g j) -> p g j", j=128),
                in1=mx4[:].unsqueeze(2).broadcast_to([128, 4, 128]),
                op=ALU.mult), [("vector", rc)])
            E.op("vector", lambda e, qt_i=qt_i, sl=sl: e.tensor_tensor(
                out=nrm[:, qt_i, sl * 4:(sl + 1) * 4], in0=den4[:], in1=mx4[:],
                op=ALU.mult), [("vector", wn)])
            tp = 0
            for g in range(4):
                wl = [("vector", wn)] + ([psbig_free, ("vector", wt_last),
                                          ("sync", id_ld)] if g == 0 else [])
                tp = E.op("tensor", lambda e, g=g: e.transpose(
                    out=ps_big[:, g * 128:(g + 1) * 128],
                    in_=wsm[:, g * 128:(g + 1) * 128], identity=ident[:]), wl)
            wsm_free = ("tensor", tp)
            wt_last = E.op("vector", lambda e, qt_i=qt_i: e.tensor_copy(
                out=wTs[:, 0:4, qt_i * 128:(qt_i + 1) * 128],
                in_=ps_big[:, 0:512].rearrange("p (g j) -> p g j", j=128)),
                [("tensor", tp), wts_free])
            psbig_free = ("vector", wt_last)
        for dv in range(KT):
            for (a, b) in WIN_OWN:
                mm = 0
                for g in range(4):
                    wl = ([("sync", lv), ("vector", wt_last), psf["C"]]
                          if g == 0 else [])
                    mm = E.op("tensor", lambda e, dv=dv, g=g, a=a, b=b,
                              st=(g == 0), sp=(g == 3): e.matmul(
                        out=ps["C"][:, 0:512],
                        lhsT=vslab[:, g, dv * 128:(dv + 1) * 128],
                        rhs=wTs[:, g, a:b], start=st, stop=sp,
                        skip_group_check=True), wl)
                avm = E.op("vector", lambda e, dv=dv, a=a: e.tensor_tensor(
                    out=at_acc[:, dv, a:a + 512], in0=ps["C"][:, 0:512],
                    in1=at_acc[:, dv, a:a + 512], op=ALU.add), [("tensor", mm)])
                psf["C"] = ("vector", avm)
        wts_free = ("tensor", E.cnt["tensor"])

    nr1 = E.op("vector", lambda e: e.tensor_reduce(
        out=nrt[:], in_=nrm[:], axis=AX.X, op=ALU.add),
        [("vector", E.cnt["vector"])])
    nr2 = E.op("vector", lambda e: e.tensor_scalar(
        out=nrt[:], in0=nrt[:], scalar1=1e-6, scalar2=None, op0=ALU.add),
        [("vector", nr1)])
    nr3 = E.op("vector", lambda e: e.reciprocal(out=nrt[:], in_=nrt[:]),
               [("vector", nr2)])
    ntp = E.op("tensor", lambda e: e.transpose(
        out=ps["D"][0:16, 0:128], in_=nrt[:], identity=ident[:]),
        [("vector", nr3), psf["D"], ("sync", id_ld)])
    psf["D"] = ("tensor", ntp)
    nrow = E.op("vector", lambda e: e.tensor_copy(out=nr_row[:], in_=ps["D"][0:16, 0:128]),
                [("tensor", ntp)])
    psf["D"] = ("vector", nrow)
    nb = dma("sync", d_nrm.rearrange("a (q j) -> (a q) j", j=128), nr_row[:],
             [("vector", nrow)])
    if "attn" in taps:
        snap = E.snap()
        for k in range(KT):
            dma("sync", taps["attn"][k * 128:(k + 1) * 128, :], at_acc[:, k, :],
                E.bw(snap))

    wo_used = [0, 0]
    out_last = 0
    sc_last = 0
    for wi, (a, b) in enumerate(WIN_OWN):
        rb = dma("sync", rnb[:], d_nrm[0:1, a:b].broadcast_to([128, 512]),
                 [("sync", nb), ("vector", sc_last)])
        sc = 0
        for dv in range(KT):
            sc = E.op("vector", lambda e, dv=dv, a=a: e.tensor_tensor(
                out=at_acc[:, dv, a:a + 512], in0=at_acc[:, dv, a:a + 512],
                in1=rnb[:], op=ALU.mult), [("sync", rb)])
        sc_last = sc
        fx1 = dma("gpsimd", xf_t2[:],
                  d_x1.rearrange("(k p) t -> p k t", p=128)[:, :, a:b],
                  [("vector", out_last)])
        fg1 = dma("gpsimd", g1w[:],
                  d_g.rearrange("(k p) t -> p k t", p=128)[:, :, a:b],
                  [("vector", out_last)])
        for m in range(KT):
            pslot = "A" if m % 2 == 0 else "B"
            ldw = dma("gpsimd", wo_sl[:],
                      rw(woT, D)[:, :, m * 128:(m + 1) * 128],
                      [("tensor", wo_used[0])])
            mm = 0
            for k in range(KT):
                wl = ([("gpsimd", ldw), ("vector", sc), psf[pslot]]
                      if k == 0 else [])
                mm = E.op("tensor", lambda e, m=m, k=k, a=a, pslot=pslot,
                          st=(k == 0), sp=(k == KT - 1): e.matmul(
                    out=ps[pslot][:, 0:512],
                    lhsT=wo_sl[:, k, :],
                    rhs=at_acc[:, k, a:a + 512], start=st, stop=sp,
                    skip_group_check=True), wl)
            wo_used[0] = mm
            sev = E.op("scalar", lambda e, m=m, pslot=pslot: e.activation(
                out=sattn[:, m, :], in_=ps[pslot][:, 0:512], func=AF.Identity,
                bias=bo_sb[:, m, 0:1]), [("tensor", mm), ("sync", bo_ld)])
            psf[pslot] = ("scalar", sev)
            v1 = E.op("vector", lambda e, m=m: e.tensor_tensor(
                out=sattn[:, m, :], in0=sattn[:, m, :], in1=g1w[:, m, :],
                op=ALU.mult), [("scalar", sev), ("gpsimd", fg1)])
            v2 = E.op("vector", lambda e, m=m: e.tensor_tensor(
                out=sattn[:, m, :], in0=sattn[:, m, :], in1=xf_t2[:, m, :],
                op=ALU.add), [("vector", v1), ("gpsimd", fx1)])
            out_last = v2
            dma("sync", out_fm[m * 128:(m + 1) * 128, a:b], sattn[:, m, :],
                [("vector", v2)])
    if "sattn" in taps:
        snap = E.snap()
        dma("sync", taps["sattn"], out_fm, E.bw(snap))
    r6.close()

    fin = E.snap()
    for e in ENGINES:
        E.q[e].append((None, [("sync*", fin["sync"]), ("gpsimd*", fin["gpsimd"])],
                       None, None))

    with nc.Block() as block:
        def runner(name):
            def go(eng):
                for (fn, waits, inc_sem, inc_val) in E.q[name]:
                    resolved = {}
                    def addw(h, v):
                        if v > resolved.get(h, 0):
                            resolved[h] = v
                    for (s, v) in waits:
                        if s in ("sync", "gpsimd"):
                            j, c, _ = E.qs[s]["map"][v]
                            addw((s, j), 16 * c)
                        elif s in ("sync*", "gpsimd*"):
                            qn = s[:-1]
                            if v > 0:
                                vec = E.qs[qn]["map"][v][2]
                                for j in range(NQS):
                                    if vec[j] > 0:
                                        addw((qn, j), 16 * vec[j])
                        else:
                            addw(s, v)
                    for h, v in resolved.items():
                        if isinstance(h, tuple):
                            eng.wait_ge(qsems[h[0]][h[1]], v)
                        else:
                            eng.wait_ge(sem[h], v)
                    if fn is None:
                        continue
                    if inc_sem in ("sync", "gpsimd"):
                        j, c, _ = E.qs[inc_sem]["map"][inc_val]
                        if c > 1:
                            eng.wait_ge(qsems[inc_sem][j], 16 * (c - 1))
                        ins = fn(eng)
                        ins.then_inc(qsems[inc_sem][j], 16)
                    else:
                        ins = fn(eng)
                        ins.then_inc(sem[inc_sem], 1)
            return go
        block.tensor(runner("tensor"))
        block.vector(runner("vector"))
        block.scalar(runner("scalar"))
        block.gpsimd(runner("gpsimd"))
        block.sync(runner("sync"))

    es.close()
    return nc


def _host_inputs(inputs):
    x = np.asarray(inputs["L0_attn_x"])
    cores = []
    T = lambda a: np.ascontiguousarray(a, dtype=np.float32)

    def pcol(arr, cols):
        a = np.asarray(arr).reshape(-1, 128, cols)
        return np.ascontiguousarray(
            a.transpose(1, 0, 2).reshape(128, -1), dtype=np.float32)

    for c in range(8):
        b, h = c // 2, c % 2
        xb = x[b]
        if h == 0:
            own = xb[0:2048]
            ah = np.zeros((3, D), np.float32)
            bh = xb[2048:2051][::-1]
            A_sfx, B_sfx = "f", "b"
        else:
            own = xb[2048:4096][::-1]
            ah = np.zeros((3, D), np.float32)
            bh = xb[2045:2048]
            A_sfx, B_sfx = "b", "f"
        xall = np.concatenate([ah, own, bh], axis=0)
        d = {"xf": T(xall.T)}
        for pn, sfx in (("a", A_sfx), ("b", B_sfx)):
            d[pn + "_winT"] = T(np.asarray(inputs[f"L0_attn_win_{sfx}"]).T)
            d[pn + "_wconv"] = pcol(np.asarray(inputs[f"L0_attn_wconv_{sfx}"]).reshape(DI, 4), 4)
            d[pn + "_bconv"] = pcol(inputs[f"L0_attn_bconv_{sfx}"], 1)
            d[pn + "_wxT"] = T(np.asarray(inputs[f"L0_attn_wx_{sfx}"]).T)
            d[pn + "_wdtT"] = T(np.asarray(inputs[f"L0_attn_wdt_{sfx}"]).T)
            d[pn + "_bdt"] = pcol(inputs[f"L0_attn_bdt_{sfx}"], 1)
            d[pn + "_alog"] = pcol(inputs[f"L0_attn_Alog_{sfx}"], NS)
            d[pn + "_dp"] = pcol(inputs[f"L0_attn_Dp_{sfx}"], 1)
            d[pn + "_woutT"] = T(np.asarray(inputs[f"L0_attn_wout_{sfx}"]).T)
            half = slice(0, D) if sfx == "f" else slice(D, 2 * D)
            d[pn + "_wprojT"] = T(np.asarray(inputs["L0_attn_wproj"])[:, half].T)
        d["bproj"] = pcol(inputs["L0_attn_bproj"], 1)
        d["w1"] = T(inputs["L0_ffn_W1"])
        d["b1"] = pcol(inputs["L0_ffn_b1"], 1)
        d["w2"] = T(inputs["L0_ffn_W2"])
        d["b2"] = pcol(inputs["L0_ffn_b2"], 1)
        d["wqT"] = T(np.asarray(inputs["L1_attn_Wq"]).T)
        d["bq"] = pcol(inputs["L1_attn_bq"], 1)
        d["wkT"] = T(np.asarray(inputs["L1_attn_Wk"]).T)
        d["bk"] = pcol(inputs["L1_attn_bk"], 1)
        d["wvT"] = T(np.asarray(inputs["L1_attn_Wv"]).T)
        d["bv_row"] = T(np.asarray(inputs["L1_attn_bv"]).reshape(1, D))
        d["woT"] = T(np.asarray(inputs["L1_attn_Wo"]).T)
        d["bo"] = pcol(inputs["L1_attn_bo"], 1)
        d["qTw"] = np.ascontiguousarray(
            np.transpose(np.asarray(inputs["L1_ffn_qweights"]), (0, 2, 1)),
            dtype=np.int32)
        scr = np.repeat(np.asarray(inputs["L1_ffn_scales"]), 16, axis=2)
        d["scrT"] = T(np.transpose(scr, (0, 2, 1)))
        d["qb"] = np.stack([pcol(np.asarray(inputs["L1_ffn_biases"])[i], 1) for i in range(6)])
        d["laT"] = T(np.transpose(np.asarray(inputs["L1_ffn_lora_a"]), (0, 2, 1)))
        d["lbT"] = T(np.transpose(np.asarray(inputs["L1_ffn_lora_b"]), (0, 2, 1)))
        d["lng"] = np.stack([pcol(np.asarray(inputs["L1_ffn_ln_gamma"])[i], 1) for i in range(3)])
        d["lnb"] = np.stack([pcol(np.asarray(inputs["L1_ffn_ln_beta"])[i], 1) for i in range(3)])
        d["ident_in"] = np.eye(128, dtype=np.float32)
        cores.append(d)
    return cores


def kernel(**inputs):
    global LAST_TAPS
    nc = build(tuple(TAPS))
    in_maps = _host_inputs(inputs)
    res = run_bass_kernel_spmd(nc, in_maps, core_ids=list(range(8)))
    out = np.zeros((4, SEQ, D), np.float32)
    LAST_TAPS = {}
    for c in range(8):
        b, h = c // 2, c % 2
        o = np.asarray(res.results[c]["out_fm"])
        if h == 0:
            out[b, 0:2048] = o.T
        else:
            out[b, 2048:4096] = o.T[::-1]
        for nm in TAPS:
            LAST_TAPS.setdefault(nm, []).append(np.asarray(res.results[c]["tap_" + nm]))
    return out


# version 08
# speedup vs baseline: 1.0099x; 1.0099x over previous
import numpy as np
import concourse.bass as bass
import concourse.mybir as mybir
from concourse.bass_utils import run_bass_kernel_spmd
from contextlib import ExitStack

F32 = mybir.dt.float32
I32 = mybir.dt.int32
AF = mybir.ActivationFunctionType
ALU = mybir.AluOpType
AX = mybir.AxisListType

D = 1024
KT = 8
DI = 2048
DIT = 16
NS = 16
DTR = 64
TOK = 2048
GRID = 2051
XR = 2054
SEQ = 4096
EPS = 1e-5
WIN_OWN = [(a, a + 512) for a in range(0, 2048, 512)]
WIN_GRID = WIN_OWN + [(2048, 2051)]
ENGINES = ("tensor", "vector", "scalar", "gpsimd", "sync")
RG = [[0, 1], [2, 3], [4, 5], [6, 7]]

TAPS = []
LAST_TAPS = {}
SERIAL = False
SERIAL_SITES = set()


NQS = 6


class Em:
    def __init__(self):
        self.q = {e: [] for e in ENGINES}
        self.cnt = {e: 0 for e in ENGINES}
        self.cnt["cc"] = 0
        self.qs = {q: {"idx": 0, "c": [0] * NQS, "map": {}} for q in
                   ("sync", "gpsimd")}

    def op(self, eng, fn, waits=(), site=None):
        if (SERIAL is True or (SERIAL and SERIAL is not False and eng in SERIAL)
                or (site is not None and site in SERIAL_SITES)):
            waits = list(waits) + [(e + "*", self.cnt[e]) if e in ("sync", "gpsimd")
                                   else (e, self.cnt[e])
                                   for e in ENGINES if e != eng] + \
                    ([("cc", self.cnt["cc"])] if self.cnt["cc"] else [])
        step = 16 if eng in ("sync", "gpsimd") else 1
        self.cnt[eng] += step
        if eng in ("sync", "gpsimd"):
            qs = self.qs[eng]
            j = qs["idx"]; qs["idx"] = (j + 1) % NQS
            qs["c"][j] += 1
            qs["map"][self.cnt[eng]] = (j, qs["c"][j], tuple(qs["c"]))
        self.q[eng].append((fn, [(s, v) for (s, v) in waits if v > 0],
                            eng, self.cnt[eng]))
        return self.cnt[eng]

    def cc(self, fn, waits=()):
        self.cnt["cc"] += 1
        self.q["gpsimd"].append((fn, [(s, v) for (s, v) in waits if v > 0],
                                 "cc", self.cnt["cc"]))
        return self.cnt["cc"]

    def snap(self):
        return dict(self.cnt)

    def bw(self, snap_):
        w = []
        for e in ENGINES:
            w.append((e + "*", snap_[e]) if e in ("sync", "gpsimd")
                     else (e, snap_[e]))
        if snap_.get("cc"):
            w.append(("cc", snap_["cc"]))
        return w


def build(dbg_taps=()):
    nc = bass.Bass("TRN2", target_bir_lowering=False)
    E = Em()

    def din(name, shape, dt=F32):
        return nc.dram_tensor(name, shape, dt, kind="ExternalInput").ap()

    xf_in = din("xf", [D, XR])
    pw = {}
    for p in ("a", "b"):
        pw[p] = dict(
            winT=din(p + "_winT", [D, 2 * DI]),
            wconv=din(p + "_wconv", [128, DIT * 4]),
            bconv=din(p + "_bconv", [128, DIT]),
            wxT=din(p + "_wxT", [DI, 96]),
            wdtT=din(p + "_wdtT", [DTR, DI]),
            bdt=din(p + "_bdt", [128, DIT]),
            alog=din(p + "_alog", [128, DIT * NS]),
            dp=din(p + "_dp", [128, DIT]),
            woutT=din(p + "_woutT", [DI, D]),
            wprojT=din(p + "_wprojT", [D, D]),
        )
    bproj = din("bproj", [128, KT])
    w1 = din("w1", [D, 4 * D]); b1 = din("b1", [128, 32])
    w2 = din("w2", [4 * D, D]); b2 = din("b2", [128, KT])
    wqT = din("wqT", [D, D]); bq = din("bq", [128, KT])
    wkT = din("wkT", [D, D]); bk = din("bk", [128, KT])
    wvT = din("wvT", [D, D]); bv_row = din("bv_row", [1, D])
    woT = din("woT", [D, D]); bo = din("bo", [128, KT])
    qTw = din("qTw", [6, D, D], I32)
    scrT = din("scrT", [6, D, D])
    qb = din("qb", [6, 128, KT])
    laT = din("laT", [6, D, 32])
    lbT = din("lbT", [6, 32, D])
    lng = din("lng", [3, 128, KT])
    lnb = din("lnb", [3, 128, KT])
    ident_in = din("ident_in", [128, 128])

    out_fm = nc.dram_tensor("out_fm", [D, TOK], F32, kind="ExternalOutput").ap()

    d_u = [nc.dram_tensor(f"d_u{p}", [DIT, 128, TOK], F32).ap() for p in range(2)]
    d_zs = [nc.dram_tensor(f"d_zs{p}", [DIT, 128, TOK], F32).ap() for p in range(2)]
    d_dl = [nc.dram_tensor(f"d_dl{p}", [DIT, 128, TOK], F32).ap() for p in range(2)]
    d_y = [nc.dram_tensor(f"d_y{p}", [DIT, 128, TOK], F32).ap() for p in range(2)]
    d_bc = [nc.dram_tensor(f"d_bc{p}", [2, NS, TOK], F32).ap() for p in range(2)]
    cc_st_in = nc.dram_tensor("cc_st_in", [DI, NS], F32)
    cc_st_out = nc.dram_tensor("cc_st_out", [DI, NS], F32)
    cc_k_in = nc.dram_tensor("cc_k_in", [D, TOK], F32)
    cc_k_out = nc.dram_tensor("cc_k_out", [2, D, TOK], F32)
    cc_vin = nc.dram_tensor("cc_vin", [TOK, D], F32)
    cc_vout = nc.dram_tensor("cc_vout", [2, TOK, D], F32)
    d_q = nc.dram_tensor("d_q", [D, TOK], F32).ap()
    d_x1 = nc.dram_tensor("d_x1", [D, TOK], F32).ap()
    d_xn1 = nc.dram_tensor("d_xn1", [D, TOK], F32).ap()
    d_g = nc.dram_tensor("d_g", [D, TOK], F32).ap()
    d_nrm = nc.dram_tensor("d_nrm", [1, TOK], F32).ap()
    d_rows = nc.dram_tensor("d_rows", [2, XR], F32).ap()
    d_s = nc.dram_tensor("d_s", [D, TOK], F32).ap()
    d_g0 = nc.dram_tensor("d_g0", [D, TOK], F32).ap()

    taps = {}
    tap_shapes = {"s_acc": [D, TOK], "x1": [D, TOK], "xn": [D, XR], "g1": [D, TOK],
                  "xn1": [D, TOK], "y0": [DIT, 128, TOK], "u0": [DIT, 128, TOK],
                  "dl0": [DIT, 128, TOK], "zs0": [DIT, 128, TOK], "bc0": [2, NS, TOK],
                  "y1": [DIT, 128, TOK], "q": [D, TOK], "kf": [2, D, TOK],
                  "vf": [2, TOK, D], "attn": [D, TOK], "sattn": [D, TOK],
                  "st": [DI, NS], "bi": [DI, NS],
                  "ql0": [D, TOK], "qln0": [D, TOK], "qh0": [D, TOK], "sA": [D, TOK],
                  "ygA": [DIT, 128, 512], "gfA": [2, 128, 512], "yg2": [2, 128, 512], "g0": [D, TOK],
                  "u1t": [DIT, 128, TOK], "dl1t": [DIT, 128, TOK], "bc1t": [2, NS, TOK]}
    for nm in dbg_taps:
        taps[nm] = nc.dram_tensor("tap_" + nm, tap_shapes[nm], F32,
                                  kind="ExternalOutput").ap()

    es = ExitStack()
    sem = {}
    for name in ("tensor", "vector", "scalar", "cc"):
        sem[name] = es.enter_context(nc.semaphore("sem_" + name))
    qsems = {}
    for qn in ("sync", "gpsimd"):
        qsems[qn] = [es.enter_context(nc.semaphore(f"sem_{qn}{j}"))
                     for j in range(NQS)]

    def mk_sb(stack):
        return lambda n, shp, dt=F32: stack.enter_context(nc.sbuf_tensor(n, shp, dt))

    sb = mk_sb(es)
    ones_col = sb("ones_col", [128, 1])
    ones_row = sb("ones_row", [1, 128])
    sqt = sb("sqt", [128, 512])
    small = {}
    for p in range(2):
        small[p] = dict(
            wconv=sb(f"wconv{p}", [128, DIT, 4]),
            bconv=sb(f"bconv{p}", [128, DIT, 1]),
            bdt=sb(f"bdt{p}", [128, DIT, 1]),
            A=sb(f"A{p}", [128, DIT, NS]),
            dp=sb(f"dp{p}", [128, DIT, 1]),
        )
    bproj_sb = sb("bproj_sb", [128, KT, 1])
    states = sb("states", [128, DIT, NS])
    binit = sb("binit", [128, DIT, NS])

    ps = {}
    for nm in ("A", "B", "C", "D"):
        ps[nm] = es.enter_context(nc.psum_tensor("ps" + nm, [128, 512], F32))
    ps_big = es.enter_context(nc.psum_tensor("ps_big", [128, 2048], F32))
    psf = {k: ("vector", 0) for k in ("A", "B", "C", "D", "big")}

    def dma(qeng, out, in_, waits=(), site=None):
        return E.op(qeng, lambda e, o=out, i=in_: e.dma_start(out=o, in_=i), waits,
                    site=site)

    def rw(dram_ap, cols):
        return dram_ap.rearrange("(k p) n -> p k n", p=128)

    def ln_stats(xt, ncols, rows, waits_in):
        rm, rs, rt = rows
        wins = [(a, min(a + 512, ncols)) for a in range(0, ncols, 512)]
        vm = 0
        for (a, b) in wins:
            w = b - a
            last = 0
            for k in range(KT):
                wl = list(waits_in) + [psf["A"], ("vector", vm)] if k == 0 else []
                last = E.op("tensor", lambda e, k=k, a=a, b=b, w=w: e.matmul(
                    out=ps["A"][0:1, 0:w], lhsT=ones_col[:], rhs=xt[:, k, a:b],
                    start=(k == 0), stop=(k == KT - 1), skip_group_check=True), wl)
            m1 = E.op("vector", lambda e, a=a, w=w: e.tensor_scalar(
                out=rm[0:1, a:a + w], in0=ps["A"][0:1, 0:w], scalar1=1.0 / D,
                scalar2=None, op0=ALU.mult), [("tensor", last)])
            psf["A"] = ("vector", m1)
            last = 0
            for k in range(KT):
                wl = list(waits_in) + [psf["B"]] if k == 0 else [("tensor", last)]
                sqm = E.op("scalar", lambda e, k=k, a=a, b=b: e.activation(
                    out=sqt[:, 0:b - a], in_=xt[:, k, a:b], func=AF.Square), wl)
                last = E.op("tensor", lambda e, k=k, w=w: e.matmul(
                    out=ps["B"][0:1, 0:w], lhsT=ones_col[:], rhs=sqt[:, 0:w],
                    start=(k == 0), stop=(k == KT - 1), skip_group_check=True),
                    [("scalar", sqm)])
            vm = E.op("vector", lambda e, a=a, w=w: e.tensor_scalar(
                out=rs[0:1, a:a + w], in0=ps["B"][0:1, 0:w], scalar1=1.0 / D,
                scalar2=None, op0=ALU.mult), [("tensor", last)])
            psf["B"] = ("vector", vm)
        v1 = E.op("vector", lambda e, n=ncols: e.tensor_tensor(
            out=rt[0:1, 0:n], in0=rm[0:1, 0:n], in1=rm[0:1, 0:n], op=ALU.mult))
        v2 = E.op("vector", lambda e, n=ncols: e.tensor_tensor(
            out=rs[0:1, 0:n], in0=rs[0:1, 0:n], in1=rt[0:1, 0:n],
            op=ALU.subtract), [("vector", v1)])
        v3 = E.op("vector", lambda e, n=ncols: e.tensor_scalar(
            out=rs[0:1, 0:n], in0=rs[0:1, 0:n], scalar1=EPS, scalar2=None,
            op0=ALU.add), [("vector", v2)])
        s1 = E.op("scalar", lambda e, n=ncols: e.activation(
            out=rt[0:1, 0:n], in_=rs[0:1, 0:n], func=AF.Ln), [("vector", v3)])
        return E.op("scalar", lambda e, n=ncols: e.activation(
            out=rs[0:1, 0:n], in_=rt[0:1, 0:n], func=AF.Exp, scale=-0.5),
            [("scalar", s1)])

    def bcast2(rows, a, w, waits):
        rm, rs = rows[0], rows[1]
        t1 = E.op("tensor", lambda e, a=a, w=w: e.matmul(
            out=ps_big[:, 0:w], lhsT=ones_row[0:1, :], rhs=rm[0:1, a:a + w],
            start=True, stop=True, skip_group_check=True), list(waits) + [psf["big"]])
        return E.op("tensor", lambda e, a=a, w=w: e.matmul(
            out=ps_big[:, 512:512 + w], lhsT=ones_row[0:1, :],
            rhs=rs[0:1, a:a + w],
            start=True, stop=True, skip_group_check=True), [("tensor", t1)])

    r1 = ExitStack(); sb1 = mk_sb(r1)
    xn = sb1("xn", [128, KT, XR])
    r1a = ExitStack(); sb1a = mk_sb(r1a)
    rows0 = tuple(sb1a(f"rows0_{i}", [1, XR]) for i in range(3))
    x_t = sb1a("x_t", [128, KT, XR])

    ld_x = dma("sync", x_t[:], rw(xf_in, XR))
    for p, pn in ((0, "a"), (1, "b")):
        for nm, key, cols in (("wconv", "wconv", 4), ("bconv", "bconv", 1),
                              ("bdt", "bdt", 1), ("dp", "dp", 1), ("alog", "A", NS)):
            dma("sync", small[p][key][:],
                pw[pn][nm].rearrange("p (k c) -> p k c", c=cols))
    dma("sync", bproj_sb[:], bproj.unsqueeze(2))
    ld_all = E.cnt["sync"]

    mz = E.op("vector", lambda e: e.memset(ones_col[:], 1.0))
    mz2 = E.op("vector", lambda e: e.memset(ones_row[:], 1.0), [("vector", mz)])
    aexp_last = 0
    for p in range(2):
        s1 = E.op("scalar", lambda e, p=p: e.activation(
            out=small[p]["A"][:], in_=small[p]["A"][:], func=AF.Exp),
            [("sync*", ld_all)])
        aexp_last = E.op("vector", lambda e, p=p: e.tensor_scalar(
            out=small[p]["A"][:], in0=small[p]["A"][:], scalar1=-1.0, scalar2=None,
            op0=ALU.mult), [("scalar", s1)])

    ln0_done = ln_stats(x_t, XR, rows0, [("sync", ld_x), ("vector", mz2)])
    xn_done = 0
    for (a, b) in [(i, min(i + 512, XR)) for i in range(0, XR, 512)]:
        w = b - a
        br = bcast2(rows0, a, w, [("scalar", ln0_done)])
        for k in range(KT):
            v1 = E.op("vector", lambda e, k=k, a=a, b=b, w=w: e.tensor_tensor(
                out=xn[:, k, a:b], in0=x_t[:, k, a:b], in1=ps_big[:, 0:w],
                op=ALU.subtract), [("tensor", br)])
            xn_done = E.op("vector", lambda e, k=k, a=a, b=b, w=w: e.tensor_tensor(
                out=xn[:, k, a:b], in0=xn[:, k, a:b], in1=ps_big[:, 512:512 + w],
                op=ALU.mult), [("vector", v1)])
        psf["big"] = ("vector", xn_done)
    dma("sync", d_rows[0:1, :], rows0[0][0:1, :], [("scalar", ln0_done)])
    dma("sync", d_rows[1:2, :], rows0[1][0:1, :], [("scalar", ln0_done)])
    if "xn" in taps:
        for k in range(KT):
            dma("sync", taps["xn"][k * 128:(k + 1) * 128, :], xn[:, k, :],
                [("vector", xn_done)])
    r1a.close()

    def moving_pieces(pipe, a, b, grid):
        out = []
        if pipe == 0:
            if grid:
                out.append((0, b - a, lambda k, a=a, b=b: xn[:, k, a:b]))
            else:
                out.append((0, b - a, lambda k, a=a, b=b: xn[:, k, 3 + a:3 + b]))
        else:
            if grid:
                if a < 3:
                    h = min(b, 3)
                    out.append((0, h - a, lambda k, a=a, h=h: xn[:, k, 2051 + a:2051 + h]))
                if b > 3:
                    p0 = max(a, 3)
                    out.append((p0 - a, b - p0,
                                lambda k, p0=p0, b=b: xn[:, k, 2053 - p0:2053 - b:-1]))
            else:
                out.append((0, b - a, lambda k, a=a, b=b: xn[:, k, 2050 - a:2050 - b:-1]))
        return out

    rS2 = ExitStack(); sbS2 = mk_sb(rS2)
    bcbc = sbS2("bcbc", [128, 8, 512])
    ccbc = sbS2("ccbc", [128, 8, 512])
    dl_c = sbS2("dl_c", [128, 512])
    u_c = sbS2("u_c", [128, 512])
    du_c = sbS2("du_c", [128, 512])
    y_c = sbS2("y_c", [128, 512])
    yo_c = sbS2("yo_c", [128, 512])
    dA_c = sbS2("dA_c", [128, 2, 512])
    dBu_c = sbS2("dBu_c", [128, 512])
    h_c = sbS2("h_c", [128, 512])
    r1L = ExitStack(); sb1L = mk_sb(r1L)
    wslab = sb1L("wslab", [128, KT, 128])
    u_raw = sb1L("u_raw", [128, GRID])
    u_s = sb1L("u_s", [128, TOK])
    zbuf = sb1L("zbuf", [128, 512])
    wx_sb = sb1L("wx_sb", [128, DIT, 96])
    wdt_sb = sb1L("wdt_sb", [DTR, DI])
    dt_sb = sb1L("dt_sb", [DTR, TOK])
    bc_sb = sb1L("bc_sb", [32, TOK])
    dlw = sb1L("dlw", [128, 512])
    dl_w2 = sb1L("dl_w2", [128, 2, 512])
    slab_used = [0, 0]
    zb_used = [0, 0]

    def stage1(pipe, pname, dbc_prev_evac, bw0):
        wname = pw[pname]
        ld_wx = dma("sync", wx_sb[:], rw(wname["wxT"], 96),
                    [("vector", xn_done)] + bw0)
        ld_wdt = dma("sync", wdt_sb[:], wname["wdtT"],
                     [("vector", xn_done)] + bw0)
        u_s_free = ("gpsimd", 0)
        u_raw_free = ("vector", 0)
        for m in range(32):
            ld = dma("gpsimd", wslab[:],
                     rw(wname["winT"], 2 * DI)[:, :, m * 128:(m + 1) * 128],
                     [("tensor", slab_used[0]), ("vector", xn_done)])
            is_u = m < 16
            dt_i = m if is_u else m - 16
            wins = WIN_GRID if is_u else WIN_OWN
            for wi, (a, b) in enumerate(wins):
                pslot = "A" if wi % 2 == 0 else "B"
                last_mm = 0
                first = True
                for (coff, width, mov) in moving_pieces(pipe, a, b, is_u):
                    for k in range(KT):
                        wl = []
                        if first:
                            wl = [("gpsimd", ld), psf[pslot], ("vector", xn_done)]
                            first = False
                        last_mm = E.op("tensor", lambda e, k=k, coff=coff,
                                       width=width, mov=mov, pslot=pslot,
                                       st=(k == 0), sp=(k == KT - 1): e.matmul(
                            out=ps[pslot][:, coff:coff + width],
                            lhsT=wslab[:, k, :], rhs=mov(k),
                            start=st, stop=sp, skip_group_check=True), wl)
                slab_used[0] = last_mm
                if is_u:
                    ev = E.op("scalar", lambda e, a=a, b=b, pslot=pslot: e.activation(
                        out=u_raw[:, a:b], in_=ps[pslot][:, 0:b - a], func=AF.Copy),
                        [("tensor", last_mm), u_raw_free])
                    psf[pslot] = ("scalar", ev)
                else:
                    ev = E.op("scalar", lambda e, b=b, a=a, pslot=pslot: e.activation(
                        out=zbuf[:, 0:b - a], in_=ps[pslot][:, 0:b - a],
                        func=AF.Silu),
                        [("tensor", last_mm), ("gpsimd", zb_used[0])])
                    psf[pslot] = ("scalar", ev)
                    zb_used[0] = dma("gpsimd", d_zs[pipe][dt_i, :, a:b],
                                     zbuf[:, 0:b - a], [("scalar", ev)])
            if is_u:
                sA = psf["A"][1] if psf["A"][0] == "scalar" else 0
                sB = psf["B"][1] if psf["B"][0] == "scalar" else 0
                c0 = E.op("vector", lambda e, dt_i=dt_i, p=pipe: e.tensor_scalar(
                    out=u_s[:], in0=u_raw[:, 0:TOK],
                    scalar1=small[p]["wconv"][:, dt_i, 0:1], scalar2=None,
                    op0=ALU.mult),
                    [("scalar", sA), ("scalar", sB), ("vector", aexp_last), u_s_free])
                for j in (1, 2, 3):
                    c0 = E.op("vector", lambda e, j=j, dt_i=dt_i, p=pipe: e.scalar_tensor_tensor(
                        out=u_s[:], in0=u_raw[:, j:j + TOK],
                        scalar=small[p]["wconv"][:, dt_i, j:j + 1], in1=u_s[:],
                        op0=ALU.mult, op1=ALU.add), [("vector", c0)])
                u_raw_free = ("vector", c0)
                cs = E.op("scalar", lambda e, dt_i=dt_i, p=pipe: e.activation(
                    out=u_s[:], in_=u_s[:], func=AF.Silu,
                    bias=small[p]["bconv"][:, dt_i, 0:1]), [("vector", c0)])
                for w2 in range(4):
                    wl = [("scalar", cs), ("sync", ld_wx)]
                    if dt_i == 0 and w2 == 0:
                        wl.append(dbc_prev_evac)
                    E.op("tensor", lambda e, w2=w2, dt_i=dt_i,
                         st=(dt_i == 0), sp=(dt_i == DIT - 1): e.matmul(
                        out=ps_big[0:96, w2 * 512:(w2 + 1) * 512],
                        lhsT=wx_sb[:, dt_i, :], rhs=u_s[:, w2 * 512:(w2 + 1) * 512],
                        start=st, stop=sp, skip_group_check=True),
                        wl if w2 == 0 else [])
                usp = dma("gpsimd", d_u[pipe][dt_i, :, :], u_s[:],
                          [("scalar", cs), ("tensor", E.cnt["tensor"])])
                u_s_free = ("gpsimd", usp)
            yield
        tm = E.cnt["tensor"]
        e1 = E.op("vector", lambda e: e.tensor_copy(out=dt_sb[:], in_=ps_big[0:DTR, :]),
                  [("tensor", tm)] + bw0)
        e2 = E.op("vector", lambda e: e.tensor_copy(out=bc_sb[:], in_=ps_big[DTR:96, :]),
                  [("vector", e1)] + bw0)
        dma("gpsimd", d_bc[pipe].rearrange("a n t -> (a n) t"), bc_sb[:],
            [("vector", e2)])
        psf["big"] = ("vector", e2)
        dl_free = [("gpsimd", 0), ("gpsimd", 0)]
        vl_prev = 0
        psc_free = psf["C"]
        for dt_i in range(DIT):
            for w2 in range(4):
                ws = w2 % 2
                mm = E.op("tensor", lambda e, dt_i=dt_i, w2=w2: e.matmul(
                    out=ps["C"][:, 0:512], lhsT=wdt_sb[:, dt_i * 128:(dt_i + 1) * 128],
                    rhs=dt_sb[:, w2 * 512:(w2 + 1) * 512], start=True, stop=True,
                    skip_group_check=True),
                    [("vector", e1), ("sync", ld_wdt), psc_free])
                s1 = E.op("scalar", lambda e, dt_i=dt_i, p=pipe: e.activation(
                    out=dlw[:], in_=ps["C"][:, 0:512], func=AF.Abs,
                    bias=small[p]["bdt"][:, dt_i, 0:1]),
                    [("tensor", mm), ("vector", vl_prev)])
                s2 = E.op("scalar", lambda e: e.activation(
                    out=dlw[:], in_=dlw[:], func=AF.Exp, scale=-1.0), [("scalar", s1)])
                s3 = E.op("scalar", lambda e: e.activation(
                    out=dlw[:], in_=dlw[:], func=AF.Ln, bias=1.0), [("scalar", s2)])
                s4 = E.op("scalar", lambda e, dt_i=dt_i, ws=ws, p=pipe: e.activation(
                    out=dl_w2[:, ws, :], in_=ps["C"][:, 0:512],
                    func=AF.Relu, bias=small[p]["bdt"][:, dt_i, 0:1]),
                    [("scalar", s3), dl_free[ws]])
                psc_free = ("scalar", s4)
                vlast = E.op("vector", lambda e, ws=ws: e.tensor_tensor(
                    out=dl_w2[:, ws, :], in0=dl_w2[:, ws, :], in1=dlw[:],
                    op=ALU.add), [("scalar", s4)])
                vl_prev = vlast
                dl_free[ws] = ("gpsimd", dma(
                    "gpsimd", d_dl[pipe][dt_i, :, w2 * 512:(w2 + 1) * 512],
                    dl_w2[:, ws, :], [("vector", vlast)])[1] if False else dma(
                    "gpsimd", d_dl[pipe][dt_i, :, w2 * 512:(w2 + 1) * 512],
                    dl_w2[:, ws, :], [("vector", vlast)]))
            yield
        psf["C"] = psc_free
        st1_ret[pipe] = ("vector", e2)

    st1_ret = {}
    s2c_ret = {}

    def drive(gens):
        gens = list(gens)
        while gens:
            for g in list(gens):
                try:
                    next(g)
                except StopIteration:
                    gens.remove(g)

    def drive_until(primary, bg, ratio=1):
        while True:
            for _ in range(ratio):
                try:
                    next(primary)
                except StopIteration:
                    return
            try:
                next(bg)
            except StopIteration:
                pass

    for _ in stage1(0, "a", ("vector", 0), []):
        pass
    dbc_ev = st1_ret[0]
    snapA1 = E.snap()
    for nm, srcd in (("u0", d_u[0]), ("dl0", d_dl[0]), ("zs0", d_zs[0])):
        if nm in taps:
            for dt in range(DIT):
                dma("sync", taps[nm][dt], srcd[dt], E.bw(snapA1))
    if "bc0" in taps:
        dma("sync", taps["bc0"], d_bc[0], E.bw(snapA1))
    def pipe1_taps():
        snap = E.snap()
        for nm, srcd in (("u1t", d_u[1]), ("dl1t", d_dl[1]), ("bc1t", d_bc[1])):
            if nm in taps:
                dma("sync", taps[nm], srcd, E.bw(snap))

    g_b1 = stage1(1, "b", dbc_ev, E.bw(snapA1))
    snapB1 = E.bw

    binit_done = 0

    def stage2c(pipe, bsnap):
        bw = E.bw(bsnap)
        last_y = 0
        last_sp = 0
        yo_cons = 0
        bc_cons = 0
        for half in range(2):
            for ch in range(4):
                c0, c1 = ch * 512, (ch + 1) * 512
                bb = dma("sync", bcbc[:],
                         d_bc[pipe][0, half * 8:(half + 1) * 8, c0:c1]
                         .unsqueeze(0).broadcast_to([128, 8, 512]),
                         bw + [("vector", bc_cons)])
                cb = dma("sync", ccbc[:],
                         d_bc[pipe][1, half * 8:(half + 1) * 8, c0:c1]
                         .unsqueeze(0).broadcast_to([128, 8, 512]),
                         bw + [("vector", bc_cons)])
                for dt_i in range(DIT):
                    fd = dma("sync", dl_c[:], d_dl[pipe][dt_i, :, c0:c1],
                             bw + [("vector", last_y)])
                    fu = dma("sync", u_c[:], d_u[pipe][dt_i, :, c0:c1],
                             bw + [("vector", last_y)])
                    vdu = E.op("vector", lambda e: e.tensor_tensor(
                        out=du_c[:], in0=dl_c[:], in1=u_c[:], op=ALU.mult),
                        [("sync", fd), ("sync", fu)])
                    if half == 0:
                        vy = E.op("vector", lambda e, dt_i=dt_i, p=pipe: e.tensor_scalar(
                            out=y_c[:], in0=u_c[:],
                            scalar1=small[p]["dp"][:, dt_i, 0:1],
                            scalar2=None, op0=ALU.mult),
                            [("vector", vdu), ("sync", last_sp)])
                        fy = 0
                    else:
                        fy = dma("sync", yo_c[:], d_y[pipe][dt_i, :, c0:c1],
                                 bw + [("vector", yo_cons)])
                        vy = E.op("vector", lambda e: e.memset(y_c[:], 0.0),
                                  [("vector", vdu), ("sync", last_sp)])
                    sprev = vy
                    vs_hist = [vy, vy]
                    for n in range(8):
                        ns = half * 8 + n
                        asl = n % 2
                        sA = E.op("scalar", lambda e, dt_i=dt_i, ns=ns, asl=asl,
                                  p=pipe: e.activation(
                            out=dA_c[:, asl, :], in_=dl_c[:], func=AF.Exp,
                            scale=small[p]["A"][:, dt_i, ns:ns + 1]),
                            [("sync", fd), ("vector", vs_hist[asl])])
                        vB = E.op("vector", lambda e, n=n: e.tensor_tensor(
                            out=dBu_c[:], in0=du_c[:], in1=bcbc[:, n, :],
                            op=ALU.mult), [("vector", vy), ("sync", bb), ("sync", cb)])
                        if ch == 0:
                            if pipe == 0:
                                vS = E.op("vector", lambda e, asl=asl: e.tensor_tensor_scan(
                                    out=h_c[:], data0=dA_c[:, asl, :], data1=dBu_c[:],
                                    initial=0.0, op0=ALU.mult, op1=ALU.add),
                                    [("scalar", sA), ("vector", vB)])
                            else:
                                vS = E.op("vector", lambda e, dt_i=dt_i, ns=ns, asl=asl: e.tensor_tensor_scan(
                                    out=h_c[:], data0=dA_c[:, asl, :], data1=dBu_c[:],
                                    initial=binit[:, dt_i, ns:ns + 1],
                                    op0=ALU.mult, op1=ALU.add),
                                    [("scalar", sA), ("vector", vB),
                                     ("vector", binit_done)])
                        else:
                            vS = E.op("vector", lambda e, dt_i=dt_i, ns=ns, asl=asl: e.tensor_tensor_scan(
                                out=h_c[:], data0=dA_c[:, asl, :], data1=dBu_c[:],
                                initial=states[:, dt_i, ns:ns + 1],
                                op0=ALU.mult, op1=ALU.add),
                                [("scalar", sA), ("vector", vB)])
                        vs_hist[asl] = vS
                        vT = E.op("vector", lambda e, n=n: e.tensor_tensor(
                            out=dBu_c[:], in0=h_c[:], in1=ccbc[:, n, :],
                            op=ALU.mult), [("vector", vS)])
                        vYa = E.op("vector", lambda e: e.tensor_tensor(
                            out=y_c[:], in0=y_c[:], in1=dBu_c[:], op=ALU.add),
                            [("vector", vT)])
                        if ch < 3 or pipe == 0:
                            vYa = E.op("vector", lambda e, dt_i=dt_i, ns=ns: e.tensor_copy(
                                out=states[:, dt_i, ns:ns + 1], in_=h_c[:, 511:512]),
                                [("vector", vYa)])
                        sprev = vYa
                    if half == 1:
                        sprev = E.op("vector", lambda e: e.tensor_tensor(
                            out=y_c[:], in0=y_c[:], in1=yo_c[:], op=ALU.add),
                            [("vector", sprev), ("sync", fy)])
                        yo_cons = sprev
                    last_y = sprev
                    last_sp = dma("sync", d_y[pipe][dt_i, :, c0:c1], y_c[:],
                                  [("vector", sprev)])
                    if dt_i % 4 == 3:
                        yield
                bc_cons = last_y
        s2c_ret[pipe] = last_sp

    g_a2 = stage2c(0, snapA1)

    rF = ExitStack(); sbF = mk_sb(rF)
    h1f = sbF("h1f", [128, 8, 256])
    w1f_sl = sbF("w1f_sl", [128, KT, 256])
    w2f_sl = sbF("w2f_sl", [128, 8, 128])
    gevf = sbF("gevf", [128, 256])
    b1_sb = sbF("b1_sb", [128, 32, 1])
    b2_sb = sbF("b2_sb", [128, KT, 1])
    bwF = E.bw(snapA1)
    dma("gpsimd", b1_sb[:], b1.unsqueeze(2), bwF)
    dma("gpsimd", b2_sb[:], b2.unsqueeze(2), bwF)
    bloadF = E.cnt["gpsimd"]

    def ffn_gen():
      gev_free = ("gpsimd", 0)
      h1_free = ("tensor", 0)
      w1u = [0, 0]
      w2u = 0
      sg_last = 0
      for wi8 in range(8):
        a = wi8 * 256
        mz0 = E.op("vector", lambda e: e.memset(ps_big[:], 0.0),
                   [psf["big"], ("scalar", sg_last)])
        psf["big"] = ("vector", mz0)
        for ih in range(4):
          for it2 in range(8):
              it = ih * 8 + it2
              half2 = it2 % 2
              if half2 == 0:
                  ld1 = dma("gpsimd", w1f_sl[:],
                            rw(w1, 4 * D)[:, :, it * 128:(it + 2) * 128]
                            .rearrange("p k (i m) -> p k i m", m=128)
                            .rearrange("p k i m -> p k (i m)") if False else
                            rw(w1, 4 * D)[:, :, it * 128:(it + 2) * 128],
                            bwF + [("tensor", w1u[0])])
              mm = 0
              for k in range(KT):
                  wl = ([("gpsimd", ld1), ("vector", xn_done), psf["A"]]
                        if k == 0 else [])
                  mm = E.op("tensor", lambda e, k=k, a=a, half2=half2, st=(k == 0),
                            sp=(k == KT - 1): e.matmul(
                      out=ps["A"][:, 0:256],
                      lhsT=w1f_sl[:, k, half2 * 128:(half2 + 1) * 128],
                      rhs=xn[:, k, 3 + a:3 + a + 256], start=st, stop=sp,
                      skip_group_check=True), wl)
              w1u[0] = mm
              sev = E.op("scalar", lambda e, it2=it2, it=it: e.activation(
                  out=h1f[:, it2, :], in_=ps["A"][:, 0:256], func=AF.Relu,
                  bias=b1_sb[:, it, 0:1]),
                  [("tensor", mm), ("gpsimd", bloadF), h1_free])
              psf["A"] = ("scalar", sev)
              if it2 % 8 == 7:
                  yield
          sm_h1 = E.cnt["scalar"]
          for ot in range(KT):
              ld2 = dma("gpsimd", w2f_sl[:],
                        w2[ih * 1024:(ih + 1) * 1024, ot * 128:(ot + 1) * 128]
                        .rearrange("(k q) m -> q k m", q=128),
                        bwF + [("tensor", w2u)])
              mm2 = 0
              for it2 in range(8):
                  wl = ([("gpsimd", ld2), ("scalar", sm_h1),
                         ("vector", mz0)] if it2 == 0 else [])
                  mm2 = E.op("tensor", lambda e, it2=it2, ih=ih, ot=ot,
                             sp=(ih == 3 and it2 == 7): e.matmul(
                      out=ps_big[:, ot * 256:(ot + 1) * 256],
                      lhsT=w2f_sl[:, it2, :], rhs=h1f[:, it2, :],
                      start=False, stop=sp, skip_group_check=True), wl)
              w2u = mm2
              if ot % 4 == 3:
                  yield
          h1_free = ("tensor", w2u)
        for ot in range(KT):
            sg = E.op("scalar", lambda e, ot=ot: e.activation(
                out=gevf[:], in_=ps_big[:, ot * 256:(ot + 1) * 256], func=AF.Silu,
                bias=b2_sb[:, ot, 0:1]), [("tensor", w2u), gev_free])
            sg_last = sg
            gsp = dma("gpsimd", d_g0[ot * 128:(ot + 1) * 128, a:a + 256],
                      gevf[:], [("scalar", sg)])
            gev_free = ("gpsimd", gsp)
        yield

    drive_until(g_b1, g_a2, ratio=2)
    pipe1_taps()
    drive_until(ffn_gen(), g_a2)
    for _ in g_a2:
        pass
    yA_sp = s2c_ret[0]
    stA_v = E.cnt["vector"]
    if "y0" in taps:
        snap = E.snap()
        for dt in range(DIT):
            dma("sync", taps["y0"][dt], d_y[0][dt], E.bw(snap))
    if "g0" in taps:
        snap = E.snap()
        dma("sync", taps["g0"], d_g0, E.bw(snap))

    st_dma = dma("sync", cc_st_in.ap().rearrange("(d p) n -> p d n", p=128),
                 states[:], [("vector", stA_v), ("sync", yA_sp)])
    ccm = E.cc(lambda e: e.collective_compute(
        "AllReduce", ALU.add, replica_groups=RG,
        ins=[cc_st_in.ap().opt()], outs=[cc_st_out.ap().opt()]), [("sync", st_dma)])
    st_back = dma("sync", binit[:], cc_st_out.ap().rearrange("(d p) n -> p d n", p=128),
                  [("cc", ccm)])
    binit_done = E.op("vector", lambda e: e.tensor_tensor(
        out=binit[:], in0=binit[:], in1=states[:], op=ALU.subtract),
        [("sync", st_back)])
    if "st" in taps:
        dma("sync", taps["st"].rearrange("(d p) n -> p d n", p=128), states[:],
            [("vector", binit_done)])
    if "bi" in taps:
        dma("sync", taps["bi"].rearrange("(d p) n -> p d n", p=128), binit[:],
            [("vector", binit_done)])

    snapF = E.snap()
    rF.close()
    r1L.close()

    rA3 = ExitStack(); sbA3 = mk_sb(rA3)
    yg = sbA3("yg", [128, DIT, 512])
    g_f = sbA3("g_f", [128, 2, 2, 512])
    wout_sl = sbA3("wout_sl", [128, DIT, 128])
    wproj_sl = sbA3("wproj_sl", [128, 2, KT, 128])
    sdir = sbA3("sdir", [128, KT, 512])
    s_w = sbA3("s_w", [128, 2, 512])
    wsl_used = [0, 0]
    wpj_used = [0, 0]

    def stage3A(bsnap):
        bw = E.bw(bsnap)
        sdir_free = ("tensor", 0)
        yg_free = ("tensor", 0)
        sw_used = [0, 0]
        gm = []
        for wi, (a, b) in enumerate(WIN_OWN):
            for dt_i in range(DIT):
                slot = dt_i % 2
                anti = [("vector", gm[-2] if len(gm) >= 2 else 0)]
                f1 = dma("sync", g_f[:, slot, 0, :], d_y[0][dt_i, :, a:b],
                         bw + anti)
                f3 = dma("sync", g_f[:, slot, 1, :], d_zs[0][dt_i, :, a:b],
                         bw + anti)
                if wi == 0 and dt_i == 0 and "gfA" in taps:
                    dma("sync", taps["gfA"][0], g_f[:, 0, 0, :],
                        [("sync", f1), ("sync", f3)])
                    dma("sync", taps["gfA"][1], g_f[:, 0, 1, :],
                        [("sync", f1), ("sync", f3)])
                gm.append(E.op("vector", lambda e, slot=slot, dt_i=dt_i: e.tensor_tensor(
                    out=yg[:, dt_i, :], in0=g_f[:, slot, 0, :],
                    in1=g_f[:, slot, 1, :], op=ALU.mult),
                    [("sync", f1), ("sync", f3), yg_free]))
                if wi == 0 and dt_i == 0 and "yg2" in taps:
                    dma("sync", taps["yg2"][0], yg[:, 0, :],
                        [("vector", gm[-1])])
            if wi == 0 and "ygA" in taps:
                snapyg = E.snap()
                for dt in range(DIT):
                    dma("sync", taps["ygA"][dt], yg[:, dt, :], E.bw(snapyg))
            if wi == 0 and "yg2" in taps:
                dma("sync", taps["yg2"][1], yg[:, 0, :], E.bw(E.snap()))
            ev_last = 0
            for dm in range(KT):
                pslot = "A" if dm % 2 == 0 else "B"
                ldw = dma("gpsimd", wout_sl[:],
                          rw(pw["a"]["woutT"], D)[:, :, dm * 128:(dm + 1) * 128],
                          bw + [("tensor", wsl_used[0])])
                mm = 0
                for dt_i in range(DIT):
                    wl = ([("gpsimd", ldw), ("vector", gm[-1]), psf[pslot],
                           sdir_free] if dt_i == 0 else [])
                    mm = E.op("tensor", lambda e, dt_i=dt_i, pslot=pslot,
                              st=(dt_i == 0), sp=(dt_i == DIT - 1): e.matmul(
                        out=ps[pslot][:, 0:512], lhsT=wout_sl[:, dt_i, :],
                        rhs=yg[:, dt_i, :], start=st, stop=sp,
                        skip_group_check=True), wl)
                wsl_used[0] = mm
                ev_last = E.op("scalar", lambda e, dm=dm, pslot=pslot: e.activation(
                    out=sdir[:, dm, :], in_=ps[pslot][:, 0:512], func=AF.Copy),
                    [("tensor", mm), sdir_free])
                psf[pslot] = ("scalar", ev_last)
                if dm % 2 == 1:
                    yield
            yg_free = ("tensor", wsl_used[0])
            for dmp in range(KT):
                slot = dmp % 2
                ldp = dma("gpsimd", wproj_sl[:, slot, :, :],
                          rw(pw["a"]["wprojT"], D)[:, :, dmp * 128:(dmp + 1) * 128],
                          bw + [("tensor", wpj_used[slot])])
                mm = 0
                for dm in range(KT):
                    wl = ([("scalar", ev_last), ("gpsimd", ldp), psf["C"]]
                          if dm == 0 else [])
                    mm = E.op("tensor", lambda e, dmp=dmp, dm=dm, slot=slot,
                              st=(dm == 0), sp=(dm == KT - 1): e.matmul(
                        out=ps["C"][:, 0:512], lhsT=wproj_sl[:, slot, dm, :],
                        rhs=sdir[:, dm, :], start=st, stop=sp,
                        skip_group_check=True), wl)
                wpj_used[slot] = mm
                sdir_free = ("tensor", mm)
                sws = dmp % 2
                evc = E.op("scalar", lambda e, sws=sws, dmp=dmp: e.activation(
                    out=s_w[:, sws, :], in_=ps["C"][:, 0:512],
                    func=AF.Identity, bias=bproj_sb[:, dmp, 0:1]),
                    [("tensor", mm), ("gpsimd", sw_used[sws])])
                psf["C"] = ("scalar", evc)
                sw_used[sws] = dma("gpsimd",
                                   d_s[dmp * 128:(dmp + 1) * 128, a:b],
                                   s_w[:, sws, :], [("scalar", evc)])
                if dmp % 2 == 1:
                    yield

    stage2c_b_snap = E.snap()
    drive([stage3A(snapF), stage2c(1, stage2c_b_snap)])
    if "y1" in taps:
        snap = E.snap()
        for dt in range(DIT):
            dma("sync", taps["y1"][dt], d_y[1][dt], E.bw(snap))
    snapS3 = E.snap()
    if "sA" in taps:
        dma("sync", taps["sA"], d_s, E.bw(snapS3))
    rA3.close()
    rS2.close()
    r1.close()

    r3 = ExitStack(); sb3 = mk_sb(r3)
    s_acc = sb3("s_acc", [128, KT, TOK])
    r3a = ExitStack(); sb3a = mk_sb(r3a)
    ygB = sb3a("ygB", [128, DIT, 512])
    g_fB = sb3a("g_fB", [128, 2, 2, 512])
    wout_slB = sb3a("wout_slB", [128, 2, DIT, 128])
    wproj_sbB = sb3a("wproj_sbB", [128, KT, KT, 128])
    sdirB = sb3a("sdirB", [128, KT, 512])
    wslB_used = [0, 0]
    bwB3 = E.bw(snapS3)
    sacc_ld = 0
    for k in range(KT):
        sacc_ld = dma("sync", s_acc[:, k, :], d_s[k * 128:(k + 1) * 128, :], bwB3)

    def stage3B(bsnap):
        bw = E.bw(bsnap) + [("sync", sacc_ld)]
        ldp = dma("sync", wproj_sbB[:],
                  rw(pw["b"]["wprojT"], D).rearrange("p k (m j) -> p k m j", j=128),
                  bw)
        sdirB_free = ("tensor", 0)
        ygB_free = ("tensor", 0)
        gm = []
        for wi, (a, b) in enumerate(WIN_OWN):
            for dt_i in range(DIT):
                slot = dt_i % 2
                anti = [("vector", gm[-2] if len(gm) >= 2 else 0)]
                f1 = dma("sync", g_fB[:, slot, 0, :], d_y[1][dt_i, :, a:b], bw + anti)
                f3 = dma("gpsimd", g_fB[:, slot, 1, :], d_zs[1][dt_i, :, a:b],
                         bw + anti)
                gm.append(E.op("vector", lambda e, slot=slot, dt_i=dt_i: e.tensor_tensor(
                    out=ygB[:, dt_i, :], in0=g_fB[:, slot, 0, :],
                    in1=g_fB[:, slot, 1, :], op=ALU.mult),
                    [("sync", f1), ("gpsimd", f3), ygB_free]))
            ev_last = 0
            for dm in range(KT):
                slot = dm % 2
                pslot = "A" if dm % 2 == 0 else "B"
                ldw = dma("gpsimd", wout_slB[:, slot, :, :],
                          rw(pw["b"]["woutT"], D)[:, :, dm * 128:(dm + 1) * 128],
                          bw + [("tensor", wslB_used[slot])])
                mm = 0
                for dt_i in range(DIT):
                    wl = ([("gpsimd", ldw), ("vector", gm[-1]), psf[pslot],
                           sdirB_free] if dt_i == 0 else [])
                    mm = E.op("tensor", lambda e, slot=slot, dt_i=dt_i, pslot=pslot,
                              st=(dt_i == 0), sp=(dt_i == DIT - 1): e.matmul(
                        out=ps[pslot][:, 0:512], lhsT=wout_slB[:, slot, dt_i, :],
                        rhs=ygB[:, dt_i, :], start=st, stop=sp,
                        skip_group_check=True), wl)
                wslB_used[slot] = mm
                ev_last = E.op("scalar", lambda e, dm=dm, pslot=pslot: e.activation(
                    out=sdirB[:, dm, :], in_=ps[pslot][:, 0:512], func=AF.Copy),
                    [("tensor", mm), sdirB_free])
                psf[pslot] = ("scalar", ev_last)
            ygB_free = ("tensor", max(wsl_used))
            for dmp in range(KT):
                mm = 0
                for dm in range(KT):
                    wl = ([("scalar", ev_last), ("sync", ldp), psf["C"]]
                          if dm == 0 else [])
                    mm = E.op("tensor", lambda e, dmp=dmp, dm=dm, st=(dm == 0),
                              sp=(dm == KT - 1): e.matmul(
                        out=ps["C"][:, 0:512], lhsT=wproj_sbB[:, dm, dmp, :],
                        rhs=sdirB[:, dm, :], start=st, stop=sp,
                        skip_group_check=True), wl)
                sdirB_free = ("tensor", mm)
                evc = E.op("vector", lambda e, dmp=dmp, a=a, b=b: e.tensor_tensor(
                    out=s_acc[:, dmp, 2048 - b:2048 - a],
                    in0=ps["C"][:, 511::-1],
                    in1=s_acc[:, dmp, 2048 - b:2048 - a], op=ALU.add),
                    [("tensor", mm)])
                psf["C"] = ("vector", evc)

    stage3B(snapS3)
    if "s_acc" in taps:
        snap = E.snap()
        for k in range(KT):
            dma("sync", taps["s_acc"][k * 128:(k + 1) * 128, :], s_acc[:, k, :],
                E.bw(snap))
    r3a.close()

    s3_done = E.snap()
    bw4 = E.bw(s3_done)
    r3b = ExitStack(); sb3b = mk_sb(r3b)
    xnw = sb3b("xnw", [128, KT, 512])
    g0w = sb3b("g0w", [128, KT, 512])
    xres = sb3b("xres", [128, 512])
    rows4 = tuple(sb3b(f"rows4_{i}", [1, XR]) for i in range(2))
    rows1 = (rows4[0], rows4[1], sb3b("rows1_t", [1, TOK]))
    dma("sync", rows4[0][0:1, :], d_rows[0:1, :], bw4)
    dma("sync", rows4[1][0:1, :], d_rows[1:2, :], bw4)
    bload = E.cnt["sync"]
    x1_last = 0
    for wi, (a, b) in enumerate(WIN_OWN):
        fg = dma("sync", g0w[:], rw(d_g0, TOK)[:, :, a:b],
                 bw4 + [("sync*", bload), ("vector", x1_last)])
        for ot in range(KT):
            v1 = E.op("vector", lambda e, ot=ot, a=a: e.tensor_tensor(
                out=g0w[:, ot, :], in0=g0w[:, ot, :], in1=s_acc[:, ot, a:a + 512],
                op=ALU.mult), [("sync", fg), ("vector", x1_last)])
            fxr = dma("sync", xres[:],
                      rw(xf_in, XR)[:, ot, 3 + a:3 + b].unsqueeze(1),
                      [("vector", x1_last)])
            v2 = E.op("vector", lambda e, ot=ot, a=a: e.tensor_tensor(
                out=s_acc[:, ot, a:a + 512], in0=xres[:].unsqueeze(1),
                in1=g0w[:, ot, :].unsqueeze(1), op=ALU.add),
                [("vector", v1), ("sync", fxr)])
            x1_last = v2
            dma("gpsimd", d_x1[ot * 128:(ot + 1) * 128, a:a + 512],
                s_acc[:, ot, a:a + 512], [("vector", v2)])
    x1 = s_acc
    if "x1" in taps:
        snap = E.snap()
        for k in range(KT):
            dma("sync", taps["x1"][k * 128:(k + 1) * 128, :], x1[:, k, :], E.bw(snap))

    p4_done = E.snap()
    ln1_done = ln_stats(x1, TOK, rows1, E.bw(p4_done))
    xn1_done = 0
    for (a, b) in WIN_OWN:
        br = bcast2(rows1, a, 512, [("scalar", ln1_done)])
        for k in range(KT):
            v1 = E.op("vector", lambda e, k=k, a=a, b=b: e.tensor_tensor(
                out=x1[:, k, a:b], in0=x1[:, k, a:b], in1=ps_big[:, 0:512],
                op=ALU.subtract), [("tensor", br)])
            xn1_done = E.op("vector", lambda e, k=k, a=a, b=b: e.tensor_tensor(
                out=x1[:, k, a:b], in0=x1[:, k, a:b], in1=ps_big[:, 512:1024],
                op=ALU.mult), [("vector", v1)])
        psf["big"] = ("vector", xn1_done)
    for k in range(KT):
        dma("sync", d_xn1[k * 128:(k + 1) * 128, :], x1[:, k, :],
            [("vector", xn1_done)])
    if "xn1" in taps:
        snap = E.snap()
        dma("sync", taps["xn1"], d_xn1, E.bw(snap))
    r3b.close()
    r3.close()

    p4x_done = E.snap()
    bw5 = E.bw(p4x_done)
    r4 = ExitStack(); sb4 = mk_sb(r4)
    xn1_sb = sb4("xn1_sb", [128, KT, TOK])
    whold = sb4("whold", [128, KT, D])
    kv_t = sb4("kv_t", [128, 512])
    bq_sb = sb4("bq_sb", [128, KT, 1])
    bk_sb = sb4("bk_sb", [128, KT, 1])
    bv_b = sb4("bv_b", [128, D])
    xn1_ld = dma("sync", xn1_sb[:], rw(d_xn1, TOK), bw5)
    dma("sync", bq_sb[:], bq.unsqueeze(2), bw5)
    dma("sync", bk_sb[:], bk.unsqueeze(2), bw5)
    bvb = dma("sync", bv_b[:], bv_row.broadcast_to([128, D]), bw5)
    bld = E.cnt["sync"]
    vq = E.op("vector", lambda e: e.tensor_scalar(
        out=bq_sb[:], in0=bq_sb[:], scalar1=1.0 / 32.0, scalar2=None, op0=ALU.mult),
        [("sync*", bld)])
    whold_free = ("tensor", 0)
    kvt_free = ("gpsimd", 0)

    def proj_fm(wdram, dest, bias_sb, scale):
        nonlocal whold_free, kvt_free
        ld = dma("sync", whold[:], rw(wdram, D), bw5 + [whold_free])
        last_sp = 0
        for m in range(KT):
            for (a, b) in WIN_OWN:
                pslot = "A" if (m + a // 512) % 2 == 0 else "B"
                mm = 0
                for k in range(KT):
                    wl = ([("sync", ld), ("sync", xn1_ld), psf[pslot]]
                          if k == 0 else [])
                    mm = E.op("tensor", lambda e, m=m, k=k, a=a, b=b, pslot=pslot,
                              st=(k == 0), sp=(k == KT - 1): e.matmul(
                        out=ps[pslot][:, 0:512],
                        lhsT=whold[:, k, m * 128:(m + 1) * 128],
                        rhs=xn1_sb[:, k, a:b], start=st, stop=sp,
                        skip_group_check=True), wl)
                ev = E.op("scalar", lambda e, m=m, scale=scale, pslot=pslot: e.activation(
                    out=kv_t[:], in_=ps[pslot][:, 0:512], func=AF.Identity,
                    bias=bias_sb[:, m, 0:1], scale=scale),
                    [("tensor", mm), kvt_free, ("vector", vq)])
                psf[pslot] = ("scalar", ev)
                last_sp = dma("gpsimd", dest[m * 128:(m + 1) * 128, a:b], kv_t[:],
                              [("scalar", ev)])
                kvt_free = ("gpsimd", last_sp)
        whold_free = ("tensor", E.cnt["tensor"])
        return last_sp

    proj_fm(wqT, d_q, bq_sb, 1.0 / 32.0)
    kd = proj_fm(wkT, cc_k_in.ap(), bk_sb, 1.0)
    ldv = dma("sync", whold[:], rw(wvT, D), bw5 + [whold_free])
    vsp = 0
    for tt in range(16):
        for dvw in range(2):
            pslot = "C" if (tt + dvw) % 2 == 0 else "D"
            mm = 0
            for k in range(KT):
                wl = ([("sync", ldv), ("sync", xn1_ld), psf[pslot]]
                      if k == 0 else [])
                mm = E.op("tensor", lambda e, tt=tt, dvw=dvw, k=k, pslot=pslot,
                          st=(k == 0), sp=(k == KT - 1): e.matmul(
                    out=ps[pslot][:, 0:512],
                    lhsT=xn1_sb[:, k, tt * 128:(tt + 1) * 128],
                    rhs=whold[:, k, dvw * 512:(dvw + 1) * 512],
                    start=st, stop=sp, skip_group_check=True), wl)
            ev = E.op("vector", lambda e, dvw=dvw, pslot=pslot: e.tensor_tensor(
                out=kv_t[:], in0=ps[pslot][:, 0:512],
                in1=bv_b[:, dvw * 512:(dvw + 1) * 512], op=ALU.add),
                [("tensor", mm), ("sync", bvb), kvt_free])
            psf[pslot] = ("vector", ev)
            vsp = dma("gpsimd", cc_vin.ap()[tt * 128:(tt + 1) * 128,
                                             dvw * 512:(dvw + 1) * 512],
                      kv_t[:], [("vector", ev)])
            kvt_free = ("gpsimd", vsp)
    whold_free = ("tensor", E.cnt["tensor"])
    cc_k = E.cc(lambda e: e.collective_compute(
        "AllGather", ALU.bypass, replica_groups=RG,
        ins=[cc_k_in.ap().opt()], outs=[cc_k_out.ap().opt()]),
        [("gpsimd", vsp), ("gpsimd", kd)])
    cc_v = E.cc(lambda e: e.collective_compute(
        "AllGather", ALU.bypass, replica_groups=RG,
        ins=[cc_vin.ap().opt()], outs=[cc_vout.ap().opt()]), [("cc", cc_k)])
    if "q" in taps:
        snap = E.snap()
        dma("sync", taps["q"], d_q, E.bw(snap))
    r4.close()

    p5_done = E.snap()
    bw6 = [(s, v) for (s, v) in E.bw(p5_done) if s != "cc"]
    r5 = ExitStack(); sb5 = mk_sb(r5)
    hq = sb5("hq", [128, KT, TOK])
    tq = sb5("tq", [128, KT, TOK])
    rowsq_mr = tuple(sb5(f"rowsq_{i}", [1, TOK]) for i in range(2))
    wdeq = sb5("wdeq", [128, KT, D])
    qraw = sb5("qraw", [128, 512], I32)
    sraw = sb5("sraw", [128, 512])
    ar_sb = sb5("ar_sb", [32, TOK])
    la_sb = sb5("la_sb", [128, KT, 32])
    lb_sb = sb5("lb_sb", [32, D])
    qb_sb = sb5("qb_sb", [128, KT, 1])
    lgt_g = sb5("lgt_g", [128, KT, 1])
    lgt_b = sb5("lgt_b", [128, KT, 1])
    h_ready = dma("sync", hq[:], rw(d_xn1, TOK), bw6)
    wdeq_free = ("tensor", 0)

    def qlora_linear(idx, src, dst, add_residual, waits0):
        nonlocal wdeq_free
        dq_last = 0
        for dt_i in range(KT):
            for jh in range(2):
                anti = [("vector", dq_last), wdeq_free]
                l1 = dma("sync", qraw[:],
                         qTw[idx, dt_i * 128:(dt_i + 1) * 128,
                             jh * 512:(jh + 1) * 512], list(waits0) + anti)
                l2 = dma("sync", sraw[:],
                         scrT[idx, dt_i * 128:(dt_i + 1) * 128,
                              jh * 512:(jh + 1) * 512], list(waits0))
                s1 = E.op("scalar", lambda e, dt_i=dt_i, jh=jh: e.activation(
                    out=wdeq[:, dt_i, jh * 512:(jh + 1) * 512], in_=qraw[:],
                    func=AF.Copy, scale=1.0 / 7.5, bias=-1.0), [("sync", l2)])
                dq_last = E.op("vector", lambda e, dt_i=dt_i, jh=jh: e.tensor_tensor(
                    out=wdeq[:, dt_i, jh * 512:(jh + 1) * 512],
                    in0=wdeq[:, dt_i, jh * 512:(jh + 1) * 512], in1=sraw[:],
                    op=ALU.mult), [("scalar", s1)])
        lb_ld = dma("sync", lb_sb[:], lbT[idx], list(waits0))
        la_ld = dma("sync", la_sb[:], laT[idx].rearrange("(k p) r -> p k r", p=128),
                    list(waits0))
        qb_ld = dma("sync", qb_sb[:], qb[idx].unsqueeze(2), list(waits0))
        ar_done = 0
        for (a, b) in WIN_OWN:
            mm = 0
            for k in range(KT):
                wl = ([("sync", la_ld), psf["D"]] + list(waits0)) if k == 0 else []
                mm = E.op("tensor", lambda e, k=k, a=a, b=b, st=(k == 0),
                          sp=(k == KT - 1): e.matmul(
                    out=ps["D"][0:32, 0:512], lhsT=la_sb[:, k, :], rhs=src[:, k, a:b],
                    start=st, stop=sp, skip_group_check=True), wl)
            ar_done = E.op("vector", lambda e, a=a: e.tensor_copy(
                out=ar_sb[:, a:a + 512], in_=ps["D"][0:32, 0:512]), [("tensor", mm)])
            psf["D"] = ("vector", ar_done)
        out_last = 0
        for m in range(KT):
            for (a, b) in WIN_OWN:
                pslot = "A" if (m + a // 512) % 2 == 0 else "B"
                mm = 0
                for k in range(KT):
                    wl = ([("vector", dq_last), psf[pslot]] + list(waits0)
                          if k == 0 else [])
                    mm = E.op("tensor", lambda e, m=m, k=k, a=a, b=b, pslot=pslot,
                              st=(k == 0): e.matmul(
                        out=ps[pslot][:, 0:512],
                        lhsT=wdeq[:, k, m * 128:(m + 1) * 128], rhs=src[:, k, a:b],
                        start=st, stop=False, skip_group_check=True), wl)
                mm = E.op("tensor", lambda e, m=m, a=a, b=b, pslot=pslot: e.matmul(
                    out=ps[pslot][:, 0:512], lhsT=lb_sb[:, m * 128:(m + 1) * 128],
                    rhs=ar_sb[:, a:b], start=False, stop=True, skip_group_check=True),
                    [("sync", lb_ld), ("vector", ar_done)])
                if add_residual:
                    sev = E.op("scalar", lambda e, m=m, pslot=pslot: e.activation(
                        out=sraw[:], in_=ps[pslot][:, 0:512], func=AF.Identity,
                        bias=qb_sb[:, m, 0:1]),
                        [("tensor", mm), ("sync", qb_ld),
                         ("vector", out_last if isinstance(out_last, int) else 0)])
                    psf[pslot] = ("scalar", sev)
                    out_last = E.op("vector", lambda e, m=m, a=a: e.tensor_tensor(
                        out=dst[:, m, a:a + 512], in0=sraw[:],
                        in1=hq[:, m, a:a + 512], op=ALU.add), [("scalar", sev)])
                else:
                    sev = E.op("scalar", lambda e, m=m, a=a, pslot=pslot: e.activation(
                        out=tq[:, m, a:a + 512], in_=ps[pslot][:, 0:512],
                        func=AF.Identity,
                        bias=qb_sb[:, m, 0:1]), [("tensor", mm), ("sync", qb_ld)])
                    psf[pslot] = ("scalar", sev)
                    out_last = sev
        wdeq_free = ("tensor", E.cnt["tensor"])
        return out_last

    prevw = [("sync", h_ready)]
    for blk in range(3):
        m1 = qlora_linear(2 * blk, hq, tq, False,
                          prevw + (bw6 if blk == 0 else []))
        if blk == 0 and "ql0" in taps:
            snap = E.snap()
            for k in range(KT):
                dma("sync", taps["ql0"][k * 128:(k + 1) * 128, :], tq[:, k, :],
                    E.bw(snap))
        lb_ld2 = dma("sync", lgt_b[:], lnb[blk].unsqueeze(2), bw6)
        lg_ld = dma("sync", lgt_g[:], lng[blk].unsqueeze(2), bw6)
        lnm = ln_stats(tq, TOK, (rowsq_mr[0], rowsq_mr[1], ar_sb), [("scalar", m1)])
        gm_last = 0
        for (a, b) in WIN_OWN:
            br = bcast2(rowsq_mr, a, 512, [("scalar", lnm)])
            for k in range(KT):
                v1 = E.op("vector", lambda e, k=k, a=a, b=b: e.tensor_tensor(
                    out=tq[:, k, a:b], in0=tq[:, k, a:b], in1=ps_big[:, 0:512],
                    op=ALU.subtract), [("tensor", br)])
                v2 = E.op("vector", lambda e, k=k, a=a, b=b: e.tensor_tensor(
                    out=tq[:, k, a:b], in0=tq[:, k, a:b], in1=ps_big[:, 512:1024],
                    op=ALU.mult), [("vector", v1)])
                v3 = E.op("vector", lambda e, k=k, a=a, b=b: e.tensor_scalar(
                    out=tq[:, k, a:b], in0=tq[:, k, a:b], scalar1=lgt_g[:, k, 0:1],
                    scalar2=lgt_b[:, k, 0:1], op0=ALU.mult, op1=ALU.add),
                    [("vector", v2), ("sync", lg_ld), ("sync", lb_ld2)])
                gm_last = E.op("scalar", lambda e, k=k, a=a, b=b: e.activation(
                    out=tq[:, k, a:b], in_=tq[:, k, a:b], func=AF.Gelu),
                    [("vector", v3)])
            psf["big"] = ("scalar", gm_last)
        if blk == 0 and "qln0" in taps:
            snap = E.snap()
            for k in range(KT):
                dma("sync", taps["qln0"][k * 128:(k + 1) * 128, :], tq[:, k, :],
                    E.bw(snap))
        ql2 = qlora_linear(2 * blk + 1, tq, hq, True, [("scalar", gm_last)])
        prevw = [("vector", ql2)]
        if blk == 0 and "qh0" in taps:
            snap = E.snap()
            for k in range(KT):
                dma("sync", taps["qh0"][k * 128:(k + 1) * 128, :], hq[:, k, :],
                    E.bw(snap))
    g1_done = 0
    for k in range(KT):
        g1_done = E.op("scalar", lambda e, k=k: e.activation(
            out=hq[:, k, :], in_=hq[:, k, :], func=AF.Silu),
            [("vector", E.cnt["vector"])])
        dma("sync", d_g[k * 128:(k + 1) * 128, :], hq[:, k, :],
            [("scalar", g1_done)])
    if "g1" in taps:
        snap = E.snap()
        dma("sync", taps["g1"], d_g, E.bw(snap))
    if "kf" in taps:
        snap = E.snap()
        dma("sync", taps["kf"], cc_k_out.ap(), E.bw(snap) + [("cc", E.cnt["cc"])])
    if "vf" in taps:
        snap = E.snap()
        dma("sync", taps["vf"], cc_vout.ap(), E.bw(snap) + [("cc", E.cnt["cc"])])
    r5.close()

    p6_done = E.snap()
    bw7 = E.bw(p6_done) + [("cc", E.cnt["cc"])]
    r6 = ExitStack(); sb6 = mk_sb(r6)
    kslab = sb6("kslab", [128, KT, 512])
    vslab = sb6("vslab", [128, 4, D])
    qtile = sb6("qtile", [128, KT, 128])
    wsm = sb6("wsm", [128, 512])
    wTs = sb6("wTs", [128, 4, TOK])
    mx4 = sb6("mx4", [128, 4])
    den4 = sb6("den4", [128, 4])
    nrm = sb6("nrm", [128, 16, 32])
    nrt = sb6("nrt", [128, 16])
    nr_row = sb6("nr_row", [16, 128])
    at_acc = sb6("at_acc", [128, KT, TOK])
    sattn = sb6("sattn", [128, KT, 512])
    xf_t2 = sb6("xf_t2", [128, KT, 512])
    g1w = sb6("g1w", [128, KT, 512])
    rnb = sb6("rnb", [128, 512])
    wo_sl = sb6("wo_sl", [128, KT, 128])
    bo_sb = sb6("bo_sb", [128, KT, 1])
    ident = sb6("ident", [128, 128])
    bo_ld = dma("sync", bo_sb[:], bo.unsqueeze(2), bw7)
    id_ld = dma("sync", ident[:], ident_in, bw7)
    acc_zero = 0
    for k in range(KT):
        acc_zero = E.op("vector", lambda e, k=k: e.memset(at_acc[:, k, :], 0.0),
                        bw7 if k == 0 else [])
    acc_zero = E.op("vector", lambda e: e.memset(nrm[:], 0.0))
    wsm_free = ("tensor", 0)
    wts_free = ("tensor", 0)
    psbig_free = psf["big"]
    qt_free = ("tensor", 0)

    for sl in range(8):
        half, q4 = sl // 4, sl % 4
        lk = dma("sync", kslab[:],
                 cc_k_out.ap()[half].rearrange("(k p) t -> p k t", p=128)
                 [:, :, q4 * 512:(q4 + 1) * 512],
                 bw7 + [("tensor", E.cnt["tensor"])])
        lv = dma("sync", vslab[:],
                 cc_vout.ap()[half, q4 * 512:(q4 + 1) * 512, :]
                 .rearrange("(s p) d -> p s d", p=128),
                 bw7 + [("tensor", E.cnt["tensor"])])
        wt_last = 0
        for qt_i in range(16):
            lq = dma("gpsimd", qtile[:],
                     d_q.rearrange("(k p) t -> p k t", p=128)
                     [:, :, qt_i * 128:(qt_i + 1) * 128], bw7 + [qt_free])
            mm = 0
            for k in range(KT):
                wl = ([("sync", lk), ("gpsimd", lq), psf["A"],
                       ("vector", acc_zero)] if k == 0 else [])
                mm = E.op("tensor", lambda e, k=k, st=(k == 0),
                          sp=(k == KT - 1): e.matmul(
                    out=ps["A"][:, 0:512], lhsT=qtile[:, k, :], rhs=kslab[:, k, :],
                    start=st, stop=sp, skip_group_check=True), wl)
            qt_free = ("tensor", mm)
            r1m = E.op("vector", lambda e: e.tensor_reduce(
                out=mx4[:], in_=ps["A"][:, 0:512].rearrange("p (g j) -> p g j", j=128),
                axis=AX.X, op=ALU.max), [("tensor", mm)])
            r2m = E.op("vector", lambda e: e.tensor_tensor(
                out=wsm[:].rearrange("p (g j) -> p g j", j=128),
                in0=ps["A"][:, 0:512].rearrange("p (g j) -> p g j", j=128),
                in1=mx4[:].unsqueeze(2).broadcast_to([128, 4, 128]),
                op=ALU.subtract),
                [("vector", r1m), wsm_free])
            psf["A"] = ("vector", r2m)
            se = 0
            for g in range(4):
                se = E.op("scalar", lambda e, g=g: e.activation(
                    out=wsm[:, g * 128:(g + 1) * 128],
                    in_=wsm[:, g * 128:(g + 1) * 128], func=AF.Exp,
                    accum_out=den4[:, g:g + 1]), [("vector", r2m)])
            rc = E.op("vector", lambda e: e.reciprocal(out=mx4[:], in_=den4[:]),
                      [("scalar", se)])
            wn = E.op("vector", lambda e: e.tensor_tensor(
                out=wsm[:].rearrange("p (g j) -> p g j", j=128),
                in0=wsm[:].rearrange("p (g j) -> p g j", j=128),
                in1=mx4[:].unsqueeze(2).broadcast_to([128, 4, 128]),
                op=ALU.mult), [("vector", rc)])
            E.op("vector", lambda e, qt_i=qt_i, sl=sl: e.tensor_tensor(
                out=nrm[:, qt_i, sl * 4:(sl + 1) * 4], in0=den4[:], in1=mx4[:],
                op=ALU.mult), [("vector", wn)])
            tp = 0
            for g in range(4):
                wl = [("vector", wn)] + ([psbig_free, ("vector", wt_last),
                                          ("sync", id_ld)] if g == 0 else [])
                tp = E.op("tensor", lambda e, g=g: e.transpose(
                    out=ps_big[:, g * 128:(g + 1) * 128],
                    in_=wsm[:, g * 128:(g + 1) * 128], identity=ident[:]), wl)
            wsm_free = ("tensor", tp)
            wt_last = E.op("vector", lambda e, qt_i=qt_i: e.tensor_copy(
                out=wTs[:, 0:4, qt_i * 128:(qt_i + 1) * 128],
                in_=ps_big[:, 0:512].rearrange("p (g j) -> p g j", j=128)),
                [("tensor", tp), wts_free])
            psbig_free = ("vector", wt_last)
        for dv in range(KT):
            for (a, b) in WIN_OWN:
                mm = 0
                for g in range(4):
                    wl = ([("sync", lv), ("vector", wt_last), psf["C"]]
                          if g == 0 else [])
                    mm = E.op("tensor", lambda e, dv=dv, g=g, a=a, b=b,
                              st=(g == 0), sp=(g == 3): e.matmul(
                        out=ps["C"][:, 0:512],
                        lhsT=vslab[:, g, dv * 128:(dv + 1) * 128],
                        rhs=wTs[:, g, a:b], start=st, stop=sp,
                        skip_group_check=True), wl)
                avm = E.op("vector", lambda e, dv=dv, a=a: e.tensor_tensor(
                    out=at_acc[:, dv, a:a + 512], in0=ps["C"][:, 0:512],
                    in1=at_acc[:, dv, a:a + 512], op=ALU.add), [("tensor", mm)])
                psf["C"] = ("vector", avm)
        wts_free = ("tensor", E.cnt["tensor"])

    nr1 = E.op("vector", lambda e: e.tensor_reduce(
        out=nrt[:], in_=nrm[:], axis=AX.X, op=ALU.add),
        [("vector", E.cnt["vector"])])
    nr2 = E.op("vector", lambda e: e.tensor_scalar(
        out=nrt[:], in0=nrt[:], scalar1=1e-6, scalar2=None, op0=ALU.add),
        [("vector", nr1)])
    nr3 = E.op("vector", lambda e: e.reciprocal(out=nrt[:], in_=nrt[:]),
               [("vector", nr2)])
    ntp = E.op("tensor", lambda e: e.transpose(
        out=ps["D"][0:16, 0:128], in_=nrt[:], identity=ident[:]),
        [("vector", nr3), psf["D"], ("sync", id_ld)])
    psf["D"] = ("tensor", ntp)
    nrow = E.op("vector", lambda e: e.tensor_copy(out=nr_row[:], in_=ps["D"][0:16, 0:128]),
                [("tensor", ntp)])
    psf["D"] = ("vector", nrow)
    nb = dma("sync", d_nrm.rearrange("a (q j) -> (a q) j", j=128), nr_row[:],
             [("vector", nrow)])
    if "attn" in taps:
        snap = E.snap()
        for k in range(KT):
            dma("sync", taps["attn"][k * 128:(k + 1) * 128, :], at_acc[:, k, :],
                E.bw(snap))

    wo_used = [0, 0]
    out_last = 0
    sc_last = 0
    for wi, (a, b) in enumerate(WIN_OWN):
        rb = dma("sync", rnb[:], d_nrm[0:1, a:b].broadcast_to([128, 512]),
                 [("sync", nb), ("vector", sc_last)])
        sc = 0
        for dv in range(KT):
            sc = E.op("vector", lambda e, dv=dv, a=a: e.tensor_tensor(
                out=at_acc[:, dv, a:a + 512], in0=at_acc[:, dv, a:a + 512],
                in1=rnb[:], op=ALU.mult), [("sync", rb)])
        sc_last = sc
        fx1 = dma("gpsimd", xf_t2[:],
                  d_x1.rearrange("(k p) t -> p k t", p=128)[:, :, a:b],
                  [("vector", out_last)])
        fg1 = dma("gpsimd", g1w[:],
                  d_g.rearrange("(k p) t -> p k t", p=128)[:, :, a:b],
                  [("vector", out_last)])
        for m in range(KT):
            pslot = "A" if m % 2 == 0 else "B"
            ldw = dma("gpsimd", wo_sl[:],
                      rw(woT, D)[:, :, m * 128:(m + 1) * 128],
                      [("tensor", wo_used[0])])
            mm = 0
            for k in range(KT):
                wl = ([("gpsimd", ldw), ("vector", sc), psf[pslot]]
                      if k == 0 else [])
                mm = E.op("tensor", lambda e, m=m, k=k, a=a, pslot=pslot,
                          st=(k == 0), sp=(k == KT - 1): e.matmul(
                    out=ps[pslot][:, 0:512],
                    lhsT=wo_sl[:, k, :],
                    rhs=at_acc[:, k, a:a + 512], start=st, stop=sp,
                    skip_group_check=True), wl)
            wo_used[0] = mm
            sev = E.op("scalar", lambda e, m=m, pslot=pslot: e.activation(
                out=sattn[:, m, :], in_=ps[pslot][:, 0:512], func=AF.Identity,
                bias=bo_sb[:, m, 0:1]), [("tensor", mm), ("sync", bo_ld)])
            psf[pslot] = ("scalar", sev)
            v1 = E.op("vector", lambda e, m=m: e.tensor_tensor(
                out=sattn[:, m, :], in0=sattn[:, m, :], in1=g1w[:, m, :],
                op=ALU.mult), [("scalar", sev), ("gpsimd", fg1)])
            v2 = E.op("vector", lambda e, m=m: e.tensor_tensor(
                out=sattn[:, m, :], in0=sattn[:, m, :], in1=xf_t2[:, m, :],
                op=ALU.add), [("vector", v1), ("gpsimd", fx1)])
            out_last = v2
            dma("sync", out_fm[m * 128:(m + 1) * 128, a:b], sattn[:, m, :],
                [("vector", v2)])
    if "sattn" in taps:
        snap = E.snap()
        dma("sync", taps["sattn"], out_fm, E.bw(snap))
    r6.close()

    fin = E.snap()
    for e in ENGINES:
        E.q[e].append((None, [("sync*", fin["sync"]), ("gpsimd*", fin["gpsimd"])],
                       None, None))

    with nc.Block() as block:
        def runner(name):
            def go(eng):
                for (fn, waits, inc_sem, inc_val) in E.q[name]:
                    resolved = {}
                    def addw(h, v):
                        if v > resolved.get(h, 0):
                            resolved[h] = v
                    for (s, v) in waits:
                        if s in ("sync", "gpsimd"):
                            j, c, _ = E.qs[s]["map"][v]
                            addw((s, j), 16 * c)
                        elif s in ("sync*", "gpsimd*"):
                            qn = s[:-1]
                            if v > 0:
                                vec = E.qs[qn]["map"][v][2]
                                for j in range(NQS):
                                    if vec[j] > 0:
                                        addw((qn, j), 16 * vec[j])
                        else:
                            addw(s, v)
                    for h, v in resolved.items():
                        if isinstance(h, tuple):
                            eng.wait_ge(qsems[h[0]][h[1]], v)
                        else:
                            eng.wait_ge(sem[h], v)
                    if fn is None:
                        continue
                    if inc_sem in ("sync", "gpsimd"):
                        j, c, _ = E.qs[inc_sem]["map"][inc_val]
                        if c > 1:
                            eng.wait_ge(qsems[inc_sem][j], 16 * (c - 1))
                        ins = fn(eng)
                        ins.then_inc(qsems[inc_sem][j], 16)
                    else:
                        ins = fn(eng)
                        ins.then_inc(sem[inc_sem], 1)
            return go
        block.tensor(runner("tensor"))
        block.vector(runner("vector"))
        block.scalar(runner("scalar"))
        block.gpsimd(runner("gpsimd"))
        block.sync(runner("sync"))

    es.close()
    return nc


def _host_inputs(inputs):
    x = np.asarray(inputs["L0_attn_x"])
    cores = []
    T = lambda a: np.ascontiguousarray(a, dtype=np.float32)

    def pcol(arr, cols):
        a = np.asarray(arr).reshape(-1, 128, cols)
        return np.ascontiguousarray(
            a.transpose(1, 0, 2).reshape(128, -1), dtype=np.float32)

    for c in range(8):
        b, h = c // 2, c % 2
        xb = x[b]
        if h == 0:
            own = xb[0:2048]
            ah = np.zeros((3, D), np.float32)
            bh = xb[2048:2051][::-1]
            A_sfx, B_sfx = "f", "b"
        else:
            own = xb[2048:4096][::-1]
            ah = np.zeros((3, D), np.float32)
            bh = xb[2045:2048]
            A_sfx, B_sfx = "b", "f"
        xall = np.concatenate([ah, own, bh], axis=0)
        d = {"xf": T(xall.T)}
        for pn, sfx in (("a", A_sfx), ("b", B_sfx)):
            d[pn + "_winT"] = T(np.asarray(inputs[f"L0_attn_win_{sfx}"]).T)
            d[pn + "_wconv"] = pcol(np.asarray(inputs[f"L0_attn_wconv_{sfx}"]).reshape(DI, 4), 4)
            d[pn + "_bconv"] = pcol(inputs[f"L0_attn_bconv_{sfx}"], 1)
            d[pn + "_wxT"] = T(np.asarray(inputs[f"L0_attn_wx_{sfx}"]).T)
            d[pn + "_wdtT"] = T(np.asarray(inputs[f"L0_attn_wdt_{sfx}"]).T)
            d[pn + "_bdt"] = pcol(inputs[f"L0_attn_bdt_{sfx}"], 1)
            d[pn + "_alog"] = pcol(inputs[f"L0_attn_Alog_{sfx}"], NS)
            d[pn + "_dp"] = pcol(inputs[f"L0_attn_Dp_{sfx}"], 1)
            d[pn + "_woutT"] = T(np.asarray(inputs[f"L0_attn_wout_{sfx}"]).T)
            half = slice(0, D) if sfx == "f" else slice(D, 2 * D)
            d[pn + "_wprojT"] = T(np.asarray(inputs["L0_attn_wproj"])[:, half].T)
        d["bproj"] = pcol(inputs["L0_attn_bproj"], 1)
        d["w1"] = T(inputs["L0_ffn_W1"])
        d["b1"] = pcol(inputs["L0_ffn_b1"], 1)
        d["w2"] = T(inputs["L0_ffn_W2"])
        d["b2"] = pcol(inputs["L0_ffn_b2"], 1)
        d["wqT"] = T(np.asarray(inputs["L1_attn_Wq"]).T)
        d["bq"] = pcol(inputs["L1_attn_bq"], 1)
        d["wkT"] = T(np.asarray(inputs["L1_attn_Wk"]).T)
        d["bk"] = pcol(inputs["L1_attn_bk"], 1)
        d["wvT"] = T(np.asarray(inputs["L1_attn_Wv"]).T)
        d["bv_row"] = T(np.asarray(inputs["L1_attn_bv"]).reshape(1, D))
        d["woT"] = T(np.asarray(inputs["L1_attn_Wo"]).T)
        d["bo"] = pcol(inputs["L1_attn_bo"], 1)
        d["qTw"] = np.ascontiguousarray(
            np.transpose(np.asarray(inputs["L1_ffn_qweights"]), (0, 2, 1)),
            dtype=np.int32)
        scr = np.repeat(np.asarray(inputs["L1_ffn_scales"]), 16, axis=2)
        d["scrT"] = T(np.transpose(scr, (0, 2, 1)))
        d["qb"] = np.stack([pcol(np.asarray(inputs["L1_ffn_biases"])[i], 1) for i in range(6)])
        d["laT"] = T(np.transpose(np.asarray(inputs["L1_ffn_lora_a"]), (0, 2, 1)))
        d["lbT"] = T(np.transpose(np.asarray(inputs["L1_ffn_lora_b"]), (0, 2, 1)))
        d["lng"] = np.stack([pcol(np.asarray(inputs["L1_ffn_ln_gamma"])[i], 1) for i in range(3)])
        d["lnb"] = np.stack([pcol(np.asarray(inputs["L1_ffn_ln_beta"])[i], 1) for i in range(3)])
        d["ident_in"] = np.eye(128, dtype=np.float32)
        cores.append(d)
    return cores


def kernel(**inputs):
    global LAST_TAPS
    nc = build(tuple(TAPS))
    in_maps = _host_inputs(inputs)
    res = run_bass_kernel_spmd(nc, in_maps, core_ids=list(range(8)))
    out = np.zeros((4, SEQ, D), np.float32)
    LAST_TAPS = {}
    for c in range(8):
        b, h = c // 2, c % 2
        o = np.asarray(res.results[c]["out_fm"])
        if h == 0:
            out[b, 0:2048] = o.T
        else:
            out[b, 2048:4096] = o.T[::-1]
        for nm in TAPS:
            LAST_TAPS.setdefault(nm, []).append(np.asarray(res.results[c]["tap_" + nm]))
    return out


# version 09
# speedup vs baseline: 1.2406x; 1.2284x over previous
import numpy as np
import concourse.bass as bass
import concourse.mybir as mybir
from concourse.bass_utils import run_bass_kernel_spmd
from contextlib import ExitStack

F32 = mybir.dt.float32
I32 = mybir.dt.int32
AF = mybir.ActivationFunctionType
ALU = mybir.AluOpType
AX = mybir.AxisListType

D = 1024
KT = 8
DI = 2048
DIT = 16
NS = 16
DTR = 64
TOK = 2048
GRID = 2051
XR = 2054
SEQ = 4096
EPS = 1e-5
WIN_OWN = [(a, a + 512) for a in range(0, 2048, 512)]
WIN_GRID = WIN_OWN + [(2048, 2051)]
ENGINES = ("tensor", "vector", "scalar", "gpsimd", "sync")
RG = [[0, 1], [2, 3], [4, 5], [6, 7]]

TAPS = []
LAST_TAPS = {}
SERIAL = False
SERIAL_SITES = set()


NQS = 6


class Em:
    def __init__(self):
        self.q = {e: [] for e in ENGINES}
        self.cnt = {e: 0 for e in ENGINES}
        self.cnt["cc"] = 0
        self.qs = {q: {"idx": 0, "c": [0] * NQS, "map": {}} for q in
                   ("sync", "gpsimd")}

    def op(self, eng, fn, waits=(), site=None):
        if (SERIAL is True or (SERIAL and SERIAL is not False and eng in SERIAL)
                or (site is not None and site in SERIAL_SITES)):
            waits = list(waits) + [(e + "*", self.cnt[e]) if e in ("sync", "gpsimd")
                                   else (e, self.cnt[e])
                                   for e in ENGINES if e != eng] + \
                    ([("cc", self.cnt["cc"])] if self.cnt["cc"] else [])
        step = 16 if eng in ("sync", "gpsimd") else 1
        self.cnt[eng] += step
        if eng in ("sync", "gpsimd"):
            qs = self.qs[eng]
            j = qs["idx"]; qs["idx"] = (j + 1) % NQS
            qs["c"][j] += 1
            qs["map"][self.cnt[eng]] = (j, qs["c"][j], tuple(qs["c"]))
        self.q[eng].append((fn, [(s, v) for (s, v) in waits if v > 0],
                            eng, self.cnt[eng]))
        return self.cnt[eng]

    def cc(self, fn, waits=()):
        self.cnt["cc"] += 1
        self.q["gpsimd"].append((fn, [(s, v) for (s, v) in waits if v > 0],
                                 "cc", self.cnt["cc"]))
        return self.cnt["cc"]

    def snap(self):
        return dict(self.cnt)

    def bw(self, snap_):
        w = []
        for e in ENGINES:
            w.append((e + "*", snap_[e]) if e in ("sync", "gpsimd")
                     else (e, snap_[e]))
        if snap_.get("cc"):
            w.append(("cc", snap_["cc"]))
        return w


def build(dbg_taps=()):
    nc = bass.Bass("TRN2", target_bir_lowering=False)
    E = Em()

    def din(name, shape, dt=F32):
        return nc.dram_tensor(name, shape, dt, kind="ExternalInput").ap()

    xf_in = din("xf", [D, XR])
    pw = {}
    for p in ("a", "b"):
        pw[p] = dict(
            winT=din(p + "_winT", [D, 2 * DI]),
            wconv=din(p + "_wconv", [128, DIT * 4]),
            bconv=din(p + "_bconv", [128, DIT]),
            wxT=din(p + "_wxT", [DI, 96]),
            wdtT=din(p + "_wdtT", [DTR, DI]),
            bdt=din(p + "_bdt", [128, DIT]),
            alog=din(p + "_alog", [128, DIT * NS]),
            dp=din(p + "_dp", [128, DIT]),
            woutT=din(p + "_woutT", [DI, D]),
            wprojT=din(p + "_wprojT", [D, D]),
        )
    bproj = din("bproj", [128, KT])
    w1 = din("w1", [D, 4 * D]); b1 = din("b1", [128, 32])
    w2 = din("w2", [4 * D, D]); b2 = din("b2", [128, KT])
    wqT = din("wqT", [D, D]); bq = din("bq", [128, KT])
    wkT = din("wkT", [D, D]); bk = din("bk", [128, KT])
    wvT = din("wvT", [D, D]); bv_row = din("bv_row", [1, D])
    woT = din("woT", [D, D]); bo = din("bo", [128, KT])
    qTw = din("qTw", [6, D, D], I32)
    scrT = din("scrT", [6, D, D])
    qb = din("qb", [6, 128, KT])
    laT = din("laT", [6, D, 32])
    lbT = din("lbT", [6, 32, D])
    lng = din("lng", [3, 128, KT])
    lnb = din("lnb", [3, 128, KT])
    ident_in = din("ident_in", [128, 128])

    out_fm = nc.dram_tensor("out_fm", [D, TOK], F32, kind="ExternalOutput").ap()

    d_u = [nc.dram_tensor(f"d_u{p}", [DIT, 128, TOK], F32).ap() for p in range(2)]
    d_zs = [nc.dram_tensor(f"d_zs{p}", [DIT, 128, TOK], F32).ap() for p in range(2)]
    d_dl = [nc.dram_tensor(f"d_dl{p}", [DIT, 128, TOK], F32).ap() for p in range(2)]
    d_y = [nc.dram_tensor(f"d_y{p}", [DIT, 128, TOK], F32).ap() for p in range(2)]
    d_bc = [nc.dram_tensor(f"d_bc{p}", [2, NS, TOK], F32).ap() for p in range(2)]
    cc_st_in = nc.dram_tensor("cc_st_in", [DI, NS], F32)
    cc_st_out = nc.dram_tensor("cc_st_out", [DI, NS], F32)
    cc_k_in = nc.dram_tensor("cc_k_in", [D, TOK], F32)
    cc_k_out = nc.dram_tensor("cc_k_out", [2, D, TOK], F32)
    cc_vin = nc.dram_tensor("cc_vin", [TOK, D], F32)
    cc_vout = nc.dram_tensor("cc_vout", [2, TOK, D], F32)
    d_q = nc.dram_tensor("d_q", [D, TOK], F32).ap()
    d_x1 = nc.dram_tensor("d_x1", [D, TOK], F32).ap()
    d_xn1 = nc.dram_tensor("d_xn1", [D, TOK], F32).ap()
    d_g = nc.dram_tensor("d_g", [D, TOK], F32).ap()
    d_nrm = nc.dram_tensor("d_nrm", [1, TOK], F32).ap()
    d_rows = nc.dram_tensor("d_rows", [2, XR], F32).ap()

    taps = {}
    tap_shapes = {"s_acc": [D, TOK], "x1": [D, TOK], "xn": [D, XR], "g1": [D, TOK],
                  "xn1": [D, TOK], "y0": [DIT, 128, TOK], "u0": [DIT, 128, TOK],
                  "dl0": [DIT, 128, TOK], "zs0": [DIT, 128, TOK], "bc0": [2, NS, TOK],
                  "y1": [DIT, 128, TOK], "q": [D, TOK], "kf": [2, D, TOK],
                  "vf": [2, TOK, D], "attn": [D, TOK], "sattn": [D, TOK],
                  "st": [DI, NS], "bi": [DI, NS],
                  "ql0": [D, TOK], "qln0": [D, TOK], "qh0": [D, TOK]}
    for nm in dbg_taps:
        taps[nm] = nc.dram_tensor("tap_" + nm, tap_shapes[nm], F32,
                                  kind="ExternalOutput").ap()

    es = ExitStack()
    sem = {}
    for name in ("tensor", "vector", "scalar", "cc"):
        sem[name] = es.enter_context(nc.semaphore("sem_" + name))
    qsems = {}
    for qn in ("sync", "gpsimd"):
        qsems[qn] = [es.enter_context(nc.semaphore(f"sem_{qn}{j}"))
                     for j in range(NQS)]

    def mk_sb(stack):
        return lambda n, shp, dt=F32: stack.enter_context(nc.sbuf_tensor(n, shp, dt))

    sb = mk_sb(es)
    ones_col = sb("ones_col", [128, 1])
    ones_row = sb("ones_row", [1, 128])
    sqt = sb("sqt", [128, 512])
    small = {}
    for p in range(2):
        small[p] = dict(
            wconv=sb(f"wconv{p}", [128, DIT, 4]),
            bconv=sb(f"bconv{p}", [128, DIT, 1]),
            bdt=sb(f"bdt{p}", [128, DIT, 1]),
            A=sb(f"A{p}", [128, DIT, NS]),
            dp=sb(f"dp{p}", [128, DIT, 1]),
        )
    bproj_sb = sb("bproj_sb", [128, KT, 1])
    states = sb("states", [128, DIT, NS])
    binit = sb("binit", [128, DIT, NS])

    ps = {}
    for nm in ("A", "B", "C", "D"):
        ps[nm] = es.enter_context(nc.psum_tensor("ps" + nm, [128, 512], F32))
    ps_big = es.enter_context(nc.psum_tensor("ps_big", [128, 2048], F32))
    psf = {k: ("vector", 0) for k in ("A", "B", "C", "D", "big")}

    def dma(qeng, out, in_, waits=(), site=None):
        return E.op(qeng, lambda e, o=out, i=in_: e.dma_start(out=o, in_=i), waits,
                    site=site)

    def rw(dram_ap, cols):
        return dram_ap.rearrange("(k p) n -> p k n", p=128)

    def ln_stats(xt, ncols, rows, waits_in):
        rm, rs, rt = rows
        wins = [(a, min(a + 512, ncols)) for a in range(0, ncols, 512)]
        vm = 0
        for (a, b) in wins:
            w = b - a
            last = 0
            for k in range(KT):
                wl = list(waits_in) + [psf["A"], ("vector", vm)] if k == 0 else []
                last = E.op("tensor", lambda e, k=k, a=a, b=b, w=w: e.matmul(
                    out=ps["A"][0:1, 0:w], lhsT=ones_col[:], rhs=xt[:, k, a:b],
                    start=(k == 0), stop=(k == KT - 1), skip_group_check=True), wl)
            m1 = E.op("vector", lambda e, a=a, w=w: e.tensor_scalar(
                out=rm[0:1, a:a + w], in0=ps["A"][0:1, 0:w], scalar1=1.0 / D,
                scalar2=None, op0=ALU.mult), [("tensor", last)])
            psf["A"] = ("vector", m1)
            last = 0
            for k in range(KT):
                wl = list(waits_in) + [psf["B"]] if k == 0 else [("tensor", last)]
                sqm = E.op("scalar", lambda e, k=k, a=a, b=b: e.activation(
                    out=sqt[:, 0:b - a], in_=xt[:, k, a:b], func=AF.Square), wl)
                last = E.op("tensor", lambda e, k=k, w=w: e.matmul(
                    out=ps["B"][0:1, 0:w], lhsT=ones_col[:], rhs=sqt[:, 0:w],
                    start=(k == 0), stop=(k == KT - 1), skip_group_check=True),
                    [("scalar", sqm)])
            vm = E.op("vector", lambda e, a=a, w=w: e.tensor_scalar(
                out=rs[0:1, a:a + w], in0=ps["B"][0:1, 0:w], scalar1=1.0 / D,
                scalar2=None, op0=ALU.mult), [("tensor", last)])
            psf["B"] = ("vector", vm)
        v1 = E.op("vector", lambda e, n=ncols: e.tensor_tensor(
            out=rt[0:1, 0:n], in0=rm[0:1, 0:n], in1=rm[0:1, 0:n], op=ALU.mult),
            [("vector", m1), ("vector", vm)])
        v2 = E.op("vector", lambda e, n=ncols: e.tensor_tensor(
            out=rs[0:1, 0:n], in0=rs[0:1, 0:n], in1=rt[0:1, 0:n],
            op=ALU.subtract), [("vector", v1)])
        v3 = E.op("vector", lambda e, n=ncols: e.tensor_scalar(
            out=rs[0:1, 0:n], in0=rs[0:1, 0:n], scalar1=EPS, scalar2=None,
            op0=ALU.add), [("vector", v2)])
        s1 = E.op("scalar", lambda e, n=ncols: e.activation(
            out=rt[0:1, 0:n], in_=rs[0:1, 0:n], func=AF.Ln), [("vector", v3)])
        return E.op("scalar", lambda e, n=ncols: e.activation(
            out=rs[0:1, 0:n], in_=rt[0:1, 0:n], func=AF.Exp, scale=-0.5),
            [("scalar", s1)])

    def bcast2(rows, a, w, waits):
        rm, rs = rows[0], rows[1]
        t1 = E.op("tensor", lambda e, a=a, w=w: e.matmul(
            out=ps_big[:, 0:w], lhsT=ones_row[0:1, :], rhs=rm[0:1, a:a + w],
            start=True, stop=True, skip_group_check=True), list(waits) + [psf["big"]])
        return E.op("tensor", lambda e, a=a, w=w: e.matmul(
            out=ps_big[:, 512:512 + w], lhsT=ones_row[0:1, :],
            rhs=rs[0:1, a:a + w],
            start=True, stop=True, skip_group_check=True), [("tensor", t1)])

    r1 = ExitStack(); sb1 = mk_sb(r1)
    rows0 = tuple(sb1(f"rows0_{i}", [1, XR]) for i in range(3))
    xn = sb1("xn", [128, KT, XR])
    r1a = ExitStack(); sb1a = mk_sb(r1a)
    x_t = sb1a("x_t", [128, KT, XR])

    ld_x = dma("sync", x_t[:], rw(xf_in, XR))
    for p, pn in ((0, "a"), (1, "b")):
        for nm, key, cols in (("wconv", "wconv", 4), ("bconv", "bconv", 1),
                              ("bdt", "bdt", 1), ("dp", "dp", 1), ("alog", "A", NS)):
            dma("sync", small[p][key][:],
                pw[pn][nm].rearrange("p (k c) -> p k c", c=cols))
    dma("sync", bproj_sb[:], bproj.unsqueeze(2))
    ld_all = E.cnt["sync"]

    mz = E.op("vector", lambda e: e.memset(ones_col[:], 1.0))
    mz2 = E.op("vector", lambda e: e.memset(ones_row[:], 1.0), [("vector", mz)])
    aexp_last = 0
    for p in range(2):
        s1 = E.op("scalar", lambda e, p=p: e.activation(
            out=small[p]["A"][:], in_=small[p]["A"][:], func=AF.Exp),
            [("sync*", ld_all)])
        aexp_last = E.op("vector", lambda e, p=p: e.tensor_scalar(
            out=small[p]["A"][:], in0=small[p]["A"][:], scalar1=-1.0, scalar2=None,
            op0=ALU.mult), [("scalar", s1)])

    ln0_done = ln_stats(x_t, XR, rows0, [("sync", ld_x), ("vector", mz2)])
    xn_done = 0
    for (a, b) in [(i, min(i + 512, XR)) for i in range(0, XR, 512)]:
        w = b - a
        br = bcast2(rows0, a, w, [("scalar", ln0_done)])
        for k in range(KT):
            v1 = E.op("vector", lambda e, k=k, a=a, b=b, w=w: e.tensor_tensor(
                out=xn[:, k, a:b], in0=x_t[:, k, a:b], in1=ps_big[:, 0:w],
                op=ALU.subtract), [("tensor", br)])
            xn_done = E.op("vector", lambda e, k=k, a=a, b=b, w=w: e.tensor_tensor(
                out=xn[:, k, a:b], in0=xn[:, k, a:b], in1=ps_big[:, 512:512 + w],
                op=ALU.mult), [("vector", v1)])
        psf["big"] = ("vector", xn_done)
    dma("sync", d_rows[0:1, :], rows0[0][0:1, :], [("scalar", ln0_done)])
    dma("sync", d_rows[1:2, :], rows0[1][0:1, :], [("scalar", ln0_done)])
    if "xn" in taps:
        for k in range(KT):
            dma("sync", taps["xn"][k * 128:(k + 1) * 128, :], xn[:, k, :],
                [("vector", xn_done)])
    r1a.close()

    def moving_pieces(pipe, a, b, grid):
        out = []
        if pipe == 0:
            if grid:
                out.append((0, b - a, lambda k, a=a, b=b: xn[:, k, a:b]))
            else:
                out.append((0, b - a, lambda k, a=a, b=b: xn[:, k, 3 + a:3 + b]))
        else:
            if grid:
                if a < 3:
                    h = min(b, 3)
                    out.append((0, h - a, lambda k, a=a, h=h: xn[:, k, 2051 + a:2051 + h]))
                if b > 3:
                    p0 = max(a, 3)
                    out.append((p0 - a, b - p0,
                                lambda k, p0=p0, b=b: xn[:, k, 2053 - p0:2053 - b:-1]))
            else:
                out.append((0, b - a, lambda k, a=a, b=b: xn[:, k, 2050 - a:2050 - b:-1]))
        return out

    wslab = sb1("wslab", [128, 2, KT, 128])
    u_raw = sb1("u_raw", [128, GRID])
    u_s = sb1("u_s", [128, TOK])
    zbuf = sb1("zbuf", [128, 2, 512])
    wx_sb = sb1("wx_sb", [128, DIT, 96])
    wdt_sb = sb1("wdt_sb", [DTR, DI])
    dt_sb = sb1("dt_sb", [DTR, TOK])
    bc_sb = sb1("bc_sb", [32, TOK])
    dlw = sb1("dlw", [128, 512])
    dl_t = sb1("dl_t", [128, TOK])
    slab_used = [0, 0]
    zb_used = [0, 0]

    def stage1(pipe, pname, dbc_prev_evac, bw0):
        wname = pw[pname]
        ld_wx = dma("sync", wx_sb[:], rw(wname["wxT"], 96),
                    [("vector", xn_done)] + bw0)
        ld_wdt = dma("sync", wdt_sb[:], wname["wdtT"],
                     [("vector", xn_done)] + bw0)
        u_s_free = ("gpsimd", 0)
        u_raw_free = ("vector", 0)
        for m in range(32):
            slot = m % 2
            ld = dma("gpsimd", wslab[:, slot, :, :],
                     rw(wname["winT"], 2 * DI)[:, :, m * 128:(m + 1) * 128],
                     [("tensor", slab_used[slot]), ("vector", xn_done)])
            is_u = m < 16
            dt_i = m if is_u else m - 16
            wins = WIN_GRID if is_u else WIN_OWN
            for wi, (a, b) in enumerate(wins):
                pslot = "A" if wi % 2 == 0 else "B"
                last_mm = 0
                first = True
                for (coff, width, mov) in moving_pieces(pipe, a, b, is_u):
                    for k in range(KT):
                        wl = []
                        if first:
                            wl = [("gpsimd", ld), psf[pslot], ("vector", xn_done)]
                            first = False
                        last_mm = E.op("tensor", lambda e, k=k, coff=coff,
                                       width=width, mov=mov, pslot=pslot, slot=slot,
                                       st=(k == 0), sp=(k == KT - 1): e.matmul(
                            out=ps[pslot][:, coff:coff + width],
                            lhsT=wslab[:, slot, k, :], rhs=mov(k),
                            start=st, stop=sp, skip_group_check=True), wl)
                slab_used[slot] = last_mm
                if is_u:
                    ev = E.op("scalar", lambda e, a=a, b=b, pslot=pslot: e.activation(
                        out=u_raw[:, a:b], in_=ps[pslot][:, 0:b - a], func=AF.Copy),
                        [("tensor", last_mm), u_raw_free])
                    psf[pslot] = ("scalar", ev)
                else:
                    zslot = wi % 2
                    ev = E.op("scalar", lambda e, b=b, a=a, pslot=pslot,
                              zslot=zslot: e.activation(
                        out=zbuf[:, zslot, 0:b - a], in_=ps[pslot][:, 0:b - a],
                        func=AF.Silu),
                        [("tensor", last_mm), ("gpsimd", zb_used[zslot])])
                    psf[pslot] = ("scalar", ev)
                    zb_used[zslot] = dma("gpsimd", d_zs[pipe][dt_i, :, a:b],
                                         zbuf[:, zslot, 0:b - a], [("scalar", ev)])
            if is_u:
                sA = psf["A"][1] if psf["A"][0] == "scalar" else 0
                sB = psf["B"][1] if psf["B"][0] == "scalar" else 0
                c0 = E.op("vector", lambda e, dt_i=dt_i, p=pipe: e.tensor_scalar(
                    out=u_s[:], in0=u_raw[:, 0:TOK],
                    scalar1=small[p]["wconv"][:, dt_i, 0:1], scalar2=None,
                    op0=ALU.mult),
                    [("scalar", sA), ("scalar", sB), ("vector", aexp_last), u_s_free])
                for j in (1, 2, 3):
                    c0 = E.op("vector", lambda e, j=j, dt_i=dt_i, p=pipe: e.scalar_tensor_tensor(
                        out=u_s[:], in0=u_raw[:, j:j + TOK],
                        scalar=small[p]["wconv"][:, dt_i, j:j + 1], in1=u_s[:],
                        op0=ALU.mult, op1=ALU.add), [("vector", c0)])
                u_raw_free = ("vector", c0)
                cs = E.op("scalar", lambda e, dt_i=dt_i, p=pipe: e.activation(
                    out=u_s[:], in_=u_s[:], func=AF.Silu,
                    bias=small[p]["bconv"][:, dt_i, 0:1]), [("vector", c0)])
                for w2 in range(4):
                    wl = [("scalar", cs), ("sync", ld_wx)]
                    if dt_i == 0 and w2 == 0:
                        wl.append(dbc_prev_evac)
                    E.op("tensor", lambda e, w2=w2, dt_i=dt_i,
                         st=(dt_i == 0), sp=(dt_i == DIT - 1): e.matmul(
                        out=ps_big[0:96, w2 * 512:(w2 + 1) * 512],
                        lhsT=wx_sb[:, dt_i, :], rhs=u_s[:, w2 * 512:(w2 + 1) * 512],
                        start=st, stop=sp, skip_group_check=True),
                        wl if w2 == 0 else [])
                usp = dma("gpsimd", d_u[pipe][dt_i, :, :], u_s[:],
                          [("scalar", cs), ("tensor", E.cnt["tensor"])])
                u_s_free = ("gpsimd", usp)
        tm = E.cnt["tensor"]
        e1 = E.op("vector", lambda e: e.tensor_copy(out=dt_sb[:], in_=ps_big[0:DTR, :]),
                  [("tensor", tm)] + bw0)
        e2 = E.op("vector", lambda e: e.tensor_copy(out=bc_sb[:], in_=ps_big[DTR:96, :]),
                  [("vector", e1)] + bw0)
        dma("gpsimd", d_bc[pipe].rearrange("a n t -> (a n) t"), bc_sb[:],
            [("vector", e2)])
        psf["big"] = ("vector", e2)
        dl_free = ("gpsimd", 0)
        psc_free = psf["C"]
        for dt_i in range(DIT):
            vlast = 0
            for w2 in range(4):
                mm = E.op("tensor", lambda e, dt_i=dt_i, w2=w2: e.matmul(
                    out=ps["C"][:, 0:512], lhsT=wdt_sb[:, dt_i * 128:(dt_i + 1) * 128],
                    rhs=dt_sb[:, w2 * 512:(w2 + 1) * 512], start=True, stop=True,
                    skip_group_check=True),
                    [("vector", e1), ("sync", ld_wdt), psc_free])
                s1 = E.op("scalar", lambda e, dt_i=dt_i, p=pipe: e.activation(
                    out=dlw[:], in_=ps["C"][:, 0:512], func=AF.Abs,
                    bias=small[p]["bdt"][:, dt_i, 0:1]), [("tensor", mm), dl_free])
                s2 = E.op("scalar", lambda e: e.activation(
                    out=dlw[:], in_=dlw[:], func=AF.Exp, scale=-1.0), [("scalar", s1)])
                s3 = E.op("scalar", lambda e: e.activation(
                    out=dlw[:], in_=dlw[:], func=AF.Ln, bias=1.0), [("scalar", s2)])
                s4 = E.op("scalar", lambda e, dt_i=dt_i, w2=w2, p=pipe: e.activation(
                    out=dl_t[:, w2 * 512:(w2 + 1) * 512], in_=ps["C"][:, 0:512],
                    func=AF.Relu, bias=small[p]["bdt"][:, dt_i, 0:1]), [("scalar", s3)])
                psc_free = ("scalar", s4)
                vlast = E.op("vector", lambda e, w2=w2: e.tensor_tensor(
                    out=dl_t[:, w2 * 512:(w2 + 1) * 512],
                    in0=dl_t[:, w2 * 512:(w2 + 1) * 512], in1=dlw[:], op=ALU.add),
                    [("scalar", s4)])
            dsp = dma("gpsimd", d_dl[pipe][dt_i, :, :], dl_t[:], [("vector", vlast)])
            dl_free = ("gpsimd", dsp)
        psf["C"] = psc_free
        return ("vector", e2)

    dbc_ev = stage1(0, "a", ("vector", 0), [])
    stage1(1, "b", dbc_ev, E.bw(E.snap()))
    for nm, src in (("u0", d_u[0]), ("dl0", d_dl[0]), ("zs0", d_zs[0])):
        if nm in taps:
            snap = E.snap()
            for dt in range(DIT):
                dma("sync", taps[nm][dt], src[dt], E.bw(snap))
    if "bc0" in taps:
        snap = E.snap()
        dma("sync", taps["bc0"], d_bc[0], E.bw(snap))
    r1.close()

    r2 = ExitStack(); sb2 = mk_sb(r2)
    bcb = sb2("bcb", [128, 8, TOK])
    ccb = sb2("ccb", [128, 8, TOK])
    dl_f = sb2("dl_f", [128, TOK])
    u_f = sb2("u_f", [128, TOK])
    du_b = sb2("du_b", [128, TOK])
    y_b = sb2("y_b", [128, TOK])
    y_old = sb2("y_old", [128, TOK])
    dA_b = sb2("dA_b", [128, TOK])
    dBu_b = sb2("dBu_b", [128, TOK])
    h_b = sb2("h_b", [128, TOK])
    binit_done = 0

    def stage2(pipe, bsnap):
        bw = E.bw(bsnap)
        last_y = 0
        last_ysp = 0
        for half in range(2):
            bcw = bw + ([("vector", last_y)] if last_y else [])
            bb = dma("sync", bcb[:],
                     d_bc[pipe][0, half * 8:(half + 1) * 8, :].unsqueeze(0)
                     .broadcast_to([128, 8, TOK]), bcw)
            cb = dma("sync", ccb[:],
                     d_bc[pipe][1, half * 8:(half + 1) * 8, :].unsqueeze(0)
                     .broadcast_to([128, 8, TOK]), bcw)
            for dt_i in range(DIT):
                anti = [("vector", last_y)]
                fd = dma("sync", dl_f[:], d_dl[pipe][dt_i], bw + anti)
                fu = dma("gpsimd", u_f[:], d_u[pipe][dt_i], bw + anti)
                vdu = E.op("vector", lambda e: e.tensor_tensor(
                    out=du_b[:], in0=dl_f[:], in1=u_f[:], op=ALU.mult),
                    [("sync", fd), ("gpsimd", fu)])
                if half == 0:
                    vy = E.op("vector", lambda e, dt_i=dt_i, p=pipe: e.tensor_scalar(
                        out=y_b[:], in0=u_f[:], scalar1=small[p]["dp"][:, dt_i, 0:1],
                        scalar2=None, op0=ALU.mult),
                        [("vector", vdu), ("gpsimd", last_ysp)])
                    fy = 0
                else:
                    fy = dma("gpsimd", y_old[:], d_y[pipe][dt_i], bw + anti)
                    vy = E.op("vector", lambda e: e.memset(y_b[:], 0.0),
                              [("vector", vdu), ("gpsimd", last_ysp)])
                sprev = vy
                for n in range(8):
                    ns = half * 8 + n
                    sA = E.op("scalar", lambda e, dt_i=dt_i,
                              ns=ns, p=pipe: e.activation(
                        out=dA_b[:], in_=dl_f[:], func=AF.Exp,
                        scale=small[p]["A"][:, dt_i, ns:ns + 1]),
                        [("sync", fd), ("vector", sprev)])
                    vB = E.op("vector", lambda e, n=n: e.tensor_tensor(
                        out=dBu_b[:], in0=du_b[:], in1=bcb[:, n, :], op=ALU.mult),
                        [("vector", vy), ("sync", cb), ("sync", bb)])
                    if pipe == 0:
                        vS = E.op("vector", lambda e: e.tensor_tensor_scan(
                            out=h_b[:], data0=dA_b[:], data1=dBu_b[:],
                            initial=0.0, op0=ALU.mult, op1=ALU.add),
                            [("scalar", sA), ("vector", vB)])
                    else:
                        vS = E.op("vector", lambda e, dt_i=dt_i,
                                  ns=ns: e.tensor_tensor_scan(
                            out=h_b[:], data0=dA_b[:], data1=dBu_b[:],
                            initial=binit[:, dt_i, ns:ns + 1], op0=ALU.mult,
                            op1=ALU.add),
                            [("scalar", sA), ("vector", vB), ("vector", binit_done)])
                    vT = E.op("vector", lambda e, n=n: e.tensor_tensor(
                        out=dBu_b[:], in0=h_b[:], in1=ccb[:, n, :], op=ALU.mult),
                        [("vector", vS)])
                    vYa = E.op("vector", lambda e: e.tensor_tensor(
                        out=y_b[:], in0=y_b[:], in1=dBu_b[:], op=ALU.add),
                        [("vector", vT)])
                    if pipe == 0:
                        vYa = E.op("vector", lambda e, dt_i=dt_i, ns=ns: e.tensor_copy(
                            out=states[:, dt_i, ns:ns + 1], in_=h_b[:, TOK - 1:TOK]),
                            [("vector", vYa)])
                    sprev = vYa
                if half == 1:
                    sprev = E.op("vector", lambda e: e.tensor_tensor(
                        out=y_b[:], in0=y_b[:], in1=y_old[:], op=ALU.add),
                        [("vector", sprev), ("gpsimd", fy)])
                last_y = sprev
                last_ysp = dma("gpsimd", d_y[pipe][dt_i], y_b[:], [("vector", sprev)])
        return last_ysp

    y_a_done = stage2(0, E.snap())
    st_dma = dma("sync", cc_st_in.ap().rearrange("(d p) n -> p d n", p=128),
                 states[:], [("gpsimd", y_a_done), ("vector", E.cnt["vector"])])
    ccm = E.cc(lambda e: e.collective_compute(
        "AllReduce", ALU.add, replica_groups=RG,
        ins=[cc_st_in.ap().opt()], outs=[cc_st_out.ap().opt()]), [("sync", st_dma)])
    st_back = dma("sync", binit[:], cc_st_out.ap().rearrange("(d p) n -> p d n", p=128),
                  [("cc", ccm)])
    binit_done = E.op("vector", lambda e: e.tensor_tensor(
        out=binit[:], in0=binit[:], in1=states[:], op=ALU.subtract),
        [("sync", st_back)])
    if "st" in taps:
        dma("sync", taps["st"].rearrange("(d p) n -> p d n", p=128), states[:],
            [("vector", binit_done)])
    if "bi" in taps:
        dma("sync", taps["bi"].rearrange("(d p) n -> p d n", p=128), binit[:],
            [("vector", binit_done)])
    stage2(1, E.snap())
    for nm, src in (("y0", d_y[0]), ("y1", d_y[1])):
        if nm in taps:
            snap = E.snap()
            for dt in range(DIT):
                dma("sync", taps[nm][dt], src[dt], E.bw(snap))
    r2.close()

    r3 = ExitStack(); sb3 = mk_sb(r3)
    s_acc = sb3("s_acc", [128, KT, TOK])
    r3a = ExitStack(); sb3a = mk_sb(r3a)
    yg = sb3a("yg", [128, DIT, 512])
    g_f = sb3a("g_f", [128, 2, 3, 512])
    wout_sl = sb3a("wout_sl", [128, 2, DIT, 128])
    wproj_sb = sb3a("wproj_sb", [128, KT, KT, 128])
    sdir = sb3a("sdir", [128, KT, 512])
    wsl_used = [0, 0]

    def stage3(pipe, pname, bsnap):
        bw = E.bw(bsnap)
        ldp = dma("sync", wproj_sb[:],
                  rw(pw[pname]["wprojT"], D).rearrange("p k (m j) -> p k m j", j=128),
                  bw)
        sdir_free = ("tensor", 0)
        yg_free = ("tensor", 0)
        gm = []
        for wi, (a, b) in enumerate(WIN_OWN):
            for dt_i in range(DIT):
                slot = dt_i % 2
                anti = [("vector", gm[-2] if len(gm) >= 2 else 0)]
                f1 = dma("sync", g_f[:, slot, 0, :], d_y[pipe][dt_i, :, a:b],
                         bw + anti, site="s3_f1")
                f3 = dma("gpsimd", g_f[:, slot, 2, :], d_zs[pipe][dt_i, :, a:b],
                         bw + anti, site="s3_f3")
                gm.append(E.op("vector", lambda e, slot=slot, dt_i=dt_i: e.tensor_tensor(
                    out=yg[:, dt_i, :], in0=g_f[:, slot, 0, :],
                    in1=g_f[:, slot, 2, :], op=ALU.mult),
                    [("sync", f1), ("gpsimd", f3), yg_free]))
            ev_last = 0
            for dm in range(KT):
                slot = dm % 2
                pslot = "A" if dm % 2 == 0 else "B"
                ldw = dma("gpsimd", wout_sl[:, slot, :, :],
                          rw(pw[pname]["woutT"], D)[:, :, dm * 128:(dm + 1) * 128],
                          bw + [("tensor", wsl_used[slot])], site="s3_ldw")
                mm = 0
                for dt_i in range(DIT):
                    wl = ([("gpsimd", ldw), ("vector", gm[-1]), psf[pslot],
                           sdir_free] if dt_i == 0 else [])
                    mm = E.op("tensor", lambda e, slot=slot, dt_i=dt_i, pslot=pslot,
                              st=(dt_i == 0), sp=(dt_i == DIT - 1): e.matmul(
                        out=ps[pslot][:, 0:512], lhsT=wout_sl[:, slot, dt_i, :],
                        rhs=yg[:, dt_i, :], start=st, stop=sp,
                        skip_group_check=True), wl)
                wsl_used[slot] = mm
                ev_last = E.op("scalar", lambda e, dm=dm, pslot=pslot: e.activation(
                    out=sdir[:, dm, :], in_=ps[pslot][:, 0:512], func=AF.Copy),
                    [("tensor", mm), sdir_free])
                psf[pslot] = ("scalar", ev_last)
            yg_free = ("tensor", wsl_used[0] if wsl_used[0] > wsl_used[1]
                       else wsl_used[1])
            for dmp in range(KT):
                mm = 0
                for dm in range(KT):
                    wl = ([("scalar", ev_last), ("sync", ldp), psf["C"]]
                          if dm == 0 else [])
                    mm = E.op("tensor", lambda e, dmp=dmp, dm=dm, st=(dm == 0),
                              sp=(dm == KT - 1): e.matmul(
                        out=ps["C"][:, 0:512], lhsT=wproj_sb[:, dm, dmp, :],
                        rhs=sdir[:, dm, :], start=st, stop=sp,
                        skip_group_check=True), wl)
                sdir_free = ("tensor", mm)
                if pipe == 0:
                    evc = E.op("scalar", lambda e, dmp=dmp, a=a: e.activation(
                        out=s_acc[:, dmp, a:a + 512], in_=ps["C"][:, 0:512],
                        func=AF.Identity, bias=bproj_sb[:, dmp, 0:1]), [("tensor", mm)])
                    psf["C"] = ("scalar", evc)
                else:
                    evc = E.op("vector", lambda e, dmp=dmp, a=a, b=b: e.tensor_tensor(
                        out=s_acc[:, dmp, 2048 - b:2048 - a],
                        in0=ps["C"][:, 511::-1],
                        in1=s_acc[:, dmp, 2048 - b:2048 - a], op=ALU.add),
                        [("tensor", mm)])
                    psf["C"] = ("vector", evc)

    stage3(0, "a", E.snap())
    stage3(1, "b", E.snap())
    if "s_acc" in taps:
        snap = E.snap()
        for k in range(KT):
            dma("sync", taps["s_acc"][k * 128:(k + 1) * 128, :], s_acc[:, k, :],
                E.bw(snap))
    r3a.close()

    s3_done = E.snap()
    bw4 = E.bw(s3_done)
    r3b = ExitStack(); sb3b = mk_sb(r3b)
    h1 = sb3b("h1", [128, 32, 512])
    xnw = sb3b("xnw", [128, KT, 512])
    xres = sb3b("xres", [128, 512])
    w1_sl = sb3b("w1_sl", [128, 2, KT, 128])
    w2_sl = sb3b("w2_sl", [128, 32, 128])
    b1_sb = sb3b("b1_sb", [128, 32, 1])
    b2_sb = sb3b("b2_sb", [128, KT, 1])
    rows4 = tuple(sb3b(f"rows4_{i}", [1, XR]) for i in range(2))
    rows1 = (rows4[0], rows4[1], sb3b("rows1_t", [1, TOK]))
    dma("sync", b1_sb[:], b1.unsqueeze(2), bw4)
    dma("sync", b2_sb[:], b2.unsqueeze(2), bw4)
    dma("sync", rows4[0][0:1, :], d_rows[0:1, :], bw4)
    dma("sync", rows4[1][0:1, :], d_rows[1:2, :], bw4)
    bload = E.cnt["sync"]
    w1_used = [0, 0]
    w2_used = 0
    x1_last = 0
    h1_free = ("tensor", 0)
    for wi, (a, b) in enumerate(WIN_OWN):
        fx = dma("sync", xnw[:], rw(xf_in, XR)[:, :, 3 + a:3 + b],
                 bw4 + [("sync*", bload), ("vector", x1_last)])
        br = bcast2(rows4, 3 + a, 512, bw4 + [("sync*", bload)])
        xnm = 0
        for k in range(KT):
            v1 = E.op("vector", lambda e, k=k: e.tensor_tensor(
                out=xnw[:, k, :], in0=xnw[:, k, :], in1=ps_big[:, 0:512],
                op=ALU.subtract), [("tensor", br), ("sync", fx),
                                   ("vector", x1_last)])
            xnm = E.op("vector", lambda e, k=k: e.tensor_tensor(
                out=xnw[:, k, :], in0=xnw[:, k, :], in1=ps_big[:, 512:1024],
                op=ALU.mult), [("vector", v1)])
        psf["big"] = ("vector", xnm)
        for it in range(32):
            slot = it % 2
            pslot = "A" if it % 2 == 0 else "B"
            ldw = dma("gpsimd", w1_sl[:, slot, :, :],
                      rw(w1, 4 * D)[:, :, it * 128:(it + 1) * 128],
                      bw4 + [("tensor", w1_used[slot])])
            mm = 0
            for k in range(KT):
                wl = ([("gpsimd", ldw), ("vector", xnm), psf[pslot], h1_free]
                      if k == 0 else [])
                mm = E.op("tensor", lambda e, slot=slot, k=k, pslot=pslot,
                          st=(k == 0), sp=(k == KT - 1): e.matmul(
                    out=ps[pslot][:, 0:512], lhsT=w1_sl[:, slot, k, :],
                    rhs=xnw[:, k, :], start=st, stop=sp, skip_group_check=True), wl)
            w1_used[slot] = mm
            sev = E.op("scalar", lambda e, it=it, pslot=pslot: e.activation(
                out=h1[:, it, :], in_=ps[pslot][:, 0:512], func=AF.Relu,
                bias=b1_sb[:, it, 0:1]), [("tensor", mm)])
            psf[pslot] = ("scalar", sev)
        sm_h1 = E.cnt["scalar"]
        for ot in range(KT):
            ldw = dma("gpsimd", w2_sl[:],
                      rw(w2, D)[:, :, ot * 128:(ot + 1) * 128],
                      bw4 + [("tensor", w2_used)])
            mm = 0
            for it in range(32):
                wl = ([("gpsimd", ldw), ("scalar", sm_h1), psf["C"]]
                      if it == 0 else [])
                mm = E.op("tensor", lambda e, it=it, st=(it == 0),
                          sp=(it == 31): e.matmul(
                    out=ps["C"][:, 0:512], lhsT=w2_sl[:, it, :],
                    rhs=h1[:, it, :], start=st, stop=sp, skip_group_check=True), wl)
            w2_used = mm
            h1_free = ("tensor", mm)
            sg = E.op("scalar", lambda e, ot=ot: e.activation(
                out=xnw[:, ot, :], in_=ps["C"][:, 0:512], func=AF.Silu,
                bias=b2_sb[:, ot, 0:1]), [("tensor", mm)])
            psf["C"] = ("scalar", sg)
            v1 = E.op("vector", lambda e, ot=ot, a=a: e.tensor_tensor(
                out=xnw[:, ot, :], in0=xnw[:, ot, :], in1=s_acc[:, ot, a:a + 512],
                op=ALU.mult), [("scalar", sg)])
            fxr = dma("sync", xres[:],
                      rw(xf_in, XR)[:, ot, 3 + a:3 + b].unsqueeze(1),
                      [("vector", x1_last)])
            v2 = E.op("vector", lambda e, ot=ot, a=a: e.tensor_tensor(
                out=s_acc[:, ot, a:a + 512], in0=xres[:].unsqueeze(1),
                in1=xnw[:, ot, :].unsqueeze(1), op=ALU.add),
                [("vector", v1), ("sync", fxr)])
            x1_last = v2
            dma("gpsimd", d_x1[ot * 128:(ot + 1) * 128, a:a + 512],
                s_acc[:, ot, a:a + 512], [("vector", v2)])
    x1 = s_acc
    if "x1" in taps:
        snap = E.snap()
        for k in range(KT):
            dma("sync", taps["x1"][k * 128:(k + 1) * 128, :], x1[:, k, :], E.bw(snap))

    p4_done = E.snap()
    ln1_done = ln_stats(x1, TOK, rows1, E.bw(p4_done))
    xn1_done = 0
    for (a, b) in WIN_OWN:
        br = bcast2(rows1, a, 512, [("scalar", ln1_done)])
        for k in range(KT):
            v1 = E.op("vector", lambda e, k=k, a=a, b=b: e.tensor_tensor(
                out=x1[:, k, a:b], in0=x1[:, k, a:b], in1=ps_big[:, 0:512],
                op=ALU.subtract), [("tensor", br)])
            xn1_done = E.op("vector", lambda e, k=k, a=a, b=b: e.tensor_tensor(
                out=x1[:, k, a:b], in0=x1[:, k, a:b], in1=ps_big[:, 512:1024],
                op=ALU.mult), [("vector", v1)])
        psf["big"] = ("vector", xn1_done)
    for k in range(KT):
        dma("sync", d_xn1[k * 128:(k + 1) * 128, :], x1[:, k, :],
            [("vector", xn1_done)])
    if "xn1" in taps:
        snap = E.snap()
        dma("sync", taps["xn1"], d_xn1, E.bw(snap))
    r3b.close()
    r3.close()

    p4x_done = E.snap()
    bw5 = E.bw(p4x_done)
    r4 = ExitStack(); sb4 = mk_sb(r4)
    xn1_sb = sb4("xn1_sb", [128, KT, TOK])
    whold = sb4("whold", [128, KT, D])
    kv_t = sb4("kv_t", [128, 512])
    bq_sb = sb4("bq_sb", [128, KT, 1])
    bk_sb = sb4("bk_sb", [128, KT, 1])
    bv_b = sb4("bv_b", [128, D])
    xn1_ld = dma("sync", xn1_sb[:], rw(d_xn1, TOK), bw5)
    dma("sync", bq_sb[:], bq.unsqueeze(2), bw5)
    dma("sync", bk_sb[:], bk.unsqueeze(2), bw5)
    bvb = dma("sync", bv_b[:], bv_row.broadcast_to([128, D]), bw5)
    bld = E.cnt["sync"]
    vq = E.op("vector", lambda e: e.tensor_scalar(
        out=bq_sb[:], in0=bq_sb[:], scalar1=1.0 / 32.0, scalar2=None, op0=ALU.mult),
        [("sync*", bld)])
    whold_free = ("tensor", 0)
    kvt_free = ("gpsimd", 0)

    def proj_fm(wdram, dest, bias_sb, scale):
        nonlocal whold_free, kvt_free
        ld = dma("sync", whold[:], rw(wdram, D), bw5 + [whold_free])
        last_sp = 0
        for m in range(KT):
            for (a, b) in WIN_OWN:
                pslot = "A" if (m + a // 512) % 2 == 0 else "B"
                mm = 0
                for k in range(KT):
                    wl = ([("sync", ld), ("sync", xn1_ld), psf[pslot]]
                          if k == 0 else [])
                    mm = E.op("tensor", lambda e, m=m, k=k, a=a, b=b, pslot=pslot,
                              st=(k == 0), sp=(k == KT - 1): e.matmul(
                        out=ps[pslot][:, 0:512],
                        lhsT=whold[:, k, m * 128:(m + 1) * 128],
                        rhs=xn1_sb[:, k, a:b], start=st, stop=sp,
                        skip_group_check=True), wl)
                ev = E.op("scalar", lambda e, m=m, scale=scale, pslot=pslot: e.activation(
                    out=kv_t[:], in_=ps[pslot][:, 0:512], func=AF.Identity,
                    bias=bias_sb[:, m, 0:1], scale=scale),
                    [("tensor", mm), kvt_free, ("vector", vq)])
                psf[pslot] = ("scalar", ev)
                last_sp = dma("gpsimd", dest[m * 128:(m + 1) * 128, a:b], kv_t[:],
                              [("scalar", ev)])
                kvt_free = ("gpsimd", last_sp)
        whold_free = ("tensor", E.cnt["tensor"])
        return last_sp

    proj_fm(wqT, d_q, bq_sb, 1.0 / 32.0)
    kd = proj_fm(wkT, cc_k_in.ap(), bk_sb, 1.0)
    ldv = dma("sync", whold[:], rw(wvT, D), bw5 + [whold_free])
    vsp = 0
    for tt in range(16):
        for dvw in range(2):
            pslot = "C" if (tt + dvw) % 2 == 0 else "D"
            mm = 0
            for k in range(KT):
                wl = ([("sync", ldv), ("sync", xn1_ld), psf[pslot]]
                      if k == 0 else [])
                mm = E.op("tensor", lambda e, tt=tt, dvw=dvw, k=k, pslot=pslot,
                          st=(k == 0), sp=(k == KT - 1): e.matmul(
                    out=ps[pslot][:, 0:512],
                    lhsT=xn1_sb[:, k, tt * 128:(tt + 1) * 128],
                    rhs=whold[:, k, dvw * 512:(dvw + 1) * 512],
                    start=st, stop=sp, skip_group_check=True), wl)
            ev = E.op("vector", lambda e, dvw=dvw, pslot=pslot: e.tensor_tensor(
                out=kv_t[:], in0=ps[pslot][:, 0:512],
                in1=bv_b[:, dvw * 512:(dvw + 1) * 512], op=ALU.add),
                [("tensor", mm), ("sync", bvb), kvt_free])
            psf[pslot] = ("vector", ev)
            vsp = dma("gpsimd", cc_vin.ap()[tt * 128:(tt + 1) * 128,
                                             dvw * 512:(dvw + 1) * 512],
                      kv_t[:], [("vector", ev)])
            kvt_free = ("gpsimd", vsp)
    whold_free = ("tensor", E.cnt["tensor"])
    cc_k = E.cc(lambda e: e.collective_compute(
        "AllGather", ALU.bypass, replica_groups=RG,
        ins=[cc_k_in.ap().opt()], outs=[cc_k_out.ap().opt()]),
        [("gpsimd", vsp), ("gpsimd", kd)])
    cc_v = E.cc(lambda e: e.collective_compute(
        "AllGather", ALU.bypass, replica_groups=RG,
        ins=[cc_vin.ap().opt()], outs=[cc_vout.ap().opt()]), [("cc", cc_k)])
    if "q" in taps:
        snap = E.snap()
        dma("sync", taps["q"], d_q, E.bw(snap))
    r4.close()

    p5_done = E.snap()
    bw6 = [(s, v) for (s, v) in E.bw(p5_done) if s != "cc"]
    r5 = ExitStack(); sb5 = mk_sb(r5)
    hq = sb5("hq", [128, KT, TOK])
    tq = sb5("tq", [128, KT, TOK])
    rowsq_mr = tuple(sb5(f"rowsq_{i}", [1, TOK]) for i in range(2))
    wdeq = sb5("wdeq", [128, KT, D])
    qraw = sb5("qraw", [128, 512], I32)
    sraw = sb5("sraw", [128, 512])
    ar_sb = sb5("ar_sb", [32, TOK])
    la_sb = sb5("la_sb", [128, KT, 32])
    lb_sb = sb5("lb_sb", [32, D])
    qb_sb = sb5("qb_sb", [128, KT, 1])
    lgt_g = sb5("lgt_g", [128, KT, 1])
    lgt_b = sb5("lgt_b", [128, KT, 1])
    h_ready = dma("sync", hq[:], rw(d_xn1, TOK), bw6)
    wdeq_free = ("tensor", 0)

    def qlora_linear(idx, src, dst, add_residual, waits0):
        nonlocal wdeq_free
        dq_last = 0
        for dt_i in range(KT):
            for jh in range(2):
                anti = [("vector", dq_last), wdeq_free]
                l1 = dma("sync", qraw[:],
                         qTw[idx, dt_i * 128:(dt_i + 1) * 128,
                             jh * 512:(jh + 1) * 512], list(waits0) + anti)
                l2 = dma("sync", sraw[:],
                         scrT[idx, dt_i * 128:(dt_i + 1) * 128,
                              jh * 512:(jh + 1) * 512], list(waits0))
                s1 = E.op("scalar", lambda e, dt_i=dt_i, jh=jh: e.activation(
                    out=wdeq[:, dt_i, jh * 512:(jh + 1) * 512], in_=qraw[:],
                    func=AF.Copy, scale=1.0 / 7.5, bias=-1.0), [("sync", l2)])
                dq_last = E.op("vector", lambda e, dt_i=dt_i, jh=jh: e.tensor_tensor(
                    out=wdeq[:, dt_i, jh * 512:(jh + 1) * 512],
                    in0=wdeq[:, dt_i, jh * 512:(jh + 1) * 512], in1=sraw[:],
                    op=ALU.mult), [("scalar", s1)])
        lb_ld = dma("sync", lb_sb[:], lbT[idx], list(waits0))
        la_ld = dma("sync", la_sb[:], laT[idx].rearrange("(k p) r -> p k r", p=128),
                    list(waits0))
        qb_ld = dma("sync", qb_sb[:], qb[idx].unsqueeze(2), list(waits0))
        ar_done = 0
        for (a, b) in WIN_OWN:
            mm = 0
            for k in range(KT):
                wl = ([("sync", la_ld), psf["D"]] + list(waits0)) if k == 0 else []
                mm = E.op("tensor", lambda e, k=k, a=a, b=b, st=(k == 0),
                          sp=(k == KT - 1): e.matmul(
                    out=ps["D"][0:32, 0:512], lhsT=la_sb[:, k, :], rhs=src[:, k, a:b],
                    start=st, stop=sp, skip_group_check=True), wl)
            ar_done = E.op("vector", lambda e, a=a: e.tensor_copy(
                out=ar_sb[:, a:a + 512], in_=ps["D"][0:32, 0:512]), [("tensor", mm)])
            psf["D"] = ("vector", ar_done)
        out_last = 0
        for m in range(KT):
            for (a, b) in WIN_OWN:
                pslot = "A" if (m + a // 512) % 2 == 0 else "B"
                mm = 0
                for k in range(KT):
                    wl = ([("vector", dq_last), psf[pslot]] + list(waits0)
                          if k == 0 else [])
                    mm = E.op("tensor", lambda e, m=m, k=k, a=a, b=b, pslot=pslot,
                              st=(k == 0): e.matmul(
                        out=ps[pslot][:, 0:512],
                        lhsT=wdeq[:, k, m * 128:(m + 1) * 128], rhs=src[:, k, a:b],
                        start=st, stop=False, skip_group_check=True), wl)
                mm = E.op("tensor", lambda e, m=m, a=a, b=b, pslot=pslot: e.matmul(
                    out=ps[pslot][:, 0:512], lhsT=lb_sb[:, m * 128:(m + 1) * 128],
                    rhs=ar_sb[:, a:b], start=False, stop=True, skip_group_check=True),
                    [("sync", lb_ld), ("vector", ar_done)])
                if add_residual:
                    sev = E.op("scalar", lambda e, m=m, pslot=pslot: e.activation(
                        out=sraw[:], in_=ps[pslot][:, 0:512], func=AF.Identity,
                        bias=qb_sb[:, m, 0:1]),
                        [("tensor", mm), ("sync", qb_ld),
                         ("vector", out_last if isinstance(out_last, int) else 0)])
                    psf[pslot] = ("scalar", sev)
                    out_last = E.op("vector", lambda e, m=m, a=a: e.tensor_tensor(
                        out=dst[:, m, a:a + 512], in0=sraw[:],
                        in1=hq[:, m, a:a + 512], op=ALU.add), [("scalar", sev)])
                else:
                    sev = E.op("scalar", lambda e, m=m, a=a, pslot=pslot: e.activation(
                        out=tq[:, m, a:a + 512], in_=ps[pslot][:, 0:512],
                        func=AF.Identity,
                        bias=qb_sb[:, m, 0:1]), [("tensor", mm), ("sync", qb_ld)])
                    psf[pslot] = ("scalar", sev)
                    out_last = sev
        wdeq_free = ("tensor", E.cnt["tensor"])
        return out_last

    prevw = [("sync", h_ready)]
    for blk in range(3):
        m1 = qlora_linear(2 * blk, hq, tq, False,
                          prevw + (bw6 if blk == 0 else []))
        if blk == 0 and "ql0" in taps:
            snap = E.snap()
            for k in range(KT):
                dma("sync", taps["ql0"][k * 128:(k + 1) * 128, :], tq[:, k, :],
                    E.bw(snap))
        lb_ld2 = dma("sync", lgt_b[:], lnb[blk].unsqueeze(2), bw6)
        lg_ld = dma("sync", lgt_g[:], lng[blk].unsqueeze(2), bw6)
        lnm = ln_stats(tq, TOK, (rowsq_mr[0], rowsq_mr[1], ar_sb), [("scalar", m1)])
        gm_last = 0
        for (a, b) in WIN_OWN:
            br = bcast2(rowsq_mr, a, 512, [("scalar", lnm)])
            for k in range(KT):
                v1 = E.op("vector", lambda e, k=k, a=a, b=b: e.tensor_tensor(
                    out=tq[:, k, a:b], in0=tq[:, k, a:b], in1=ps_big[:, 0:512],
                    op=ALU.subtract), [("tensor", br)])
                v2 = E.op("vector", lambda e, k=k, a=a, b=b: e.tensor_tensor(
                    out=tq[:, k, a:b], in0=tq[:, k, a:b], in1=ps_big[:, 512:1024],
                    op=ALU.mult), [("vector", v1)])
                v3 = E.op("vector", lambda e, k=k, a=a, b=b: e.tensor_scalar(
                    out=tq[:, k, a:b], in0=tq[:, k, a:b], scalar1=lgt_g[:, k, 0:1],
                    scalar2=lgt_b[:, k, 0:1], op0=ALU.mult, op1=ALU.add),
                    [("vector", v2), ("sync", lg_ld), ("sync", lb_ld2)])
                gm_last = E.op("scalar", lambda e, k=k, a=a, b=b: e.activation(
                    out=tq[:, k, a:b], in_=tq[:, k, a:b], func=AF.Gelu),
                    [("vector", v3)])
            psf["big"] = ("scalar", gm_last)
        if blk == 0 and "qln0" in taps:
            snap = E.snap()
            for k in range(KT):
                dma("sync", taps["qln0"][k * 128:(k + 1) * 128, :], tq[:, k, :],
                    E.bw(snap))
        ql2 = qlora_linear(2 * blk + 1, tq, hq, True, [("scalar", gm_last)])
        prevw = [("vector", ql2)]
        if blk == 0 and "qh0" in taps:
            snap = E.snap()
            for k in range(KT):
                dma("sync", taps["qh0"][k * 128:(k + 1) * 128, :], hq[:, k, :],
                    E.bw(snap))
    g1_done = 0
    for k in range(KT):
        g1_done = E.op("scalar", lambda e, k=k: e.activation(
            out=hq[:, k, :], in_=hq[:, k, :], func=AF.Silu),
            [("vector", E.cnt["vector"])])
        dma("sync", d_g[k * 128:(k + 1) * 128, :], hq[:, k, :],
            [("scalar", g1_done)])
    if "g1" in taps:
        snap = E.snap()
        dma("sync", taps["g1"], d_g, E.bw(snap))
    if "kf" in taps:
        snap = E.snap()
        dma("sync", taps["kf"], cc_k_out.ap(), E.bw(snap) + [("cc", E.cnt["cc"])])
    if "vf" in taps:
        snap = E.snap()
        dma("sync", taps["vf"], cc_vout.ap(), E.bw(snap) + [("cc", E.cnt["cc"])])
    r5.close()

    p6_done = E.snap()
    bw7 = E.bw(p6_done) + [("cc", E.cnt["cc"])]
    r6 = ExitStack(); sb6 = mk_sb(r6)
    kslab = sb6("kslab", [128, KT, 512])
    vslab = sb6("vslab", [128, 4, D])
    qtile = sb6("qtile", [128, KT, 128])
    wsm = sb6("wsm", [128, 512])
    wTs = sb6("wTs", [128, 4, TOK])
    mx4 = sb6("mx4", [128, 4])
    den4 = sb6("den4", [128, 4])
    nrm = sb6("nrm", [128, 16, 32])
    nrt = sb6("nrt", [128, 16])
    nr_row = sb6("nr_row", [16, 128])
    at_acc = sb6("at_acc", [128, KT, TOK])
    sattn = sb6("sattn", [128, KT, 512])
    xf_t2 = sb6("xf_t2", [128, KT, 512])
    g1w = sb6("g1w", [128, KT, 512])
    rnb = sb6("rnb", [128, 512])
    wo_sl = sb6("wo_sl", [128, KT, 128])
    bo_sb = sb6("bo_sb", [128, KT, 1])
    ident = sb6("ident", [128, 128])
    bo_ld = dma("sync", bo_sb[:], bo.unsqueeze(2), bw7)
    id_ld = dma("sync", ident[:], ident_in, bw7)
    acc_zero = 0
    for k in range(KT):
        acc_zero = E.op("vector", lambda e, k=k: e.memset(at_acc[:, k, :], 0.0),
                        bw7 if k == 0 else [])
    acc_zero = E.op("vector", lambda e: e.memset(nrm[:], 0.0))
    wsm_free = ("tensor", 0)
    wts_free = ("tensor", 0)
    psbig_free = psf["big"]
    qt_free = ("tensor", 0)

    for sl in range(8):
        half, q4 = sl // 4, sl % 4
        lk = dma("sync", kslab[:],
                 cc_k_out.ap()[half].rearrange("(k p) t -> p k t", p=128)
                 [:, :, q4 * 512:(q4 + 1) * 512],
                 bw7 + [("tensor", E.cnt["tensor"])])
        lv = dma("sync", vslab[:],
                 cc_vout.ap()[half, q4 * 512:(q4 + 1) * 512, :]
                 .rearrange("(s p) d -> p s d", p=128),
                 bw7 + [("tensor", E.cnt["tensor"])])
        wt_last = 0
        for qt_i in range(16):
            lq = dma("gpsimd", qtile[:],
                     d_q.rearrange("(k p) t -> p k t", p=128)
                     [:, :, qt_i * 128:(qt_i + 1) * 128], bw7 + [qt_free])
            mm = 0
            for k in range(KT):
                wl = ([("sync", lk), ("gpsimd", lq), psf["A"],
                       ("vector", acc_zero)] if k == 0 else [])
                mm = E.op("tensor", lambda e, k=k, st=(k == 0),
                          sp=(k == KT - 1): e.matmul(
                    out=ps["A"][:, 0:512], lhsT=qtile[:, k, :], rhs=kslab[:, k, :],
                    start=st, stop=sp, skip_group_check=True), wl)
            qt_free = ("tensor", mm)
            r1m = E.op("vector", lambda e: e.tensor_reduce(
                out=mx4[:], in_=ps["A"][:, 0:512].rearrange("p (g j) -> p g j", j=128),
                axis=AX.X, op=ALU.max), [("tensor", mm)])
            r2m = E.op("vector", lambda e: e.tensor_tensor(
                out=wsm[:].rearrange("p (g j) -> p g j", j=128),
                in0=ps["A"][:, 0:512].rearrange("p (g j) -> p g j", j=128),
                in1=mx4[:].unsqueeze(2).broadcast_to([128, 4, 128]),
                op=ALU.subtract),
                [("vector", r1m), wsm_free])
            psf["A"] = ("vector", r2m)
            se = 0
            for g in range(4):
                se = E.op("scalar", lambda e, g=g: e.activation(
                    out=wsm[:, g * 128:(g + 1) * 128],
                    in_=wsm[:, g * 128:(g + 1) * 128], func=AF.Exp,
                    accum_out=den4[:, g:g + 1]), [("vector", r2m)])
            rc = E.op("vector", lambda e: e.reciprocal(out=mx4[:], in_=den4[:]),
                      [("scalar", se)])
            wn = E.op("vector", lambda e: e.tensor_tensor(
                out=wsm[:].rearrange("p (g j) -> p g j", j=128),
                in0=wsm[:].rearrange("p (g j) -> p g j", j=128),
                in1=mx4[:].unsqueeze(2).broadcast_to([128, 4, 128]),
                op=ALU.mult), [("vector", rc)])
            E.op("vector", lambda e, qt_i=qt_i, sl=sl: e.tensor_tensor(
                out=nrm[:, qt_i, sl * 4:(sl + 1) * 4], in0=den4[:], in1=mx4[:],
                op=ALU.mult), [("vector", wn)])
            tp = 0
            for g in range(4):
                wl = [("vector", wn)] + ([psbig_free, ("vector", wt_last),
                                          ("sync", id_ld)] if g == 0 else [])
                tp = E.op("tensor", lambda e, g=g: e.transpose(
                    out=ps_big[:, g * 128:(g + 1) * 128],
                    in_=wsm[:, g * 128:(g + 1) * 128], identity=ident[:]), wl)
            wsm_free = ("tensor", tp)
            wt_last = E.op("vector", lambda e, qt_i=qt_i: e.tensor_copy(
                out=wTs[:, 0:4, qt_i * 128:(qt_i + 1) * 128],
                in_=ps_big[:, 0:512].rearrange("p (g j) -> p g j", j=128)),
                [("tensor", tp), wts_free])
            psbig_free = ("vector", wt_last)
        for dv in range(KT):
            for (a, b) in WIN_OWN:
                mm = 0
                for g in range(4):
                    wl = ([("sync", lv), ("vector", wt_last), psf["C"]]
                          if g == 0 else [])
                    mm = E.op("tensor", lambda e, dv=dv, g=g, a=a, b=b,
                              st=(g == 0), sp=(g == 3): e.matmul(
                        out=ps["C"][:, 0:512],
                        lhsT=vslab[:, g, dv * 128:(dv + 1) * 128],
                        rhs=wTs[:, g, a:b], start=st, stop=sp,
                        skip_group_check=True), wl)
                avm = E.op("vector", lambda e, dv=dv, a=a: e.tensor_tensor(
                    out=at_acc[:, dv, a:a + 512], in0=ps["C"][:, 0:512],
                    in1=at_acc[:, dv, a:a + 512], op=ALU.add), [("tensor", mm)])
                psf["C"] = ("vector", avm)
        wts_free = ("tensor", E.cnt["tensor"])

    nr1 = E.op("vector", lambda e: e.tensor_reduce(
        out=nrt[:], in_=nrm[:], axis=AX.X, op=ALU.add),
        [("vector", E.cnt["vector"])])
    nr2 = E.op("vector", lambda e: e.tensor_scalar(
        out=nrt[:], in0=nrt[:], scalar1=1e-6, scalar2=None, op0=ALU.add),
        [("vector", nr1)])
    nr3 = E.op("vector", lambda e: e.reciprocal(out=nrt[:], in_=nrt[:]),
               [("vector", nr2)])
    ntp = E.op("tensor", lambda e: e.transpose(
        out=ps["D"][0:16, 0:128], in_=nrt[:], identity=ident[:]),
        [("vector", nr3), psf["D"], ("sync", id_ld)])
    psf["D"] = ("tensor", ntp)
    nrow = E.op("vector", lambda e: e.tensor_copy(out=nr_row[:], in_=ps["D"][0:16, 0:128]),
                [("tensor", ntp)])
    psf["D"] = ("vector", nrow)
    nb = dma("sync", d_nrm.rearrange("a (q j) -> (a q) j", j=128), nr_row[:],
             [("vector", nrow)])
    if "attn" in taps:
        snap = E.snap()
        for k in range(KT):
            dma("sync", taps["attn"][k * 128:(k + 1) * 128, :], at_acc[:, k, :],
                E.bw(snap))

    wo_used = [0, 0]
    out_last = 0
    sc_last = 0
    for wi, (a, b) in enumerate(WIN_OWN):
        rb = dma("sync", rnb[:], d_nrm[0:1, a:b].broadcast_to([128, 512]),
                 [("sync", nb), ("vector", sc_last)])
        sc = 0
        for dv in range(KT):
            sc = E.op("vector", lambda e, dv=dv, a=a: e.tensor_tensor(
                out=at_acc[:, dv, a:a + 512], in0=at_acc[:, dv, a:a + 512],
                in1=rnb[:], op=ALU.mult), [("sync", rb)])
        sc_last = sc
        fx1 = dma("gpsimd", xf_t2[:],
                  d_x1.rearrange("(k p) t -> p k t", p=128)[:, :, a:b],
                  [("vector", out_last)])
        fg1 = dma("gpsimd", g1w[:],
                  d_g.rearrange("(k p) t -> p k t", p=128)[:, :, a:b],
                  [("vector", out_last)])
        for m in range(KT):
            pslot = "A" if m % 2 == 0 else "B"
            ldw = dma("gpsimd", wo_sl[:],
                      rw(woT, D)[:, :, m * 128:(m + 1) * 128],
                      [("tensor", wo_used[0])])
            mm = 0
            for k in range(KT):
                wl = ([("gpsimd", ldw), ("vector", sc), psf[pslot]]
                      if k == 0 else [])
                mm = E.op("tensor", lambda e, m=m, k=k, a=a, pslot=pslot,
                          st=(k == 0), sp=(k == KT - 1): e.matmul(
                    out=ps[pslot][:, 0:512],
                    lhsT=wo_sl[:, k, :],
                    rhs=at_acc[:, k, a:a + 512], start=st, stop=sp,
                    skip_group_check=True), wl)
            wo_used[0] = mm
            sev = E.op("scalar", lambda e, m=m, pslot=pslot: e.activation(
                out=sattn[:, m, :], in_=ps[pslot][:, 0:512], func=AF.Identity,
                bias=bo_sb[:, m, 0:1]), [("tensor", mm), ("sync", bo_ld)])
            psf[pslot] = ("scalar", sev)
            v1 = E.op("vector", lambda e, m=m: e.tensor_tensor(
                out=sattn[:, m, :], in0=sattn[:, m, :], in1=g1w[:, m, :],
                op=ALU.mult), [("scalar", sev), ("gpsimd", fg1)])
            v2 = E.op("vector", lambda e, m=m: e.tensor_tensor(
                out=sattn[:, m, :], in0=sattn[:, m, :], in1=xf_t2[:, m, :],
                op=ALU.add), [("vector", v1), ("gpsimd", fx1)])
            out_last = v2
            dma("sync", out_fm[m * 128:(m + 1) * 128, a:b], sattn[:, m, :],
                [("vector", v2)])
    if "sattn" in taps:
        snap = E.snap()
        dma("sync", taps["sattn"], out_fm, E.bw(snap))
    r6.close()

    fin = E.snap()
    for e in ENGINES:
        E.q[e].append((None, [("sync*", fin["sync"]), ("gpsimd*", fin["gpsimd"])],
                       None, None))

    with nc.Block() as block:
        def runner(name):
            def go(eng):
                for (fn, waits, inc_sem, inc_val) in E.q[name]:
                    resolved = {}
                    def addw(h, v):
                        if v > resolved.get(h, 0):
                            resolved[h] = v
                    for (s, v) in waits:
                        if s in ("sync", "gpsimd"):
                            j, c, _ = E.qs[s]["map"][v]
                            addw((s, j), 16 * c)
                        elif s in ("sync*", "gpsimd*"):
                            qn = s[:-1]
                            if v > 0:
                                vec = E.qs[qn]["map"][v][2]
                                for j in range(NQS):
                                    if vec[j] > 0:
                                        addw((qn, j), 16 * vec[j])
                        else:
                            addw(s, v)
                    for h, v in resolved.items():
                        if isinstance(h, tuple):
                            eng.wait_ge(qsems[h[0]][h[1]], v)
                        else:
                            eng.wait_ge(sem[h], v)
                    if fn is None:
                        continue
                    if inc_sem in ("sync", "gpsimd"):
                        j, c, _ = E.qs[inc_sem]["map"][inc_val]
                        if c > 1:
                            eng.wait_ge(qsems[inc_sem][j], 16 * (c - 1))
                        ins = fn(eng)
                        ins.then_inc(qsems[inc_sem][j], 16)
                    else:
                        ins = fn(eng)
                        ins.then_inc(sem[inc_sem], 1)
            return go
        block.tensor(runner("tensor"))
        block.vector(runner("vector"))
        block.scalar(runner("scalar"))
        block.gpsimd(runner("gpsimd"))
        block.sync(runner("sync"))

    es.close()
    return nc


def _host_inputs(inputs):
    x = np.asarray(inputs["L0_attn_x"])
    cores = []
    T = lambda a: np.ascontiguousarray(a, dtype=np.float32)

    def pcol(arr, cols):
        a = np.asarray(arr).reshape(-1, 128, cols)
        return np.ascontiguousarray(
            a.transpose(1, 0, 2).reshape(128, -1), dtype=np.float32)

    for c in range(8):
        b, h = c // 2, c % 2
        xb = x[b]
        if h == 0:
            own = xb[0:2048]
            ah = np.zeros((3, D), np.float32)
            bh = xb[2048:2051][::-1]
            A_sfx, B_sfx = "f", "b"
        else:
            own = xb[2048:4096][::-1]
            ah = np.zeros((3, D), np.float32)
            bh = xb[2045:2048]
            A_sfx, B_sfx = "b", "f"
        xall = np.concatenate([ah, own, bh], axis=0)
        d = {"xf": T(xall.T)}
        for pn, sfx in (("a", A_sfx), ("b", B_sfx)):
            d[pn + "_winT"] = T(np.asarray(inputs[f"L0_attn_win_{sfx}"]).T)
            d[pn + "_wconv"] = pcol(np.asarray(inputs[f"L0_attn_wconv_{sfx}"]).reshape(DI, 4), 4)
            d[pn + "_bconv"] = pcol(inputs[f"L0_attn_bconv_{sfx}"], 1)
            d[pn + "_wxT"] = T(np.asarray(inputs[f"L0_attn_wx_{sfx}"]).T)
            d[pn + "_wdtT"] = T(np.asarray(inputs[f"L0_attn_wdt_{sfx}"]).T)
            d[pn + "_bdt"] = pcol(inputs[f"L0_attn_bdt_{sfx}"], 1)
            d[pn + "_alog"] = pcol(inputs[f"L0_attn_Alog_{sfx}"], NS)
            d[pn + "_dp"] = pcol(inputs[f"L0_attn_Dp_{sfx}"], 1)
            d[pn + "_woutT"] = T(np.asarray(inputs[f"L0_attn_wout_{sfx}"]).T)
            half = slice(0, D) if sfx == "f" else slice(D, 2 * D)
            d[pn + "_wprojT"] = T(np.asarray(inputs["L0_attn_wproj"])[:, half].T)
        d["bproj"] = pcol(inputs["L0_attn_bproj"], 1)
        d["w1"] = T(inputs["L0_ffn_W1"])
        d["b1"] = pcol(inputs["L0_ffn_b1"], 1)
        d["w2"] = T(inputs["L0_ffn_W2"])
        d["b2"] = pcol(inputs["L0_ffn_b2"], 1)
        d["wqT"] = T(np.asarray(inputs["L1_attn_Wq"]).T)
        d["bq"] = pcol(inputs["L1_attn_bq"], 1)
        d["wkT"] = T(np.asarray(inputs["L1_attn_Wk"]).T)
        d["bk"] = pcol(inputs["L1_attn_bk"], 1)
        d["wvT"] = T(np.asarray(inputs["L1_attn_Wv"]).T)
        d["bv_row"] = T(np.asarray(inputs["L1_attn_bv"]).reshape(1, D))
        d["woT"] = T(np.asarray(inputs["L1_attn_Wo"]).T)
        d["bo"] = pcol(inputs["L1_attn_bo"], 1)
        d["qTw"] = np.ascontiguousarray(
            np.transpose(np.asarray(inputs["L1_ffn_qweights"]), (0, 2, 1)),
            dtype=np.int32)
        scr = np.repeat(np.asarray(inputs["L1_ffn_scales"]), 16, axis=2)
        d["scrT"] = T(np.transpose(scr, (0, 2, 1)))
        d["qb"] = np.stack([pcol(np.asarray(inputs["L1_ffn_biases"])[i], 1) for i in range(6)])
        d["laT"] = T(np.transpose(np.asarray(inputs["L1_ffn_lora_a"]), (0, 2, 1)))
        d["lbT"] = T(np.transpose(np.asarray(inputs["L1_ffn_lora_b"]), (0, 2, 1)))
        d["lng"] = np.stack([pcol(np.asarray(inputs["L1_ffn_ln_gamma"])[i], 1) for i in range(3)])
        d["lnb"] = np.stack([pcol(np.asarray(inputs["L1_ffn_ln_beta"])[i], 1) for i in range(3)])
        d["ident_in"] = np.eye(128, dtype=np.float32)
        cores.append(d)
    return cores


def kernel(**inputs):
    global LAST_TAPS
    nc = build(tuple(TAPS))
    in_maps = _host_inputs(inputs)
    res = run_bass_kernel_spmd(nc, in_maps, core_ids=list(range(8)))
    out = np.zeros((4, SEQ, D), np.float32)
    LAST_TAPS = {}
    for c in range(8):
        b, h = c // 2, c % 2
        o = np.asarray(res.results[c]["out_fm"])
        if h == 0:
            out[b, 0:2048] = o.T
        else:
            out[b, 2048:4096] = o.T[::-1]
        for nm in TAPS:
            LAST_TAPS.setdefault(nm, []).append(np.asarray(res.results[c]["tap_" + nm]))
    return out


# version 10 (final)
# speedup vs baseline: 1.2497x; 1.0074x over previous
import numpy as np
import concourse.bass as bass
import concourse.mybir as mybir
from concourse.bass_utils import run_bass_kernel_spmd
from contextlib import ExitStack

F32 = mybir.dt.float32
I32 = mybir.dt.int32
AF = mybir.ActivationFunctionType
ALU = mybir.AluOpType
AX = mybir.AxisListType

D = 1024
KT = 8
DI = 2048
DIT = 16
NS = 16
DTR = 64
TOK = 2048
GRID = 2051
XR = 2054
SEQ = 4096
EPS = 1e-5
WIN_OWN = [(a, a + 512) for a in range(0, 2048, 512)]
WIN_GRID = WIN_OWN + [(2048, 2051)]
ENGINES = ("tensor", "vector", "scalar", "gpsimd", "sync")
RG = [[0, 1], [2, 3], [4, 5], [6, 7]]

TAPS = []
LAST_TAPS = {}
SERIAL = False
SERIAL_SITES = set()


NQS = 6


class Em:
    def __init__(self):
        self.q = {e: [] for e in ENGINES}
        self.cnt = {e: 0 for e in ENGINES}
        self.cnt["cc"] = 0
        self.qs = {q: {"idx": 0, "c": [0] * NQS, "map": {}} for q in
                   ("sync", "gpsimd")}

    def op(self, eng, fn, waits=(), site=None):
        if (SERIAL is True or (SERIAL and SERIAL is not False and eng in SERIAL)
                or (site is not None and site in SERIAL_SITES)):
            waits = list(waits) + [(e + "*", self.cnt[e]) if e in ("sync", "gpsimd")
                                   else (e, self.cnt[e])
                                   for e in ENGINES if e != eng] + \
                    ([("cc", self.cnt["cc"])] if self.cnt["cc"] else [])
        step = 16 if eng in ("sync", "gpsimd") else 1
        self.cnt[eng] += step
        if eng in ("sync", "gpsimd"):
            qs = self.qs[eng]
            j = qs["idx"]; qs["idx"] = (j + 1) % NQS
            qs["c"][j] += 1
            qs["map"][self.cnt[eng]] = (j, qs["c"][j], tuple(qs["c"]))
        self.q[eng].append((fn, [(s, v) for (s, v) in waits if v > 0],
                            eng, self.cnt[eng]))
        return self.cnt[eng]

    def cc(self, fn, waits=()):
        self.cnt["cc"] += 1
        self.q["gpsimd"].append((fn, [(s, v) for (s, v) in waits if v > 0],
                                 "cc", self.cnt["cc"]))
        return self.cnt["cc"]

    def snap(self):
        return dict(self.cnt)

    def bw(self, snap_):
        w = []
        for e in ENGINES:
            w.append((e + "*", snap_[e]) if e in ("sync", "gpsimd")
                     else (e, snap_[e]))
        if snap_.get("cc"):
            w.append(("cc", snap_["cc"]))
        return w


def build(dbg_taps=()):
    nc = bass.Bass("TRN2", target_bir_lowering=False)
    E = Em()

    def din(name, shape, dt=F32):
        return nc.dram_tensor(name, shape, dt, kind="ExternalInput").ap()

    xf_in = din("xf", [D, XR])
    pw = {}
    for p in ("a", "b"):
        pw[p] = dict(
            winT=din(p + "_winT", [D, 2 * DI]),
            wconv=din(p + "_wconv", [128, DIT * 4]),
            bconv=din(p + "_bconv", [128, DIT]),
            wxT=din(p + "_wxT", [DI, 96]),
            wdtT=din(p + "_wdtT", [DTR, DI]),
            bdt=din(p + "_bdt", [128, DIT]),
            alog=din(p + "_alog", [128, DIT * NS]),
            dp=din(p + "_dp", [128, DIT]),
            woutT=din(p + "_woutT", [DI, D]),
            wprojT=din(p + "_wprojT", [D, D]),
        )
    bproj = din("bproj", [128, KT])
    w1 = din("w1", [D, 4 * D]); b1 = din("b1", [128, 32])
    w2 = din("w2", [4 * D, D]); b2 = din("b2", [128, KT])
    wqT = din("wqT", [D, D]); bq = din("bq", [128, KT])
    wkT = din("wkT", [D, D]); bk = din("bk", [128, KT])
    wvT = din("wvT", [D, D]); bv_row = din("bv_row", [1, D])
    woT = din("woT", [D, D]); bo = din("bo", [128, KT])
    qTw = din("qTw", [6, D, D], I32)
    scrT = din("scrT", [6, D, D])
    qb = din("qb", [6, 128, KT])
    laT = din("laT", [6, D, 32])
    lbT = din("lbT", [6, 32, D])
    lng = din("lng", [3, 128, KT])
    lnb = din("lnb", [3, 128, KT])
    ident_in = din("ident_in", [128, 128])

    out_fm = nc.dram_tensor("out_fm", [D, TOK], F32, kind="ExternalOutput").ap()

    d_u = [nc.dram_tensor(f"d_u{p}", [DIT, 128, TOK], F32).ap() for p in range(2)]
    d_zs = [nc.dram_tensor(f"d_zs{p}", [DIT, 128, TOK], F32).ap() for p in range(2)]
    d_dl = [nc.dram_tensor(f"d_dl{p}", [DIT, 128, TOK], F32).ap() for p in range(2)]
    d_y = [nc.dram_tensor(f"d_y{p}", [DIT, 128, TOK], F32).ap() for p in range(2)]
    d_bc = [nc.dram_tensor(f"d_bc{p}", [2, NS, TOK], F32).ap() for p in range(2)]
    cc_st_in = nc.dram_tensor("cc_st_in", [DI, NS], F32)
    cc_st_out = nc.dram_tensor("cc_st_out", [DI, NS], F32)
    cc_k_in = nc.dram_tensor("cc_k_in", [D, TOK], F32)
    cc_k_out = nc.dram_tensor("cc_k_out", [2, D, TOK], F32)
    cc_vin = nc.dram_tensor("cc_vin", [TOK, D], F32)
    cc_vout = nc.dram_tensor("cc_vout", [2, TOK, D], F32)
    d_q = nc.dram_tensor("d_q", [D, TOK], F32).ap()
    d_x1 = nc.dram_tensor("d_x1", [D, TOK], F32).ap()
    d_xn1 = nc.dram_tensor("d_xn1", [D, TOK], F32).ap()
    d_g = nc.dram_tensor("d_g", [D, TOK], F32).ap()
    d_nrm = nc.dram_tensor("d_nrm", [1, TOK], F32).ap()
    d_rows = nc.dram_tensor("d_rows", [2, XR], F32).ap()

    taps = {}
    tap_shapes = {"s_acc": [D, TOK], "x1": [D, TOK], "xn": [D, XR], "g1": [D, TOK],
                  "xn1": [D, TOK], "y0": [DIT, 128, TOK], "u0": [DIT, 128, TOK],
                  "dl0": [DIT, 128, TOK], "zs0": [DIT, 128, TOK], "bc0": [2, NS, TOK],
                  "y1": [DIT, 128, TOK], "q": [D, TOK], "kf": [2, D, TOK],
                  "vf": [2, TOK, D], "attn": [D, TOK], "sattn": [D, TOK],
                  "st": [DI, NS], "bi": [DI, NS],
                  "ql0": [D, TOK], "qln0": [D, TOK], "qh0": [D, TOK]}
    for nm in dbg_taps:
        taps[nm] = nc.dram_tensor("tap_" + nm, tap_shapes[nm], F32,
                                  kind="ExternalOutput").ap()

    es = ExitStack()
    sem = {}
    for name in ("tensor", "vector", "scalar", "cc"):
        sem[name] = es.enter_context(nc.semaphore("sem_" + name))
    qsems = {}
    for qn in ("sync", "gpsimd"):
        qsems[qn] = [es.enter_context(nc.semaphore(f"sem_{qn}{j}"))
                     for j in range(NQS)]

    def mk_sb(stack):
        return lambda n, shp, dt=F32: stack.enter_context(nc.sbuf_tensor(n, shp, dt))

    sb = mk_sb(es)
    ones_col = sb("ones_col", [128, 1])
    ones_row = sb("ones_row", [1, 128])
    sqt = sb("sqt", [128, 512])
    small = {}
    for p in range(2):
        small[p] = dict(
            wconv=sb(f"wconv{p}", [128, DIT, 4]),
            bconv=sb(f"bconv{p}", [128, DIT, 1]),
            bdt=sb(f"bdt{p}", [128, DIT, 1]),
            A=sb(f"A{p}", [128, DIT, NS]),
            dp=sb(f"dp{p}", [128, DIT, 1]),
        )
    bproj_sb = sb("bproj_sb", [128, KT, 1])
    states = sb("states", [128, DIT, NS])
    binit = sb("binit", [128, DIT, NS])

    ps = {}
    for nm in ("A", "B", "C", "D"):
        ps[nm] = es.enter_context(nc.psum_tensor("ps" + nm, [128, 512], F32))
    ps_big = es.enter_context(nc.psum_tensor("ps_big", [128, 2048], F32))
    psf = {k: ("vector", 0) for k in ("A", "B", "C", "D", "big")}

    def dma(qeng, out, in_, waits=(), site=None):
        return E.op(qeng, lambda e, o=out, i=in_: e.dma_start(out=o, in_=i), waits,
                    site=site)

    def rw(dram_ap, cols):
        return dram_ap.rearrange("(k p) n -> p k n", p=128)

    def ln_stats(xt, ncols, rows, waits_in):
        rm, rs, rt = rows
        wins = [(a, min(a + 512, ncols)) for a in range(0, ncols, 512)]
        vm = 0
        for (a, b) in wins:
            w = b - a
            last = 0
            for k in range(KT):
                wl = list(waits_in) + [psf["A"], ("vector", vm)] if k == 0 else []
                last = E.op("tensor", lambda e, k=k, a=a, b=b, w=w: e.matmul(
                    out=ps["A"][0:1, 0:w], lhsT=ones_col[:], rhs=xt[:, k, a:b],
                    start=(k == 0), stop=(k == KT - 1), skip_group_check=True), wl)
            m1 = E.op("vector", lambda e, a=a, w=w: e.tensor_scalar(
                out=rm[0:1, a:a + w], in0=ps["A"][0:1, 0:w], scalar1=1.0 / D,
                scalar2=None, op0=ALU.mult), [("tensor", last)])
            psf["A"] = ("vector", m1)
            last = 0
            for k in range(KT):
                wl = list(waits_in) + [psf["B"]] if k == 0 else [("tensor", last)]
                sqm = E.op("scalar", lambda e, k=k, a=a, b=b: e.activation(
                    out=sqt[:, 0:b - a], in_=xt[:, k, a:b], func=AF.Square), wl)
                last = E.op("tensor", lambda e, k=k, w=w: e.matmul(
                    out=ps["B"][0:1, 0:w], lhsT=ones_col[:], rhs=sqt[:, 0:w],
                    start=(k == 0), stop=(k == KT - 1), skip_group_check=True),
                    [("scalar", sqm)])
            vm = E.op("vector", lambda e, a=a, w=w: e.tensor_scalar(
                out=rs[0:1, a:a + w], in0=ps["B"][0:1, 0:w], scalar1=1.0 / D,
                scalar2=None, op0=ALU.mult), [("tensor", last)])
            psf["B"] = ("vector", vm)
        v1 = E.op("vector", lambda e, n=ncols: e.tensor_tensor(
            out=rt[0:1, 0:n], in0=rm[0:1, 0:n], in1=rm[0:1, 0:n], op=ALU.mult),
            [("vector", m1), ("vector", vm)])
        v2 = E.op("vector", lambda e, n=ncols: e.tensor_tensor(
            out=rs[0:1, 0:n], in0=rs[0:1, 0:n], in1=rt[0:1, 0:n],
            op=ALU.subtract), [("vector", v1)])
        v3 = E.op("vector", lambda e, n=ncols: e.tensor_scalar(
            out=rs[0:1, 0:n], in0=rs[0:1, 0:n], scalar1=EPS, scalar2=None,
            op0=ALU.add), [("vector", v2)])
        s1 = E.op("scalar", lambda e, n=ncols: e.activation(
            out=rt[0:1, 0:n], in_=rs[0:1, 0:n], func=AF.Ln), [("vector", v3)])
        return E.op("scalar", lambda e, n=ncols: e.activation(
            out=rs[0:1, 0:n], in_=rt[0:1, 0:n], func=AF.Exp, scale=-0.5),
            [("scalar", s1)])

    def bcast2(rows, a, w, waits):
        rm, rs = rows[0], rows[1]
        t1 = E.op("tensor", lambda e, a=a, w=w: e.matmul(
            out=ps_big[:, 0:w], lhsT=ones_row[0:1, :], rhs=rm[0:1, a:a + w],
            start=True, stop=True, skip_group_check=True), list(waits) + [psf["big"]])
        return E.op("tensor", lambda e, a=a, w=w: e.matmul(
            out=ps_big[:, 512:512 + w], lhsT=ones_row[0:1, :],
            rhs=rs[0:1, a:a + w],
            start=True, stop=True, skip_group_check=True), [("tensor", t1)])

    r1 = ExitStack(); sb1 = mk_sb(r1)
    rows0 = tuple(sb1(f"rows0_{i}", [1, XR]) for i in range(3))
    xn = sb1("xn", [128, KT, XR])
    r1a = ExitStack(); sb1a = mk_sb(r1a)
    x_t = sb1a("x_t", [128, KT, XR])

    ld_x = dma("sync", x_t[:], rw(xf_in, XR))
    for p, pn in ((0, "a"), (1, "b")):
        for nm, key, cols in (("wconv", "wconv", 4), ("bconv", "bconv", 1),
                              ("bdt", "bdt", 1), ("dp", "dp", 1), ("alog", "A", NS)):
            dma("sync", small[p][key][:],
                pw[pn][nm].rearrange("p (k c) -> p k c", c=cols))
    dma("sync", bproj_sb[:], bproj.unsqueeze(2))
    ld_all = E.cnt["sync"]

    mz = E.op("vector", lambda e: e.memset(ones_col[:], 1.0))
    mz2 = E.op("vector", lambda e: e.memset(ones_row[:], 1.0), [("vector", mz)])
    aexp_last = 0
    for p in range(2):
        s1 = E.op("scalar", lambda e, p=p: e.activation(
            out=small[p]["A"][:], in_=small[p]["A"][:], func=AF.Exp),
            [("sync*", ld_all)])
        aexp_last = E.op("vector", lambda e, p=p: e.tensor_scalar(
            out=small[p]["A"][:], in0=small[p]["A"][:], scalar1=-1.0, scalar2=None,
            op0=ALU.mult), [("scalar", s1)])

    ln0_done = ln_stats(x_t, XR, rows0, [("sync", ld_x), ("vector", mz2)])
    xn_done = 0
    for (a, b) in [(i, min(i + 512, XR)) for i in range(0, XR, 512)]:
        w = b - a
        br = bcast2(rows0, a, w, [("scalar", ln0_done)])
        for k in range(KT):
            v1 = E.op("vector", lambda e, k=k, a=a, b=b, w=w: e.tensor_tensor(
                out=xn[:, k, a:b], in0=x_t[:, k, a:b], in1=ps_big[:, 0:w],
                op=ALU.subtract), [("tensor", br)])
            xn_done = E.op("vector", lambda e, k=k, a=a, b=b, w=w: e.tensor_tensor(
                out=xn[:, k, a:b], in0=xn[:, k, a:b], in1=ps_big[:, 512:512 + w],
                op=ALU.mult), [("vector", v1)])
        psf["big"] = ("vector", xn_done)
    dma("sync", d_rows[0:1, :], rows0[0][0:1, :], [("scalar", ln0_done)])
    dma("sync", d_rows[1:2, :], rows0[1][0:1, :], [("scalar", ln0_done)])
    if "xn" in taps:
        for k in range(KT):
            dma("sync", taps["xn"][k * 128:(k + 1) * 128, :], xn[:, k, :],
                [("vector", xn_done)])
    r1a.close()

    def moving_pieces(pipe, a, b, grid):
        out = []
        if pipe == 0:
            if grid:
                out.append((0, b - a, lambda k, a=a, b=b: xn[:, k, a:b]))
            else:
                out.append((0, b - a, lambda k, a=a, b=b: xn[:, k, 3 + a:3 + b]))
        else:
            if grid:
                if a < 3:
                    h = min(b, 3)
                    out.append((0, h - a, lambda k, a=a, h=h: xn[:, k, 2051 + a:2051 + h]))
                if b > 3:
                    p0 = max(a, 3)
                    out.append((p0 - a, b - p0,
                                lambda k, p0=p0, b=b: xn[:, k, 2053 - p0:2053 - b:-1]))
            else:
                out.append((0, b - a, lambda k, a=a, b=b: xn[:, k, 2050 - a:2050 - b:-1]))
        return out

    wslab = sb1("wslab", [128, 2, KT, 128])
    u_raw = sb1("u_raw", [128, GRID])
    u_s = sb1("u_s", [128, TOK])
    zbuf = sb1("zbuf", [128, 2, 512])
    wx_sb = sb1("wx_sb", [128, DIT, 96])
    wdt_sb = sb1("wdt_sb", [DTR, DI])
    dt_sb = sb1("dt_sb", [DTR, TOK])
    bc_sb = sb1("bc_sb", [32, TOK])
    dlw = sb1("dlw", [128, 512])
    dl_t = sb1("dl_t", [128, TOK])
    slab_used = [0, 0]
    zb_used = [0, 0]

    def stage1(pipe, pname, dbc_prev_evac, bw0):
        wname = pw[pname]
        ld_wx = dma("sync", wx_sb[:], rw(wname["wxT"], 96),
                    [("vector", xn_done)] + bw0)
        ld_wdt = dma("sync", wdt_sb[:], wname["wdtT"],
                     [("vector", xn_done)] + bw0)
        u_s_free = ("gpsimd", 0)
        u_raw_free = ("vector", 0)
        for m in range(32):
            slot = m % 2
            ld = dma("gpsimd", wslab[:, slot, :, :],
                     rw(wname["winT"], 2 * DI)[:, :, m * 128:(m + 1) * 128],
                     [("tensor", slab_used[slot]), ("vector", xn_done)])
            is_u = m < 16
            dt_i = m if is_u else m - 16
            wins = WIN_GRID if is_u else WIN_OWN
            for wi, (a, b) in enumerate(wins):
                pslot = "A" if wi % 2 == 0 else "B"
                last_mm = 0
                first = True
                for (coff, width, mov) in moving_pieces(pipe, a, b, is_u):
                    for k in range(KT):
                        wl = []
                        if first:
                            wl = [("gpsimd", ld), psf[pslot], ("vector", xn_done)]
                            first = False
                        last_mm = E.op("tensor", lambda e, k=k, coff=coff,
                                       width=width, mov=mov, pslot=pslot, slot=slot,
                                       st=(k == 0), sp=(k == KT - 1): e.matmul(
                            out=ps[pslot][:, coff:coff + width],
                            lhsT=wslab[:, slot, k, :], rhs=mov(k),
                            start=st, stop=sp, skip_group_check=True), wl)
                slab_used[slot] = last_mm
                if is_u:
                    ev = E.op("scalar", lambda e, a=a, b=b, pslot=pslot: e.activation(
                        out=u_raw[:, a:b], in_=ps[pslot][:, 0:b - a], func=AF.Copy),
                        [("tensor", last_mm), u_raw_free])
                    psf[pslot] = ("scalar", ev)
                else:
                    zslot = wi % 2
                    ev = E.op("scalar", lambda e, b=b, a=a, pslot=pslot,
                              zslot=zslot: e.activation(
                        out=zbuf[:, zslot, 0:b - a], in_=ps[pslot][:, 0:b - a],
                        func=AF.Silu),
                        [("tensor", last_mm), ("gpsimd", zb_used[zslot])])
                    psf[pslot] = ("scalar", ev)
                    zb_used[zslot] = dma("gpsimd", d_zs[pipe][dt_i, :, a:b],
                                         zbuf[:, zslot, 0:b - a], [("scalar", ev)])
            if is_u:
                sA = psf["A"][1] if psf["A"][0] == "scalar" else 0
                sB = psf["B"][1] if psf["B"][0] == "scalar" else 0
                c0 = E.op("vector", lambda e, dt_i=dt_i, p=pipe: e.tensor_scalar(
                    out=u_s[:], in0=u_raw[:, 0:TOK],
                    scalar1=small[p]["wconv"][:, dt_i, 0:1], scalar2=None,
                    op0=ALU.mult),
                    [("scalar", sA), ("scalar", sB), ("vector", aexp_last), u_s_free])
                for j in (1, 2, 3):
                    c0 = E.op("vector", lambda e, j=j, dt_i=dt_i, p=pipe: e.scalar_tensor_tensor(
                        out=u_s[:], in0=u_raw[:, j:j + TOK],
                        scalar=small[p]["wconv"][:, dt_i, j:j + 1], in1=u_s[:],
                        op0=ALU.mult, op1=ALU.add), [("vector", c0)])
                u_raw_free = ("vector", c0)
                cs = E.op("scalar", lambda e, dt_i=dt_i, p=pipe: e.activation(
                    out=u_s[:], in_=u_s[:], func=AF.Silu,
                    bias=small[p]["bconv"][:, dt_i, 0:1]), [("vector", c0)])
                for w2 in range(4):
                    wl = [("scalar", cs), ("sync", ld_wx)]
                    if dt_i == 0 and w2 == 0:
                        wl.append(dbc_prev_evac)
                    E.op("tensor", lambda e, w2=w2, dt_i=dt_i,
                         st=(dt_i == 0), sp=(dt_i == DIT - 1): e.matmul(
                        out=ps_big[0:96, w2 * 512:(w2 + 1) * 512],
                        lhsT=wx_sb[:, dt_i, :], rhs=u_s[:, w2 * 512:(w2 + 1) * 512],
                        start=st, stop=sp, skip_group_check=True),
                        wl if w2 == 0 else [])
                usp = dma("gpsimd", d_u[pipe][dt_i, :, :], u_s[:],
                          [("scalar", cs), ("tensor", E.cnt["tensor"])])
                u_s_free = ("gpsimd", usp)
        tm = E.cnt["tensor"]
        e1 = E.op("vector", lambda e: e.tensor_copy(out=dt_sb[:], in_=ps_big[0:DTR, :]),
                  [("tensor", tm)] + bw0)
        e2 = E.op("vector", lambda e: e.tensor_copy(out=bc_sb[:], in_=ps_big[DTR:96, :]),
                  [("vector", e1)] + bw0)
        dma("gpsimd", d_bc[pipe].rearrange("a n t -> (a n) t"), bc_sb[:],
            [("vector", e2)])
        psf["big"] = ("vector", e2)
        dl_free = ("gpsimd", 0)
        vl_prev = 0
        psc_free = psf["C"]
        for dt_i in range(DIT):
            vlast = 0
            for w2 in range(4):
                mm = E.op("tensor", lambda e, dt_i=dt_i, w2=w2: e.matmul(
                    out=ps["C"][:, 0:512], lhsT=wdt_sb[:, dt_i * 128:(dt_i + 1) * 128],
                    rhs=dt_sb[:, w2 * 512:(w2 + 1) * 512], start=True, stop=True,
                    skip_group_check=True),
                    [("vector", e1), ("sync", ld_wdt), psc_free])
                s1 = E.op("scalar", lambda e, dt_i=dt_i, p=pipe: e.activation(
                    out=dlw[:], in_=ps["C"][:, 0:512], func=AF.Abs,
                    bias=small[p]["bdt"][:, dt_i, 0:1]),
                    [("tensor", mm), dl_free, ("vector", vl_prev)])
                s2 = E.op("scalar", lambda e: e.activation(
                    out=dlw[:], in_=dlw[:], func=AF.Exp, scale=-1.0), [("scalar", s1)])
                s3 = E.op("scalar", lambda e: e.activation(
                    out=dlw[:], in_=dlw[:], func=AF.Ln, bias=1.0), [("scalar", s2)])
                s4 = E.op("scalar", lambda e, dt_i=dt_i, w2=w2, p=pipe: e.activation(
                    out=dl_t[:, w2 * 512:(w2 + 1) * 512], in_=ps["C"][:, 0:512],
                    func=AF.Relu, bias=small[p]["bdt"][:, dt_i, 0:1]), [("scalar", s3)])
                psc_free = ("scalar", s4)
                vlast = E.op("vector", lambda e, w2=w2: e.tensor_tensor(
                    out=dl_t[:, w2 * 512:(w2 + 1) * 512],
                    in0=dl_t[:, w2 * 512:(w2 + 1) * 512], in1=dlw[:], op=ALU.add),
                    [("scalar", s4)])
                vl_prev = vlast
            dsp = dma("gpsimd", d_dl[pipe][dt_i, :, :], dl_t[:], [("vector", vlast)])
            dl_free = ("gpsimd", dsp)
        psf["C"] = psc_free
        return ("vector", e2)

    dbc_ev = stage1(0, "a", ("vector", 0), [])
    stage1(1, "b", dbc_ev, E.bw(E.snap()))
    for nm, src in (("u0", d_u[0]), ("dl0", d_dl[0]), ("zs0", d_zs[0])):
        if nm in taps:
            snap = E.snap()
            for dt in range(DIT):
                dma("sync", taps[nm][dt], src[dt], E.bw(snap))
    if "bc0" in taps:
        snap = E.snap()
        dma("sync", taps["bc0"], d_bc[0], E.bw(snap))
    r1.close()

    r2 = ExitStack(); sb2 = mk_sb(r2)
    bcb = sb2("bcb", [128, 8, TOK])
    ccb = sb2("ccb", [128, 8, TOK])
    dl_f = sb2("dl_f", [128, TOK])
    u_f = sb2("u_f", [128, TOK])
    du_b = sb2("du_b", [128, TOK])
    y_b = sb2("y_b", [128, TOK])
    y_old = sb2("y_old", [128, TOK])
    dA_b = sb2("dA_b", [128, TOK])
    dBu_b = sb2("dBu_b", [128, TOK])
    h_b = sb2("h_b", [128, TOK])
    binit_done = 0

    def stage2(pipe, bsnap):
        bw = E.bw(bsnap)
        last_y = 0
        last_ysp = 0
        for half in range(2):
            bcw = bw + ([("vector", last_y)] if last_y else [])
            bb = dma("sync", bcb[:],
                     d_bc[pipe][0, half * 8:(half + 1) * 8, :].unsqueeze(0)
                     .broadcast_to([128, 8, TOK]), bcw)
            cb = dma("sync", ccb[:],
                     d_bc[pipe][1, half * 8:(half + 1) * 8, :].unsqueeze(0)
                     .broadcast_to([128, 8, TOK]), bcw)
            for dt_i in range(DIT):
                anti = [("vector", last_y)]
                fd = dma("sync", dl_f[:], d_dl[pipe][dt_i], bw + anti)
                fu = dma("gpsimd", u_f[:], d_u[pipe][dt_i], bw + anti)
                vdu = E.op("vector", lambda e: e.tensor_tensor(
                    out=du_b[:], in0=dl_f[:], in1=u_f[:], op=ALU.mult),
                    [("sync", fd), ("gpsimd", fu)])
                if half == 0:
                    vy = E.op("vector", lambda e, dt_i=dt_i, p=pipe: e.tensor_scalar(
                        out=y_b[:], in0=u_f[:], scalar1=small[p]["dp"][:, dt_i, 0:1],
                        scalar2=None, op0=ALU.mult),
                        [("vector", vdu), ("gpsimd", last_ysp)])
                    fy = 0
                else:
                    fy = dma("gpsimd", y_old[:], d_y[pipe][dt_i], bw + anti)
                    vy = E.op("vector", lambda e: e.memset(y_b[:], 0.0),
                              [("vector", vdu), ("gpsimd", last_ysp)])
                sprev = vy
                for n in range(8):
                    ns = half * 8 + n
                    sA = E.op("scalar", lambda e, dt_i=dt_i,
                              ns=ns, p=pipe: e.activation(
                        out=dA_b[:], in_=dl_f[:], func=AF.Exp,
                        scale=small[p]["A"][:, dt_i, ns:ns + 1]),
                        [("sync", fd), ("vector", sprev)])
                    vB = E.op("vector", lambda e, n=n: e.tensor_tensor(
                        out=dBu_b[:], in0=du_b[:], in1=bcb[:, n, :], op=ALU.mult),
                        [("vector", vy), ("sync", cb), ("sync", bb)])
                    if pipe == 0:
                        vS = E.op("vector", lambda e: e.tensor_tensor_scan(
                            out=h_b[:], data0=dA_b[:], data1=dBu_b[:],
                            initial=0.0, op0=ALU.mult, op1=ALU.add),
                            [("scalar", sA), ("vector", vB)])
                    else:
                        vS = E.op("vector", lambda e, dt_i=dt_i,
                                  ns=ns: e.tensor_tensor_scan(
                            out=h_b[:], data0=dA_b[:], data1=dBu_b[:],
                            initial=binit[:, dt_i, ns:ns + 1], op0=ALU.mult,
                            op1=ALU.add),
                            [("scalar", sA), ("vector", vB), ("vector", binit_done)])
                    vT = E.op("vector", lambda e, n=n: e.tensor_tensor(
                        out=dBu_b[:], in0=h_b[:], in1=ccb[:, n, :], op=ALU.mult),
                        [("vector", vS)])
                    vYa = E.op("vector", lambda e: e.tensor_tensor(
                        out=y_b[:], in0=y_b[:], in1=dBu_b[:], op=ALU.add),
                        [("vector", vT)])
                    if pipe == 0:
                        vYa = E.op("vector", lambda e, dt_i=dt_i, ns=ns: e.tensor_copy(
                            out=states[:, dt_i, ns:ns + 1], in_=h_b[:, TOK - 1:TOK]),
                            [("vector", vYa)])
                    sprev = vYa
                if half == 1:
                    sprev = E.op("vector", lambda e: e.tensor_tensor(
                        out=y_b[:], in0=y_b[:], in1=y_old[:], op=ALU.add),
                        [("vector", sprev), ("gpsimd", fy)])
                last_y = sprev
                last_ysp = dma("gpsimd", d_y[pipe][dt_i], y_b[:], [("vector", sprev)])
        return last_ysp

    y_a_done = stage2(0, E.snap())
    st_dma = dma("sync", cc_st_in.ap().rearrange("(d p) n -> p d n", p=128),
                 states[:], [("gpsimd", y_a_done), ("vector", E.cnt["vector"])])
    ccm = E.cc(lambda e: e.collective_compute(
        "AllReduce", ALU.add, replica_groups=RG,
        ins=[cc_st_in.ap().opt()], outs=[cc_st_out.ap().opt()]), [("sync", st_dma)])
    st_back = dma("sync", binit[:], cc_st_out.ap().rearrange("(d p) n -> p d n", p=128),
                  [("cc", ccm)])
    binit_done = E.op("vector", lambda e: e.tensor_tensor(
        out=binit[:], in0=binit[:], in1=states[:], op=ALU.subtract),
        [("sync", st_back)])
    if "st" in taps:
        dma("sync", taps["st"].rearrange("(d p) n -> p d n", p=128), states[:],
            [("vector", binit_done)])
    if "bi" in taps:
        dma("sync", taps["bi"].rearrange("(d p) n -> p d n", p=128), binit[:],
            [("vector", binit_done)])
    stage2(1, E.snap())
    for nm, src in (("y0", d_y[0]), ("y1", d_y[1])):
        if nm in taps:
            snap = E.snap()
            for dt in range(DIT):
                dma("sync", taps[nm][dt], src[dt], E.bw(snap))
    r2.close()

    r3 = ExitStack(); sb3 = mk_sb(r3)
    s_acc = sb3("s_acc", [128, KT, TOK])
    r3a = ExitStack(); sb3a = mk_sb(r3a)
    yg = sb3a("yg", [128, DIT, 512])
    g_f = sb3a("g_f", [128, 2, 3, 512])
    wout_sl = sb3a("wout_sl", [128, 2, DIT, 128])
    wproj_sb = sb3a("wproj_sb", [128, KT, KT, 128])
    sdir = sb3a("sdir", [128, KT, 512])
    wsl_used = [0, 0]

    def stage3(pipe, pname, bsnap):
        bw = E.bw(bsnap)
        ldp = dma("sync", wproj_sb[:],
                  rw(pw[pname]["wprojT"], D).rearrange("p k (m j) -> p k m j", j=128),
                  bw)
        sdir_free = ("tensor", 0)
        yg_free = ("tensor", 0)
        gm = []
        for wi, (a, b) in enumerate(WIN_OWN):
            for dt_i in range(DIT):
                slot = dt_i % 2
                anti = [("vector", gm[-2] if len(gm) >= 2 else 0)]
                f1 = dma("sync", g_f[:, slot, 0, :], d_y[pipe][dt_i, :, a:b],
                         bw + anti, site="s3_f1")
                f3 = dma("gpsimd", g_f[:, slot, 2, :], d_zs[pipe][dt_i, :, a:b],
                         bw + anti, site="s3_f3")
                gm.append(E.op("vector", lambda e, slot=slot, dt_i=dt_i: e.tensor_tensor(
                    out=yg[:, dt_i, :], in0=g_f[:, slot, 0, :],
                    in1=g_f[:, slot, 2, :], op=ALU.mult),
                    [("sync", f1), ("gpsimd", f3), yg_free]))
            ev_last = 0
            for dm in range(KT):
                slot = dm % 2
                pslot = "A" if dm % 2 == 0 else "B"
                ldw = dma("gpsimd", wout_sl[:, slot, :, :],
                          rw(pw[pname]["woutT"], D)[:, :, dm * 128:(dm + 1) * 128],
                          bw + [("tensor", wsl_used[slot])], site="s3_ldw")
                mm = 0
                for dt_i in range(DIT):
                    wl = ([("gpsimd", ldw), ("vector", gm[-1]), psf[pslot],
                           sdir_free] if dt_i == 0 else [])
                    mm = E.op("tensor", lambda e, slot=slot, dt_i=dt_i, pslot=pslot,
                              st=(dt_i == 0), sp=(dt_i == DIT - 1): e.matmul(
                        out=ps[pslot][:, 0:512], lhsT=wout_sl[:, slot, dt_i, :],
                        rhs=yg[:, dt_i, :], start=st, stop=sp,
                        skip_group_check=True), wl)
                wsl_used[slot] = mm
                ev_last = E.op("scalar", lambda e, dm=dm, pslot=pslot: e.activation(
                    out=sdir[:, dm, :], in_=ps[pslot][:, 0:512], func=AF.Copy),
                    [("tensor", mm), sdir_free])
                psf[pslot] = ("scalar", ev_last)
            yg_free = ("tensor", wsl_used[0] if wsl_used[0] > wsl_used[1]
                       else wsl_used[1])
            for dmp in range(KT):
                mm = 0
                for dm in range(KT):
                    wl = ([("scalar", ev_last), ("sync", ldp), psf["C"]]
                          if dm == 0 else [])
                    mm = E.op("tensor", lambda e, dmp=dmp, dm=dm, st=(dm == 0),
                              sp=(dm == KT - 1): e.matmul(
                        out=ps["C"][:, 0:512], lhsT=wproj_sb[:, dm, dmp, :],
                        rhs=sdir[:, dm, :], start=st, stop=sp,
                        skip_group_check=True), wl)
                sdir_free = ("tensor", mm)
                if pipe == 0:
                    evc = E.op("scalar", lambda e, dmp=dmp, a=a: e.activation(
                        out=s_acc[:, dmp, a:a + 512], in_=ps["C"][:, 0:512],
                        func=AF.Identity, bias=bproj_sb[:, dmp, 0:1]), [("tensor", mm)])
                    psf["C"] = ("scalar", evc)
                else:
                    evc = E.op("vector", lambda e, dmp=dmp, a=a, b=b: e.tensor_tensor(
                        out=s_acc[:, dmp, 2048 - b:2048 - a],
                        in0=ps["C"][:, 511::-1],
                        in1=s_acc[:, dmp, 2048 - b:2048 - a], op=ALU.add),
                        [("tensor", mm)])
                    psf["C"] = ("vector", evc)

    stage3(0, "a", E.snap())
    stage3(1, "b", E.snap())
    if "s_acc" in taps:
        snap = E.snap()
        for k in range(KT):
            dma("sync", taps["s_acc"][k * 128:(k + 1) * 128, :], s_acc[:, k, :],
                E.bw(snap))
    r3a.close()

    s3_done = E.snap()
    bw4 = E.bw(s3_done)
    r3b = ExitStack(); sb3b = mk_sb(r3b)
    h1 = sb3b("h1", [128, 32, 512])
    xnw = sb3b("xnw", [128, KT, 512])
    xres = sb3b("xres", [128, 512])
    w1_sl = sb3b("w1_sl", [128, 2, KT, 128])
    w2_sl = sb3b("w2_sl", [128, 32, 128])
    b1_sb = sb3b("b1_sb", [128, 32, 1])
    b2_sb = sb3b("b2_sb", [128, KT, 1])
    rows4 = tuple(sb3b(f"rows4_{i}", [1, XR]) for i in range(2))
    rows1 = (rows4[0], rows4[1], sb3b("rows1_t", [1, TOK]))
    dma("sync", b1_sb[:], b1.unsqueeze(2), bw4)
    dma("sync", b2_sb[:], b2.unsqueeze(2), bw4)
    dma("sync", rows4[0][0:1, :], d_rows[0:1, :], bw4)
    dma("sync", rows4[1][0:1, :], d_rows[1:2, :], bw4)
    bload = E.cnt["sync"]
    w1_used = [0, 0]
    w2_used = 0
    x1_last = 0
    h1_free = ("tensor", 0)
    for wi, (a, b) in enumerate(WIN_OWN):
        fx = dma("sync", xnw[:], rw(xf_in, XR)[:, :, 3 + a:3 + b],
                 bw4 + [("sync*", bload), ("vector", x1_last)])
        br = bcast2(rows4, 3 + a, 512, bw4 + [("sync*", bload)])
        xnm = 0
        for k in range(KT):
            v1 = E.op("vector", lambda e, k=k: e.tensor_tensor(
                out=xnw[:, k, :], in0=xnw[:, k, :], in1=ps_big[:, 0:512],
                op=ALU.subtract), [("tensor", br), ("sync", fx),
                                   ("vector", x1_last)])
            xnm = E.op("vector", lambda e, k=k: e.tensor_tensor(
                out=xnw[:, k, :], in0=xnw[:, k, :], in1=ps_big[:, 512:1024],
                op=ALU.mult), [("vector", v1)])
        psf["big"] = ("vector", xnm)
        for it in range(32):
            slot = it % 2
            pslot = "A" if it % 2 == 0 else "B"
            ldw = dma("gpsimd", w1_sl[:, slot, :, :],
                      rw(w1, 4 * D)[:, :, it * 128:(it + 1) * 128],
                      bw4 + [("tensor", w1_used[slot])])
            mm = 0
            for k in range(KT):
                wl = ([("gpsimd", ldw), ("vector", xnm), psf[pslot], h1_free]
                      if k == 0 else [])
                mm = E.op("tensor", lambda e, slot=slot, k=k, pslot=pslot,
                          st=(k == 0), sp=(k == KT - 1): e.matmul(
                    out=ps[pslot][:, 0:512], lhsT=w1_sl[:, slot, k, :],
                    rhs=xnw[:, k, :], start=st, stop=sp, skip_group_check=True), wl)
            w1_used[slot] = mm
            sev = E.op("scalar", lambda e, it=it, pslot=pslot: e.activation(
                out=h1[:, it, :], in_=ps[pslot][:, 0:512], func=AF.Relu,
                bias=b1_sb[:, it, 0:1]), [("tensor", mm)])
            psf[pslot] = ("scalar", sev)
        sm_h1 = E.cnt["scalar"]
        for ot in range(KT):
            ldw = dma("gpsimd", w2_sl[:],
                      rw(w2, D)[:, :, ot * 128:(ot + 1) * 128],
                      bw4 + [("tensor", w2_used)])
            mm = 0
            for it in range(32):
                wl = ([("gpsimd", ldw), ("scalar", sm_h1), psf["C"]]
                      if it == 0 else [])
                mm = E.op("tensor", lambda e, it=it, st=(it == 0),
                          sp=(it == 31): e.matmul(
                    out=ps["C"][:, 0:512], lhsT=w2_sl[:, it, :],
                    rhs=h1[:, it, :], start=st, stop=sp, skip_group_check=True), wl)
            w2_used = mm
            h1_free = ("tensor", mm)
            sg = E.op("scalar", lambda e, ot=ot: e.activation(
                out=xnw[:, ot, :], in_=ps["C"][:, 0:512], func=AF.Silu,
                bias=b2_sb[:, ot, 0:1]), [("tensor", mm)])
            psf["C"] = ("scalar", sg)
            v1 = E.op("vector", lambda e, ot=ot, a=a: e.tensor_tensor(
                out=xnw[:, ot, :], in0=xnw[:, ot, :], in1=s_acc[:, ot, a:a + 512],
                op=ALU.mult), [("scalar", sg)])
            fxr = dma("sync", xres[:],
                      rw(xf_in, XR)[:, ot, 3 + a:3 + b].unsqueeze(1),
                      [("vector", x1_last)])
            v2 = E.op("vector", lambda e, ot=ot, a=a: e.tensor_tensor(
                out=s_acc[:, ot, a:a + 512], in0=xres[:].unsqueeze(1),
                in1=xnw[:, ot, :].unsqueeze(1), op=ALU.add),
                [("vector", v1), ("sync", fxr)])
            x1_last = v2
            dma("gpsimd", d_x1[ot * 128:(ot + 1) * 128, a:a + 512],
                s_acc[:, ot, a:a + 512], [("vector", v2)])
    x1 = s_acc
    if "x1" in taps:
        snap = E.snap()
        for k in range(KT):
            dma("sync", taps["x1"][k * 128:(k + 1) * 128, :], x1[:, k, :], E.bw(snap))

    p4_done = E.snap()
    ln1_done = ln_stats(x1, TOK, rows1, E.bw(p4_done))
    xn1_done = 0
    for (a, b) in WIN_OWN:
        br = bcast2(rows1, a, 512, [("scalar", ln1_done)])
        for k in range(KT):
            v1 = E.op("vector", lambda e, k=k, a=a, b=b: e.tensor_tensor(
                out=x1[:, k, a:b], in0=x1[:, k, a:b], in1=ps_big[:, 0:512],
                op=ALU.subtract), [("tensor", br)])
            xn1_done = E.op("vector", lambda e, k=k, a=a, b=b: e.tensor_tensor(
                out=x1[:, k, a:b], in0=x1[:, k, a:b], in1=ps_big[:, 512:1024],
                op=ALU.mult), [("vector", v1)])
        psf["big"] = ("vector", xn1_done)
    for k in range(KT):
        dma("sync", d_xn1[k * 128:(k + 1) * 128, :], x1[:, k, :],
            [("vector", xn1_done)])
    if "xn1" in taps:
        snap = E.snap()
        dma("sync", taps["xn1"], d_xn1, E.bw(snap))
    r3b.close()
    r3.close()

    p4x_done = E.snap()
    bw5 = E.bw(p4x_done)
    r4 = ExitStack(); sb4 = mk_sb(r4)
    xn1_sb = sb4("xn1_sb", [128, KT, TOK])
    whold = sb4("whold", [128, KT, D])
    kv_t = sb4("kv_t", [128, 512])
    bq_sb = sb4("bq_sb", [128, KT, 1])
    bk_sb = sb4("bk_sb", [128, KT, 1])
    bv_b = sb4("bv_b", [128, D])
    xn1_ld = dma("sync", xn1_sb[:], rw(d_xn1, TOK), bw5)
    dma("sync", bq_sb[:], bq.unsqueeze(2), bw5)
    dma("sync", bk_sb[:], bk.unsqueeze(2), bw5)
    bvb = dma("sync", bv_b[:], bv_row.broadcast_to([128, D]), bw5)
    bld = E.cnt["sync"]
    vq = E.op("vector", lambda e: e.tensor_scalar(
        out=bq_sb[:], in0=bq_sb[:], scalar1=1.0 / 32.0, scalar2=None, op0=ALU.mult),
        [("sync*", bld)])
    whold_free = ("tensor", 0)
    kvt_free = ("gpsimd", 0)

    def proj_fm(wdram, dest, bias_sb, scale):
        nonlocal whold_free, kvt_free
        ld = dma("sync", whold[:], rw(wdram, D), bw5 + [whold_free])
        last_sp = 0
        for m in range(KT):
            for (a, b) in WIN_OWN:
                pslot = "A" if (m + a // 512) % 2 == 0 else "B"
                mm = 0
                for k in range(KT):
                    wl = ([("sync", ld), ("sync", xn1_ld), psf[pslot]]
                          if k == 0 else [])
                    mm = E.op("tensor", lambda e, m=m, k=k, a=a, b=b, pslot=pslot,
                              st=(k == 0), sp=(k == KT - 1): e.matmul(
                        out=ps[pslot][:, 0:512],
                        lhsT=whold[:, k, m * 128:(m + 1) * 128],
                        rhs=xn1_sb[:, k, a:b], start=st, stop=sp,
                        skip_group_check=True), wl)
                ev = E.op("scalar", lambda e, m=m, scale=scale, pslot=pslot: e.activation(
                    out=kv_t[:], in_=ps[pslot][:, 0:512], func=AF.Identity,
                    bias=bias_sb[:, m, 0:1], scale=scale),
                    [("tensor", mm), kvt_free, ("vector", vq)])
                psf[pslot] = ("scalar", ev)
                last_sp = dma("gpsimd", dest[m * 128:(m + 1) * 128, a:b], kv_t[:],
                              [("scalar", ev)])
                kvt_free = ("gpsimd", last_sp)
        whold_free = ("tensor", E.cnt["tensor"])
        return last_sp

    proj_fm(wqT, d_q, bq_sb, 1.0 / 32.0)
    kd = proj_fm(wkT, cc_k_in.ap(), bk_sb, 1.0)
    ldv = dma("sync", whold[:], rw(wvT, D), bw5 + [whold_free])
    vsp = 0
    for tt in range(16):
        for dvw in range(2):
            pslot = "C" if (tt + dvw) % 2 == 0 else "D"
            mm = 0
            for k in range(KT):
                wl = ([("sync", ldv), ("sync", xn1_ld), psf[pslot]]
                      if k == 0 else [])
                mm = E.op("tensor", lambda e, tt=tt, dvw=dvw, k=k, pslot=pslot,
                          st=(k == 0), sp=(k == KT - 1): e.matmul(
                    out=ps[pslot][:, 0:512],
                    lhsT=xn1_sb[:, k, tt * 128:(tt + 1) * 128],
                    rhs=whold[:, k, dvw * 512:(dvw + 1) * 512],
                    start=st, stop=sp, skip_group_check=True), wl)
            ev = E.op("vector", lambda e, dvw=dvw, pslot=pslot: e.tensor_tensor(
                out=kv_t[:], in0=ps[pslot][:, 0:512],
                in1=bv_b[:, dvw * 512:(dvw + 1) * 512], op=ALU.add),
                [("tensor", mm), ("sync", bvb), kvt_free])
            psf[pslot] = ("vector", ev)
            vsp = dma("gpsimd", cc_vin.ap()[tt * 128:(tt + 1) * 128,
                                             dvw * 512:(dvw + 1) * 512],
                      kv_t[:], [("vector", ev)])
            kvt_free = ("gpsimd", vsp)
    whold_free = ("tensor", E.cnt["tensor"])
    cc_k = E.cc(lambda e: e.collective_compute(
        "AllGather", ALU.bypass, replica_groups=RG,
        ins=[cc_k_in.ap().opt()], outs=[cc_k_out.ap().opt()]),
        [("gpsimd", vsp), ("gpsimd", kd)])
    cc_v = E.cc(lambda e: e.collective_compute(
        "AllGather", ALU.bypass, replica_groups=RG,
        ins=[cc_vin.ap().opt()], outs=[cc_vout.ap().opt()]), [("cc", cc_k)])
    if "q" in taps:
        snap = E.snap()
        dma("sync", taps["q"], d_q, E.bw(snap))
    r4.close()

    p5_done = E.snap()
    bw6 = [(s, v) for (s, v) in E.bw(p5_done) if s != "cc"]
    r5 = ExitStack(); sb5 = mk_sb(r5)
    hq = sb5("hq", [128, KT, TOK])
    tq = sb5("tq", [128, KT, TOK])
    rowsq_mr = tuple(sb5(f"rowsq_{i}", [1, TOK]) for i in range(2))
    wdeq = sb5("wdeq", [128, KT, D])
    qraw = sb5("qraw", [128, 512], I32)
    sraw = sb5("sraw", [128, 512])
    ar_sb = sb5("ar_sb", [32, TOK])
    la_sb = sb5("la_sb", [128, KT, 32])
    lb_sb = sb5("lb_sb", [32, D])
    qb_sb = sb5("qb_sb", [128, KT, 1])
    lgt_g = sb5("lgt_g", [128, KT, 1])
    lgt_b = sb5("lgt_b", [128, KT, 1])
    h_ready = dma("sync", hq[:], rw(d_xn1, TOK), bw6)
    wdeq_free = ("tensor", 0)

    def qlora_linear(idx, src, dst, add_residual, waits0):
        nonlocal wdeq_free
        dq_last = 0
        for dt_i in range(KT):
            for jh in range(2):
                anti = [("vector", dq_last), wdeq_free]
                l1 = dma("sync", qraw[:],
                         qTw[idx, dt_i * 128:(dt_i + 1) * 128,
                             jh * 512:(jh + 1) * 512], list(waits0) + anti)
                l2 = dma("sync", sraw[:],
                         scrT[idx, dt_i * 128:(dt_i + 1) * 128,
                              jh * 512:(jh + 1) * 512], list(waits0))
                s1 = E.op("scalar", lambda e, dt_i=dt_i, jh=jh: e.activation(
                    out=wdeq[:, dt_i, jh * 512:(jh + 1) * 512], in_=qraw[:],
                    func=AF.Copy, scale=1.0 / 7.5, bias=-1.0), [("sync", l2)])
                dq_last = E.op("vector", lambda e, dt_i=dt_i, jh=jh: e.tensor_tensor(
                    out=wdeq[:, dt_i, jh * 512:(jh + 1) * 512],
                    in0=wdeq[:, dt_i, jh * 512:(jh + 1) * 512], in1=sraw[:],
                    op=ALU.mult), [("scalar", s1)])
        lb_ld = dma("sync", lb_sb[:], lbT[idx], list(waits0))
        la_ld = dma("sync", la_sb[:], laT[idx].rearrange("(k p) r -> p k r", p=128),
                    list(waits0))
        qb_ld = dma("sync", qb_sb[:], qb[idx].unsqueeze(2), list(waits0))
        ar_done = 0
        for (a, b) in WIN_OWN:
            mm = 0
            for k in range(KT):
                wl = ([("sync", la_ld), psf["D"]] + list(waits0)) if k == 0 else []
                mm = E.op("tensor", lambda e, k=k, a=a, b=b, st=(k == 0),
                          sp=(k == KT - 1): e.matmul(
                    out=ps["D"][0:32, 0:512], lhsT=la_sb[:, k, :], rhs=src[:, k, a:b],
                    start=st, stop=sp, skip_group_check=True), wl)
            ar_done = E.op("vector", lambda e, a=a: e.tensor_copy(
                out=ar_sb[:, a:a + 512], in_=ps["D"][0:32, 0:512]), [("tensor", mm)])
            psf["D"] = ("vector", ar_done)
        out_last = 0
        for m in range(KT):
            for (a, b) in WIN_OWN:
                pslot = "A" if (m + a // 512) % 2 == 0 else "B"
                mm = 0
                for k in range(KT):
                    wl = ([("vector", dq_last), psf[pslot]] + list(waits0)
                          if k == 0 else [])
                    mm = E.op("tensor", lambda e, m=m, k=k, a=a, b=b, pslot=pslot,
                              st=(k == 0): e.matmul(
                        out=ps[pslot][:, 0:512],
                        lhsT=wdeq[:, k, m * 128:(m + 1) * 128], rhs=src[:, k, a:b],
                        start=st, stop=False, skip_group_check=True), wl)
                mm = E.op("tensor", lambda e, m=m, a=a, b=b, pslot=pslot: e.matmul(
                    out=ps[pslot][:, 0:512], lhsT=lb_sb[:, m * 128:(m + 1) * 128],
                    rhs=ar_sb[:, a:b], start=False, stop=True, skip_group_check=True),
                    [("sync", lb_ld), ("vector", ar_done)])
                if add_residual:
                    sev = E.op("scalar", lambda e, m=m, pslot=pslot: e.activation(
                        out=sraw[:], in_=ps[pslot][:, 0:512], func=AF.Identity,
                        bias=qb_sb[:, m, 0:1]),
                        [("tensor", mm), ("sync", qb_ld),
                         ("vector", out_last if isinstance(out_last, int) else 0)])
                    psf[pslot] = ("scalar", sev)
                    out_last = E.op("vector", lambda e, m=m, a=a: e.tensor_tensor(
                        out=dst[:, m, a:a + 512], in0=sraw[:],
                        in1=hq[:, m, a:a + 512], op=ALU.add), [("scalar", sev)])
                else:
                    sev = E.op("scalar", lambda e, m=m, a=a, pslot=pslot: e.activation(
                        out=tq[:, m, a:a + 512], in_=ps[pslot][:, 0:512],
                        func=AF.Identity,
                        bias=qb_sb[:, m, 0:1]), [("tensor", mm), ("sync", qb_ld)])
                    psf[pslot] = ("scalar", sev)
                    out_last = sev
        wdeq_free = ("tensor", E.cnt["tensor"])
        return out_last

    prevw = [("sync", h_ready)]
    for blk in range(3):
        m1 = qlora_linear(2 * blk, hq, tq, False,
                          prevw + (bw6 if blk == 0 else []))
        if blk == 0 and "ql0" in taps:
            snap = E.snap()
            for k in range(KT):
                dma("sync", taps["ql0"][k * 128:(k + 1) * 128, :], tq[:, k, :],
                    E.bw(snap))
        lb_ld2 = dma("sync", lgt_b[:], lnb[blk].unsqueeze(2), bw6)
        lg_ld = dma("sync", lgt_g[:], lng[blk].unsqueeze(2), bw6)
        lnm = ln_stats(tq, TOK, (rowsq_mr[0], rowsq_mr[1], ar_sb), [("scalar", m1)])
        gm_last = 0
        for (a, b) in WIN_OWN:
            br = bcast2(rowsq_mr, a, 512, [("scalar", lnm)])
            for k in range(KT):
                v1 = E.op("vector", lambda e, k=k, a=a, b=b: e.tensor_tensor(
                    out=tq[:, k, a:b], in0=tq[:, k, a:b], in1=ps_big[:, 0:512],
                    op=ALU.subtract), [("tensor", br)])
                v2 = E.op("vector", lambda e, k=k, a=a, b=b: e.tensor_tensor(
                    out=tq[:, k, a:b], in0=tq[:, k, a:b], in1=ps_big[:, 512:1024],
                    op=ALU.mult), [("vector", v1)])
                v3 = E.op("vector", lambda e, k=k, a=a, b=b: e.tensor_scalar(
                    out=tq[:, k, a:b], in0=tq[:, k, a:b], scalar1=lgt_g[:, k, 0:1],
                    scalar2=lgt_b[:, k, 0:1], op0=ALU.mult, op1=ALU.add),
                    [("vector", v2), ("sync", lg_ld), ("sync", lb_ld2)])
                gm_last = E.op("scalar", lambda e, k=k, a=a, b=b: e.activation(
                    out=tq[:, k, a:b], in_=tq[:, k, a:b], func=AF.Gelu),
                    [("vector", v3)])
            psf["big"] = ("scalar", gm_last)
        if blk == 0 and "qln0" in taps:
            snap = E.snap()
            for k in range(KT):
                dma("sync", taps["qln0"][k * 128:(k + 1) * 128, :], tq[:, k, :],
                    E.bw(snap))
        ql2 = qlora_linear(2 * blk + 1, tq, hq, True, [("scalar", gm_last)])
        prevw = [("vector", ql2)]
        if blk == 0 and "qh0" in taps:
            snap = E.snap()
            for k in range(KT):
                dma("sync", taps["qh0"][k * 128:(k + 1) * 128, :], hq[:, k, :],
                    E.bw(snap))
    g1_done = 0
    for k in range(KT):
        g1_done = E.op("scalar", lambda e, k=k: e.activation(
            out=hq[:, k, :], in_=hq[:, k, :], func=AF.Silu),
            [("vector", E.cnt["vector"])])
        dma("sync", d_g[k * 128:(k + 1) * 128, :], hq[:, k, :],
            [("scalar", g1_done)])
    if "g1" in taps:
        snap = E.snap()
        dma("sync", taps["g1"], d_g, E.bw(snap))
    if "kf" in taps:
        snap = E.snap()
        dma("sync", taps["kf"], cc_k_out.ap(), E.bw(snap) + [("cc", E.cnt["cc"])])
    if "vf" in taps:
        snap = E.snap()
        dma("sync", taps["vf"], cc_vout.ap(), E.bw(snap) + [("cc", E.cnt["cc"])])
    r5.close()

    p6_done = E.snap()
    bw7 = E.bw(p6_done) + [("cc", E.cnt["cc"])]
    r6 = ExitStack(); sb6 = mk_sb(r6)
    kslab = sb6("kslab", [128, KT, 512])
    vslab = sb6("vslab", [128, 4, D])
    qtile = sb6("qtile", [128, KT, 128])
    wsm = sb6("wsm", [128, 512])
    wTs = sb6("wTs", [128, 4, TOK])
    mx4 = sb6("mx4", [128, 4])
    den4 = sb6("den4", [128, 4])
    nrm = sb6("nrm", [128, 16, 32])
    nrt = sb6("nrt", [128, 16])
    nr_row = sb6("nr_row", [16, 128])
    at_acc = sb6("at_acc", [128, KT, TOK])
    sattn = sb6("sattn", [128, KT, 512])
    xf_t2 = sb6("xf_t2", [128, KT, 512])
    g1w = sb6("g1w", [128, KT, 512])
    rnb = sb6("rnb", [128, 512])
    wo_sl = sb6("wo_sl", [128, KT, 128])
    bo_sb = sb6("bo_sb", [128, KT, 1])
    ident = sb6("ident", [128, 128])
    bo_ld = dma("sync", bo_sb[:], bo.unsqueeze(2), bw7)
    id_ld = dma("sync", ident[:], ident_in, bw7)
    acc_zero = 0
    for k in range(KT):
        acc_zero = E.op("vector", lambda e, k=k: e.memset(at_acc[:, k, :], 0.0),
                        bw7 if k == 0 else [])
    acc_zero = E.op("vector", lambda e: e.memset(nrm[:], 0.0))
    wsm_free = ("tensor", 0)
    wts_free = ("tensor", 0)
    psbig_free = psf["big"]
    qt_free = ("tensor", 0)

    for sl in range(8):
        half, q4 = sl // 4, sl % 4
        lk = dma("sync", kslab[:],
                 cc_k_out.ap()[half].rearrange("(k p) t -> p k t", p=128)
                 [:, :, q4 * 512:(q4 + 1) * 512],
                 bw7 + [("tensor", E.cnt["tensor"])])
        lv = dma("sync", vslab[:],
                 cc_vout.ap()[half, q4 * 512:(q4 + 1) * 512, :]
                 .rearrange("(s p) d -> p s d", p=128),
                 bw7 + [("tensor", E.cnt["tensor"])])
        wt_last = 0
        for qt_i in range(16):
            lq = dma("gpsimd", qtile[:],
                     d_q.rearrange("(k p) t -> p k t", p=128)
                     [:, :, qt_i * 128:(qt_i + 1) * 128], bw7 + [qt_free])
            mm = 0
            for k in range(KT):
                wl = ([("sync", lk), ("gpsimd", lq), psf["A"],
                       ("vector", acc_zero)] if k == 0 else [])
                mm = E.op("tensor", lambda e, k=k, st=(k == 0),
                          sp=(k == KT - 1): e.matmul(
                    out=ps["A"][:, 0:512], lhsT=qtile[:, k, :], rhs=kslab[:, k, :],
                    start=st, stop=sp, skip_group_check=True), wl)
            qt_free = ("tensor", mm)
            r1m = E.op("vector", lambda e: e.tensor_reduce(
                out=mx4[:], in_=ps["A"][:, 0:512].rearrange("p (g j) -> p g j", j=128),
                axis=AX.X, op=ALU.max), [("tensor", mm)])
            r2m = E.op("vector", lambda e: e.tensor_tensor(
                out=wsm[:].rearrange("p (g j) -> p g j", j=128),
                in0=ps["A"][:, 0:512].rearrange("p (g j) -> p g j", j=128),
                in1=mx4[:].unsqueeze(2).broadcast_to([128, 4, 128]),
                op=ALU.subtract),
                [("vector", r1m), wsm_free])
            psf["A"] = ("vector", r2m)
            se = 0
            for g in range(4):
                se = E.op("scalar", lambda e, g=g: e.activation(
                    out=wsm[:, g * 128:(g + 1) * 128],
                    in_=wsm[:, g * 128:(g + 1) * 128], func=AF.Exp,
                    accum_out=den4[:, g:g + 1]), [("vector", r2m)])
            rc = E.op("vector", lambda e: e.reciprocal(out=mx4[:], in_=den4[:]),
                      [("scalar", se)])
            wn = E.op("vector", lambda e: e.tensor_tensor(
                out=wsm[:].rearrange("p (g j) -> p g j", j=128),
                in0=wsm[:].rearrange("p (g j) -> p g j", j=128),
                in1=mx4[:].unsqueeze(2).broadcast_to([128, 4, 128]),
                op=ALU.mult), [("vector", rc)])
            E.op("vector", lambda e, qt_i=qt_i, sl=sl: e.tensor_tensor(
                out=nrm[:, qt_i, sl * 4:(sl + 1) * 4], in0=den4[:], in1=mx4[:],
                op=ALU.mult), [("vector", wn)])
            tp = 0
            for g in range(4):
                wl = [("vector", wn)] + ([psbig_free, ("vector", wt_last),
                                          ("sync", id_ld)] if g == 0 else [])
                tp = E.op("tensor", lambda e, g=g: e.transpose(
                    out=ps_big[:, g * 128:(g + 1) * 128],
                    in_=wsm[:, g * 128:(g + 1) * 128], identity=ident[:]), wl)
            wsm_free = ("tensor", tp)
            wt_last = E.op("vector", lambda e, qt_i=qt_i: e.tensor_copy(
                out=wTs[:, 0:4, qt_i * 128:(qt_i + 1) * 128],
                in_=ps_big[:, 0:512].rearrange("p (g j) -> p g j", j=128)),
                [("tensor", tp), wts_free])
            psbig_free = ("vector", wt_last)
        for dv in range(KT):
            for (a, b) in WIN_OWN:
                mm = 0
                for g in range(4):
                    wl = ([("sync", lv), ("vector", wt_last), psf["C"]]
                          if g == 0 else [])
                    mm = E.op("tensor", lambda e, dv=dv, g=g, a=a, b=b,
                              st=(g == 0), sp=(g == 3): e.matmul(
                        out=ps["C"][:, 0:512],
                        lhsT=vslab[:, g, dv * 128:(dv + 1) * 128],
                        rhs=wTs[:, g, a:b], start=st, stop=sp,
                        skip_group_check=True), wl)
                avm = E.op("vector", lambda e, dv=dv, a=a: e.tensor_tensor(
                    out=at_acc[:, dv, a:a + 512], in0=ps["C"][:, 0:512],
                    in1=at_acc[:, dv, a:a + 512], op=ALU.add), [("tensor", mm)])
                psf["C"] = ("vector", avm)
        wts_free = ("tensor", E.cnt["tensor"])

    nr1 = E.op("vector", lambda e: e.tensor_reduce(
        out=nrt[:], in_=nrm[:], axis=AX.X, op=ALU.add),
        [("vector", E.cnt["vector"])])
    nr2 = E.op("vector", lambda e: e.tensor_scalar(
        out=nrt[:], in0=nrt[:], scalar1=1e-6, scalar2=None, op0=ALU.add),
        [("vector", nr1)])
    nr3 = E.op("vector", lambda e: e.reciprocal(out=nrt[:], in_=nrt[:]),
               [("vector", nr2)])
    ntp = E.op("tensor", lambda e: e.transpose(
        out=ps["D"][0:16, 0:128], in_=nrt[:], identity=ident[:]),
        [("vector", nr3), psf["D"], ("sync", id_ld)])
    psf["D"] = ("tensor", ntp)
    nrow = E.op("vector", lambda e: e.tensor_copy(out=nr_row[:], in_=ps["D"][0:16, 0:128]),
                [("tensor", ntp)])
    psf["D"] = ("vector", nrow)
    nb = dma("sync", d_nrm.rearrange("a (q j) -> (a q) j", j=128), nr_row[:],
             [("vector", nrow)])
    if "attn" in taps:
        snap = E.snap()
        for k in range(KT):
            dma("sync", taps["attn"][k * 128:(k + 1) * 128, :], at_acc[:, k, :],
                E.bw(snap))

    wo_used = [0, 0]
    out_last = 0
    sc_last = 0
    for wi, (a, b) in enumerate(WIN_OWN):
        rb = dma("sync", rnb[:], d_nrm[0:1, a:b].broadcast_to([128, 512]),
                 [("sync", nb), ("vector", sc_last)])
        sc = 0
        for dv in range(KT):
            sc = E.op("vector", lambda e, dv=dv, a=a: e.tensor_tensor(
                out=at_acc[:, dv, a:a + 512], in0=at_acc[:, dv, a:a + 512],
                in1=rnb[:], op=ALU.mult), [("sync", rb)])
        sc_last = sc
        fx1 = dma("gpsimd", xf_t2[:],
                  d_x1.rearrange("(k p) t -> p k t", p=128)[:, :, a:b],
                  [("vector", out_last)])
        fg1 = dma("gpsimd", g1w[:],
                  d_g.rearrange("(k p) t -> p k t", p=128)[:, :, a:b],
                  [("vector", out_last)])
        for m in range(KT):
            pslot = "A" if m % 2 == 0 else "B"
            ldw = dma("gpsimd", wo_sl[:],
                      rw(woT, D)[:, :, m * 128:(m + 1) * 128],
                      [("tensor", wo_used[0])])
            mm = 0
            for k in range(KT):
                wl = ([("gpsimd", ldw), ("vector", sc), psf[pslot]]
                      if k == 0 else [])
                mm = E.op("tensor", lambda e, m=m, k=k, a=a, pslot=pslot,
                          st=(k == 0), sp=(k == KT - 1): e.matmul(
                    out=ps[pslot][:, 0:512],
                    lhsT=wo_sl[:, k, :],
                    rhs=at_acc[:, k, a:a + 512], start=st, stop=sp,
                    skip_group_check=True), wl)
            wo_used[0] = mm
            sev = E.op("scalar", lambda e, m=m, pslot=pslot: e.activation(
                out=sattn[:, m, :], in_=ps[pslot][:, 0:512], func=AF.Identity,
                bias=bo_sb[:, m, 0:1]), [("tensor", mm), ("sync", bo_ld)])
            psf[pslot] = ("scalar", sev)
            v1 = E.op("vector", lambda e, m=m: e.tensor_tensor(
                out=sattn[:, m, :], in0=sattn[:, m, :], in1=g1w[:, m, :],
                op=ALU.mult), [("scalar", sev), ("gpsimd", fg1)])
            v2 = E.op("vector", lambda e, m=m: e.tensor_tensor(
                out=sattn[:, m, :], in0=sattn[:, m, :], in1=xf_t2[:, m, :],
                op=ALU.add), [("vector", v1), ("gpsimd", fx1)])
            out_last = v2
            dma("sync", out_fm[m * 128:(m + 1) * 128, a:b], sattn[:, m, :],
                [("vector", v2)])
    if "sattn" in taps:
        snap = E.snap()
        dma("sync", taps["sattn"], out_fm, E.bw(snap))
    r6.close()

    fin = E.snap()
    for e in ENGINES:
        E.q[e].append((None, [("sync*", fin["sync"]), ("gpsimd*", fin["gpsimd"])],
                       None, None))

    with nc.Block() as block:
        def runner(name):
            def go(eng):
                for (fn, waits, inc_sem, inc_val) in E.q[name]:
                    resolved = {}
                    def addw(h, v):
                        if v > resolved.get(h, 0):
                            resolved[h] = v
                    for (s, v) in waits:
                        if s in ("sync", "gpsimd"):
                            j, c, _ = E.qs[s]["map"][v]
                            addw((s, j), 16 * c)
                        elif s in ("sync*", "gpsimd*"):
                            qn = s[:-1]
                            if v > 0:
                                vec = E.qs[qn]["map"][v][2]
                                for j in range(NQS):
                                    if vec[j] > 0:
                                        addw((qn, j), 16 * vec[j])
                        else:
                            addw(s, v)
                    for h, v in resolved.items():
                        if isinstance(h, tuple):
                            eng.wait_ge(qsems[h[0]][h[1]], v)
                        else:
                            eng.wait_ge(sem[h], v)
                    if fn is None:
                        continue
                    if inc_sem in ("sync", "gpsimd"):
                        j, c, _ = E.qs[inc_sem]["map"][inc_val]
                        if c > 1:
                            eng.wait_ge(qsems[inc_sem][j], 16 * (c - 1))
                        ins = fn(eng)
                        ins.then_inc(qsems[inc_sem][j], 16)
                    else:
                        ins = fn(eng)
                        ins.then_inc(sem[inc_sem], 1)
            return go
        block.tensor(runner("tensor"))
        block.vector(runner("vector"))
        block.scalar(runner("scalar"))
        block.gpsimd(runner("gpsimd"))
        block.sync(runner("sync"))

    es.close()
    return nc


def _host_inputs(inputs):
    x = np.asarray(inputs["L0_attn_x"])
    cores = []
    T = lambda a: np.ascontiguousarray(a, dtype=np.float32)

    def pcol(arr, cols):
        a = np.asarray(arr).reshape(-1, 128, cols)
        return np.ascontiguousarray(
            a.transpose(1, 0, 2).reshape(128, -1), dtype=np.float32)

    for c in range(8):
        b, h = c // 2, c % 2
        xb = x[b]
        if h == 0:
            own = xb[0:2048]
            ah = np.zeros((3, D), np.float32)
            bh = xb[2048:2051][::-1]
            A_sfx, B_sfx = "f", "b"
        else:
            own = xb[2048:4096][::-1]
            ah = np.zeros((3, D), np.float32)
            bh = xb[2045:2048]
            A_sfx, B_sfx = "b", "f"
        xall = np.concatenate([ah, own, bh], axis=0)
        d = {"xf": T(xall.T)}
        for pn, sfx in (("a", A_sfx), ("b", B_sfx)):
            d[pn + "_winT"] = T(np.asarray(inputs[f"L0_attn_win_{sfx}"]).T)
            d[pn + "_wconv"] = pcol(np.asarray(inputs[f"L0_attn_wconv_{sfx}"]).reshape(DI, 4), 4)
            d[pn + "_bconv"] = pcol(inputs[f"L0_attn_bconv_{sfx}"], 1)
            d[pn + "_wxT"] = T(np.asarray(inputs[f"L0_attn_wx_{sfx}"]).T)
            d[pn + "_wdtT"] = T(np.asarray(inputs[f"L0_attn_wdt_{sfx}"]).T)
            d[pn + "_bdt"] = pcol(inputs[f"L0_attn_bdt_{sfx}"], 1)
            d[pn + "_alog"] = pcol(inputs[f"L0_attn_Alog_{sfx}"], NS)
            d[pn + "_dp"] = pcol(inputs[f"L0_attn_Dp_{sfx}"], 1)
            d[pn + "_woutT"] = T(np.asarray(inputs[f"L0_attn_wout_{sfx}"]).T)
            half = slice(0, D) if sfx == "f" else slice(D, 2 * D)
            d[pn + "_wprojT"] = T(np.asarray(inputs["L0_attn_wproj"])[:, half].T)
        d["bproj"] = pcol(inputs["L0_attn_bproj"], 1)
        d["w1"] = T(inputs["L0_ffn_W1"])
        d["b1"] = pcol(inputs["L0_ffn_b1"], 1)
        d["w2"] = T(inputs["L0_ffn_W2"])
        d["b2"] = pcol(inputs["L0_ffn_b2"], 1)
        d["wqT"] = T(np.asarray(inputs["L1_attn_Wq"]).T)
        d["bq"] = pcol(inputs["L1_attn_bq"], 1)
        d["wkT"] = T(np.asarray(inputs["L1_attn_Wk"]).T)
        d["bk"] = pcol(inputs["L1_attn_bk"], 1)
        d["wvT"] = T(np.asarray(inputs["L1_attn_Wv"]).T)
        d["bv_row"] = T(np.asarray(inputs["L1_attn_bv"]).reshape(1, D))
        d["woT"] = T(np.asarray(inputs["L1_attn_Wo"]).T)
        d["bo"] = pcol(inputs["L1_attn_bo"], 1)
        d["qTw"] = np.ascontiguousarray(
            np.transpose(np.asarray(inputs["L1_ffn_qweights"]), (0, 2, 1)),
            dtype=np.int32)
        scr = np.repeat(np.asarray(inputs["L1_ffn_scales"]), 16, axis=2)
        d["scrT"] = T(np.transpose(scr, (0, 2, 1)))
        d["qb"] = np.stack([pcol(np.asarray(inputs["L1_ffn_biases"])[i], 1) for i in range(6)])
        d["laT"] = T(np.transpose(np.asarray(inputs["L1_ffn_lora_a"]), (0, 2, 1)))
        d["lbT"] = T(np.transpose(np.asarray(inputs["L1_ffn_lora_b"]), (0, 2, 1)))
        d["lng"] = np.stack([pcol(np.asarray(inputs["L1_ffn_ln_gamma"])[i], 1) for i in range(3)])
        d["lnb"] = np.stack([pcol(np.asarray(inputs["L1_ffn_ln_beta"])[i], 1) for i in range(3)])
        d["ident_in"] = np.eye(128, dtype=np.float32)
        cores.append(d)
    return cores


def kernel(**inputs):
    global LAST_TAPS
    nc = build(tuple(TAPS))
    in_maps = _host_inputs(inputs)
    res = run_bass_kernel_spmd(nc, in_maps, core_ids=list(range(8)))
    out = np.zeros((4, SEQ, D), np.float32)
    LAST_TAPS = {}
    for c in range(8):
        b, h = c // 2, c % 2
        o = np.asarray(res.results[c]["out_fm"])
        if h == 0:
            out[b, 0:2048] = o.T
        else:
            out[b, 2048:4096] = o.T[::-1]
        for nm in TAPS:
            LAST_TAPS.setdefault(nm, []).append(np.asarray(res.results[c]["tap_" + nm]))
    return out
